# version 01
# baseline (speedup 1.0000x reference)
import numpy as np
import ml_dtypes
from contextlib import ExitStack
from typing import NamedTuple

import concourse.bass as bass
import concourse.mybir as mybir
from concourse.bass_utils import run_bass_kernel_spmd

F32 = mybir.dt.float32
BF16 = mybir.dt.bfloat16
ALU = mybir.AluOpType
AF = mybir.ActivationFunctionType
AX = mybir.AxisListType
bfdt = ml_dtypes.bfloat16

MAGIC = 12582912.0
EXPB = -20.0
D, H, DH, NBC, FF = 1024, 16, 64, 32, 4096
NSEQ, NCORES = 2, 8
ENGINES = ("sync", "tensor", "vector", "scalar", "gpsimd")


class Tok(NamedTuple):
    key: str
    val: int


class _Rec:

    def __getattr__(self, name):
        def call(*a, **k):
            return (name, a, k)
        return call


class Sched:

    def __init__(self, nc, stack):
        self.nc = nc
        self.stack = stack
        self.sem = {e: stack.enter_context(nc.semaphore("S_" + e)) for e in ENGINES}
        self.cnt = {e: 0 for e in ENGINES}
        self.opl = {e: [] for e in ENGINES}
        self.water = {}
        self.dma_groups = {}
        self.pend_bar = {}
        self.wmap = {}
        self.rmap = {}
        self.pend_rw = {e: [] for e in ENGINES}

    def group(self, name):
        if name not in self.dma_groups:
            self.dma_groups[name] = self.stack.enter_context(
                self.nc.semaphore("G_" + name))
            self.cnt[name] = 0
        return name

    def barrier(self):
        toks = [Tok(k, v) for k, v in self.cnt.items() if v > 0]
        for e in ENGINES:
            self.pend_bar[e] = toks
        self.wmap = {}
        self.rmap = {}

    def _rw_deps(self, reads, writes):
        deps = []
        for k in reads:
            deps += self.wmap.get(k, [])
        for k in writes:
            deps += self.wmap.get(k, [])
            deps += self.rmap.get(k, [])
        return deps

    def _rw_commit(self, tok, reads, writes):
        for k in writes:
            self.wmap[k] = [tok]
            self.rmap[k] = []
        for k in reads:
            self.rmap.setdefault(k, []).append(tok)

    def _resolve(self, eng, deps):
        byk = {}
        for t in deps:
            if t is None:
                continue
            v = t.val
            if t.key in self.dma_groups:
                v = self.cnt[t.key]
            if byk.get(t.key, 0) < v:
                byk[t.key] = v
        waits = []
        for k, v in sorted(byk.items()):
            wk = (eng, k)
            if self.water.get(wk, 0) >= v:
                continue
            self.water[wk] = v
            waits.append((k, v))
        return waits

    def op(self, eng, fn, reads=(), writes=(), deps=(), inc=True):
        deps = list(deps) + self._rw_deps(reads, writes)
        if self.pend_bar.get(eng):
            deps += self.pend_bar[eng]
            self.pend_bar[eng] = None
        waits = self._resolve(eng, deps)
        tok, inckey = None, None
        if inc:
            self.cnt[eng] += 1
            tok = Tok(eng, self.cnt[eng])
            inckey = eng
        self.opl[eng].append((fn(_Rec()), waits, inckey, 1))
        if tok is not None:
            for (r2, w2) in self.pend_rw[eng]:
                self._rw_commit(tok, r2, w2)
            self.pend_rw[eng] = []
            self._rw_commit(tok, reads, writes)
        else:
            self.pend_rw[eng].append((list(reads), list(writes)))
        return tok

    def dma(self, eng, out, in_, reads=(), writes=(), deps=(), group="misc", **kw):
        g = self.group(group)
        deps = list(deps) + self._rw_deps(reads, writes)
        if self.pend_bar.get(eng):
            deps += self.pend_bar[eng]
            self.pend_bar[eng] = None
        waits = self._resolve(eng, deps)
        self.cnt[g] += 16
        tok = Tok(g, self.cnt[g])
        self.opl[eng].append((("dma_start", (out, in_), kw), waits, g, 16))
        self._rw_commit(tok, reads, writes)
        return tok

    def emit(self, final_eng="sync"):
        finals = [Tok(g, self.cnt[g]) for g in self.dma_groups if self.cnt[g] > 0]
        self.opl[final_eng].append((None, self._resolve(final_eng, finals), None, 0))
        with self.nc.Block() as block:
            for e in ENGINES:
                if not self.opl[e]:
                    continue
                def body(E, _ops=tuple(self.opl[e])):
                    for spec, waits, inckey, amt in _ops:
                        for (k, v) in waits:
                            E.wait_ge(self._sem_of(k), v)
                        if spec is not None:
                            name, a, kw = spec
                            inst = getattr(E, name)(*a, **kw)
                            if inckey is not None:
                                inst.then_inc(self._sem_of(inckey), amt)
                getattr(block, e)(body)

    def _sem_of(self, key):
        return self.sem[key] if key in self.sem else self.dma_groups[key]


class Arena:
    def __init__(self, tensor, cols):
        self.t, self.cols, self.off, self.marks = tensor, cols, 0, []

    def alloc(self, n):
        n = (n + 1) & ~1
        assert self.off + n <= self.cols, f"arena overflow {self.off}+{n}>{self.cols}"
        ap = self.t[:, self.off:self.off + n]
        self.off += n
        return ap

    def push(self):
        self.marks.append(self.off)

    def pop(self):
        self.off = self.marks.pop()


def build_program(S=2048, debug=False):
    TB = 128
    MLPB = 256
    NT = S // 128
    NKC = S // 128
    QB = min(512, S)
    NQB = S // QB
    NTB = S // TB
    NMB = S // MLPB
    W512 = min(512, S)

    nc = bass.Bass("TRN2", target_bir_lowering=False)
    st = ExitStack()
    S_ = Sched(nc, st)
    o, d = S_.op, S_.dma

    xin = nc.dram_tensor("xin", [NSEQ, S, D], F32, kind="ExternalInput")
    out = nc.dram_tensor("out", [NSEQ, S, D], F32, kind="ExternalOutput")
    w1t = nc.dram_tensor("w1t", [D, FF], F32, kind="ExternalInput")
    w2t = nc.dram_tensor("w2t", [FF, D], F32, kind="ExternalInput")
    convw = nc.dram_tensor("convw", [128, 3 * (FF // 128)], F32, kind="ExternalInput")
    convb = nc.dram_tensor("convb", [128, FF // 128], F32, kind="ExternalInput")
    g1d = nc.dram_tensor("g1d", [1, D], F32, kind="ExternalInput")
    b1d = nc.dram_tensor("b1d", [1, D], F32, kind="ExternalInput")
    g2d = nc.dram_tensor("g2d", [1, D], F32, kind="ExternalInput")
    b2d = nc.dram_tensor("b2d", [1, D], F32, kind="ExternalInput")
    wqd = nc.dram_tensor("wqd", [D, H * DH], F32, kind="ExternalInput")
    wkd = nc.dram_tensor("wkd", [D, H * DH], F32, kind="ExternalInput")
    wvd = nc.dram_tensor("wvd", [D, H * DH], F32, kind="ExternalInput")
    whqd = nc.dram_tensor("whqd", [H * DH, NBC], F32, kind="ExternalInput")
    whkd = nc.dram_tensor("whkd", [H * DH, NBC], F32, kind="ExternalInput")
    wod = nc.dram_tensor("wod", [H // 2, 2 * DH, D], F32, kind="ExternalInput")
    wm1d = nc.dram_tensor("wm1d", [D, FF], F32, kind="ExternalInput")
    wm2d = nc.dram_tensor("wm2d", [FF, D], F32, kind="ExternalInput")
    b1cd = nc.dram_tensor("b1cd", [128, FF // 128], F32, kind="ExternalInput")
    b2rd = nc.dram_tensor("b2rd", [1, D], F32, kind="ExternalInput")
    eyebd = nc.dram_tensor("eyebd", [128, 128], BF16, kind="ExternalInput")
    eyefd = nc.dram_tensor("eyefd", [128, 128], F32, kind="ExternalInput")
    onesfd = nc.dram_tensor("onesfd", [128, 128], F32, kind="ExternalInput")

    sx1 = nc.dram_tensor("sx1", [NSEQ, S, D], F32)
    su = nc.dram_tensor("su", [NSEQ, S, D], F32)
    sx4 = nc.dram_tensor("sx4", [NSEQ, S, D], F32)
    sq1 = nc.dram_tensor("sq1", [D, FF], BF16)
    sq2 = nc.dram_tensor("sq2", [FF, D], BF16)
    dbg = {}
    if debug:
        for nm in ("d_x1", "d_u", "d_x4"):
            dbg[nm] = nc.dram_tensor(nm, [NSEQ, S, D], F32, kind="ExternalOutput")

    BCOLS = 80 * 1024
    FCOLS = 6592
    arB_t = st.enter_context(nc.sbuf_tensor("arB", [128, BCOLS], BF16))
    arF_t = st.enter_context(nc.sbuf_tensor("arF", [128, FCOLS], F32))
    eye_b = st.enter_context(nc.sbuf_tensor("eye_b", [128, 128], BF16))
    eye_f = st.enter_context(nc.sbuf_tensor("eye_f", [128, 128], F32))
    ones_f = st.enter_context(nc.sbuf_tensor("ones_f", [128, 128], F32))
    cvw = st.enter_context(nc.sbuf_tensor("cvw", [128, 3 * (FF // 128)], F32))
    cvb = st.enter_context(nc.sbuf_tensor("cvb", [128, FF // 128], F32))
    g1r = st.enter_context(nc.sbuf_tensor("g1r", [128, D], F32))
    b1r = st.enter_context(nc.sbuf_tensor("b1r", [128, D], F32))
    g2r = st.enter_context(nc.sbuf_tensor("g2r", [128, D], F32))
    b2lr = st.enter_context(nc.sbuf_tensor("b2lr", [128, D], F32))
    b2rr = st.enter_context(nc.sbuf_tensor("b2rr", [128, D], F32))
    b1cc = st.enter_context(nc.sbuf_tensor("b1cc", [128, FF // 128], F32))
    mwr = st.enter_context(nc.sbuf_tensor("mwr", [128, 8], F32))
    nb20 = st.enter_context(nc.sbuf_tensor("nb20", [128, 1], F32))
    pS = st.enter_context(nc.psum_tensor("pS", [128, 2048], F32))
    p4 = st.enter_context(nc.psum_tensor("p4", [128, 512], F32))
    p5 = st.enter_context(nc.psum_tensor("p5", [128, 512], F32))
    p6 = st.enter_context(nc.psum_tensor("p6", [128, 512], F32))
    p7 = st.enter_context(nc.psum_tensor("p7", [128, 512], F32))
    arB = Arena(arB_t, BCOLS)
    arF = Arena(arF_t, FCOLS)

    for t_, src_ in ((eye_b, eyebd), (eye_f, eyefd), (ones_f, onesfd), (cvw, convw),
                     (cvb, convb), (b1cc, b1cd)):
        d("sync", t_[:], src_[:], writes=[t_.name], group="cst")
    o("vector", lambda v: v.memset(nb20[:], EXPB), writes=["nb20"])
    for t_, src_ in ((g1r, g1d), (b1r, b1d), (g2r, g2d), (b2lr, b2d), (b2rr, b2rd)):
        d("sync", t_[:], src_[0:1, :].partition_broadcast(128), writes=[t_.name],
          group="cst")

    def quantize_weight(wdram, sqdram, nrow, ncol, mcol):
        nstrip = nrow // 128
        arF.push(); arB.push()
        wstrip = arF.alloc(ncol)
        qs = arB.alloc(ncol)
        o("vector", lambda v: v.memset(mwr[:, 7:8], 0.0), writes=["acc"])
        for i in range(nstrip):
            d("sync", wstrip[:], wdram[i * 128:(i + 1) * 128, :],
              writes=["wstrip"], group="wqld")
            o("vector", lambda v: v.tensor_reduce(
                mwr[:, 6:7], wstrip[:], AX.X, ALU.add, apply_absolute_value=True),
              reads=["wstrip"], writes=["red"])
            o("vector", lambda v: v.tensor_tensor(
                mwr[:, 7:8], mwr[:, 7:8], mwr[:, 6:7], ALU.add),
              reads=["red", "acc"], writes=["acc"])
        o("tensor", lambda t: t.matmul(p4[0:1, 0:1], ones_f[:, 0:1], mwr[:, 7:8],
                                       start=True, stop=True),
          reads=["acc", "ones_f"], writes=["p4"])
        o("vector", lambda v: v.tensor_scalar(
            mwr[0:1, 6:7], p4[0:1, 0:1], 1.0 / (nrow * ncol), 1e-5, ALU.mult, ALU.max),
          reads=["p4"], writes=["mean1"])
        o("tensor", lambda t: t.matmul(p4[:, 1:2], ones_f[0:1, :], mwr[0:1, 6:7],
                                       start=True, stop=True),
          reads=["mean1", "ones_f"], writes=["p4"])
        o("vector", lambda v: v.tensor_copy(mwr[:, mcol:mcol + 1], p4[:, 1:2]),
          reads=["p4"], writes=[f"mw{mcol}"])
        o("vector", lambda v: v.reciprocal(mwr[:, mcol + 1:mcol + 2],
                                           mwr[:, mcol:mcol + 1]),
          reads=[f"mw{mcol}"], writes=[f"mwi{mcol}"])
        for i in range(nstrip):
            d("sync", wstrip[:], wdram[i * 128:(i + 1) * 128, :],
              writes=["wstrip"], group="wqld")
            o("vector", lambda v: v.tensor_scalar(
                wstrip[:], wstrip[:], mwr[:, mcol + 1:mcol + 2], MAGIC,
                ALU.mult, ALU.add),
              reads=["wstrip", f"mwi{mcol}"], writes=["wstrip"])
            o("vector", lambda v: v.tensor_scalar(
                wstrip[:], wstrip[:], MAGIC, None, ALU.subtract),
              reads=["wstrip"], writes=["wstrip"])
            o("vector", lambda v: v.tensor_scalar(
                qs[:], wstrip[:], -1.0, 1.0, ALU.max, ALU.min),
              reads=["wstrip"], writes=["qs"])
            d("sync", sqdram[i * 128:(i + 1) * 128, :], qs[:],
              reads=["qs"], group="wqst")
        arF.pop(); arB.pop()

    quantize_weight(w1t, sq1, D, FF, 0)
    quantize_weight(w2t, sq2, FF, D, 2)
    o("vector", lambda v: v.tensor_scalar(mwr[:, 4:5], mwr[:, 0:1],
                                          1.0 / 127.0, None, ALU.mult),
      reads=["mw0"], writes=["mw1s"])
    o("vector", lambda v: v.tensor_scalar(mwr[:, 5:6], mwr[:, 2:3],
                                          0.5 / 127.0, None, ALU.mult),
      reads=["mw2"], writes=["mw2s"])

    def stage_ffn(seq, src, dst):
        S_.barrier()
        arB.push(); arF.push()
        w1q = [arB.alloc(FF) for _ in range(8)]
        w2q = [arB.alloc(D) for _ in range(32)]
        xqw = [[arB.alloc(TB + 2) for _ in range(8)] for _ in range(2)]
        xq2 = arB.alloc(32 * TB)
        q1b = arB.alloc(D)
        csr = arB.alloc(2 * S).bitcast(F32)
        f2r = arB.alloc(2 * TB).bitcast(F32)
        xt = [arF.alloc(D) for _ in range(2)]
        sl = arF.alloc(32 * TB)
        hb = [arB.alloc(2 * (TB + 2)).bitcast(F32) for _ in range(2)]
        yb = [arB.alloc(2 * TB).bitcast(F32) for _ in range(2)]
        bcc = arF.alloc(TB + 2)
        bc2 = arF.alloc(TB)
        m1b = arF.alloc(TB)
        stt = arF.alloc(16)

        o("vector", lambda v: v.memset(bcc[:], 0.0), writes=["bcc"])
        for i in range(8):
            d("sync", w1q[i][:], sq1[i * 128:(i + 1) * 128, :],
              writes=[f"w1q{i}"], group="wst")
        for i in range(32):
            d("sync", w2q[i][:], sq2[i * 128:(i + 1) * 128, :],
              writes=[f"w2q{i}"], group="wst")

        def quant_tile(tt):
            xa, kx = xt[tt % 2], f"xt{tt % 2}"
            d("sync", xa[:], src[seq, tt * 128:(tt + 1) * 128, :],
              writes=[kx], group=f"xin{tt % 2}")
            o("vector", lambda v: v.bn_stats(stt[:, 0:6], xa[:, 0:512]),
              reads=[kx], writes=["st0"])
            o("vector", lambda v: v.bn_stats(stt[:, 6:12], xa[:, 512:1024]),
              reads=[kx], writes=["st1"])
            o("vector", lambda v: v.bn_aggr(stt[:, 12:14], stt[:, 0:12]),
              reads=["st0", "st1"], writes=["mu"])
            o("vector", lambda v: v.tensor_scalar(stt[:, 14:15], stt[:, 13:14],
                                                  1e-5, None, ALU.add),
              reads=["mu"], writes=["vpe"])
            o("scalar", lambda sc: sc.activation(stt[:, 15:16], stt[:, 14:15], AF.Sqrt),
              reads=["vpe"], writes=["sig"])
            o("vector", lambda v: v.reciprocal(stt[:, 14:15], stt[:, 15:16]),
              reads=["sig"], writes=["rstd"])
            o("vector", lambda v: v.tensor_scalar(
                xa[:], xa[:], stt[:, 12:13], None, ALU.subtract),
              reads=[kx, "mu"], writes=[kx])
            o("vector", lambda v: v.tensor_reduce(
                stt[:, 0:1], xa[:], AX.X, ALU.max, apply_absolute_value=True),
              reads=[kx], writes=["am"])
            o("vector", lambda v: v.tensor_scalar(
                stt[:, 1:2], stt[:, 0:1], stt[:, 14:15], 1e-5, ALU.mult, ALU.max),
              reads=["am", "rstd"], writes=["u2"])
            o("vector", lambda v: v.reciprocal(stt[:, 2:3], stt[:, 1:2]),
              reads=["u2"], writes=["iu2"])
            o("vector", lambda v: v.tensor_scalar(
                stt[:, 3:4], stt[:, 2:3], 127.0, None, ALU.mult),
              reads=["iu2"], writes=["f0"])
            o("vector", lambda v: v.tensor_tensor(
                stt[:, 3:4], stt[:, 3:4], stt[:, 14:15], ALU.mult),
              reads=["f0", "rstd"], writes=["f"])
            o("vector", lambda v: v.tensor_tensor(
                stt[:, 4:5], stt[:, 1:2], mwr[:, 4:5], ALU.mult),
              reads=["u2", "mw1s"], writes=["cs"])
            o("vector", lambda v: v.tensor_scalar(
                xa[:], xa[:], stt[:, 3:4], MAGIC, ALU.mult, ALU.add),
              reads=[kx, "f"], writes=[kx])
            o("vector", lambda v: v.tensor_scalar(
                q1b[:], xa[:], MAGIC, None, ALU.subtract),
              reads=[kx], writes=["q1b"])
            o("tensor", lambda t: t.matmul(pS[0:1, 0:128], stt[:, 4:5], eye_f[:],
                                           start=True, stop=True),
              reads=["cs", "eye_f"], writes=["pS0"])
            o("vector", lambda v: v.tensor_copy(
                csr[0:1, tt * 128:(tt + 1) * 128], pS[0:1, 0:128]),
              reads=["pS0"], writes=[f"csr{tt}"])
            w = tt % 2
            for c in range(4):
                o("tensor", lambda t, c=c: t.matmul(
                    pS[:, 1024 + c * 128:1024 + (c + 1) * 128],
                    q1b[:, 2 * c * 128:(2 * c + 1) * 128], eye_b[:],
                    start=True, stop=True),
                  reads=["q1b", "eye_b"], writes=["pS2"], inc=(c == 3))
                o("tensor", lambda t, c=c: t.matmul(
                    pS[:, 1536 + c * 128:1536 + (c + 1) * 128],
                    q1b[:, (2 * c + 1) * 128:(2 * c + 2) * 128], eye_b[:],
                    start=True, stop=True),
                  reads=["q1b", "eye_b"], writes=["pS3"], inc=(c == 3))
            for c in range(4):
                o("vector", lambda v, c=c, w=w: v.tensor_copy(
                    xqw[w][2 * c][:, 1:129],
                    pS[:, 1024 + c * 128:1024 + (c + 1) * 128]),
                  reads=["pS2"], writes=[f"win{w}:{2 * c}"], inc=(c == 3))
                o("scalar", lambda sc, c=c, w=w: sc.copy(
                    xqw[w][2 * c + 1][:, 1:129],
                    pS[:, 1536 + c * 128:1536 + (c + 1) * 128]),
                  reads=["pS3"], writes=[f"win{w}:{2 * c + 1}"], inc=(c == 3))
            if tt == 0:
                for dc in range(8):
                    o("vector", lambda v, dc=dc: v.memset(xqw[0][dc][:, 0:1], 0.0),
                      reads=[], writes=[f"win0:{dc}"], inc=(dc == 7))
            else:
                pw = (tt - 1) % 2
                for dc in range(8):
                    o("vector", lambda v, dc=dc, w=w, pw=pw: v.tensor_copy(
                        xqw[w][dc][:, 0:1], xqw[pw][dc][:, 128:129]),
                      reads=[f"win{pw}:{dc}"], writes=[f"win{w}:{dc}"], inc=(dc == 7))
                for dc in range(8):
                    o("vector", lambda v, dc=dc, w=w, pw=pw: v.tensor_copy(
                        xqw[pw][dc][:, 129:130], xqw[w][dc][:, 1:2]),
                      reads=[f"win{w}:{dc}"], writes=[f"win{pw}:{dc}"], inc=(dc == 7))
            if tt == NT - 1:
                for dc in range(8):
                    o("vector", lambda v, dc=dc, w=w: v.memset(xqw[w][dc][:, 129:130], 0.0),
                      reads=[], writes=[f"win{w}:{dc}"], inc=(dc == 7))

        quant_tile(0)
        for tb in range(NTB):
            if tb + 1 < NT:
                quant_tile(tb + 1)
            w = tb % 2
            t0 = tb * TB
            lo, hi = max(0, t0 - 1), min(S, t0 + TB + 1)
            rt = [f"csr{i}" for i in range(max(0, tb - 1), min(NT, tb + 2))]
            o("tensor", lambda t: t.matmul(
                pS[:, 0:hi - lo], ones_f[0:1, :], csr[0:1, lo:hi],
                start=True, stop=True),
              reads=rt + ["ones_f"], writes=["pS0"])
            o("vector", lambda v: v.tensor_copy(
                bcc[:, lo - (t0 - 1):hi - (t0 - 1)], pS[:, 0:hi - lo]),
              reads=["pS0"], writes=["bcc"])
            if t0 == 0:
                o("vector", lambda v: v.memset(bcc[:, 0:1], 0.0), writes=["bcc"])
            if t0 + TB == S:
                o("vector", lambda v: v.memset(bcc[:, TB + 1:TB + 2], 0.0), writes=["bcc"])
            o("vector", lambda v: v.memset(m1b[:], 0.0), writes=["m1b"])

            winkeys = [f"win{w}:{dc}" for dc in range(8)]
            for mc in range(32):
                pb, kpb = (p4, "p4") if mc % 2 == 0 else (p5, "p5")
                for dc in range(8):
                    o("tensor", lambda t, mc=mc, dc=dc, pb=pb, w=w: t.matmul(
                        pb[:, 0:TB + 2], w1q[dc][:, mc * 128:(mc + 1) * 128],
                        xqw[w][dc][:, 0:TB + 2],
                        start=(dc == 0), stop=(dc == 7)),
                      reads=[f"w1q{dc}"] + (winkeys if dc == 0 else []),
                      writes=[kpb], inc=(dc == 7))
                hbx, khb = (hb[0], "hb0") if mc % 2 == 0 else (hb[1], "hb1")
                o("vector", lambda v, pb=pb, hbx=hbx: v.tensor_tensor(
                    hbx[:], pb[:, 0:TB + 2], bcc[:], ALU.mult),
                  reads=[kpb, "bcc"], writes=[khb])
                ybx, kyb = (yb[0], "yb0") if mc % 2 == 0 else (yb[1], "yb1")
                o("scalar", lambda sc, hbx=hbx, mc=mc, ybx=ybx: sc.activation(
                    ybx[:], hbx[:, 1:TB + 1], AF.Identity,
                    bias=cvb[:, mc:mc + 1], scale=cvw[:, 3 * mc + 1:3 * mc + 2]),
                  reads=[khb, "cvw", "cvb"], writes=[kyb])
                o("vector", lambda v, hbx=hbx, mc=mc, ybx=ybx: v.scalar_tensor_tensor(
                    ybx[:], hbx[:, 0:TB], cvw[:, 3 * mc:3 * mc + 1],
                    ybx[:], ALU.mult, ALU.add),
                  reads=[khb, kyb, "cvw"], writes=[kyb])
                o("vector", lambda v, hbx=hbx, mc=mc, ybx=ybx: v.scalar_tensor_tensor(
                    ybx[:], hbx[:, 2:TB + 2], cvw[:, 3 * mc + 2:3 * mc + 3],
                    ybx[:], ALU.mult, ALU.add),
                  reads=[khb, kyb, "cvw"], writes=[kyb])
                o("scalar", lambda sc, mc=mc, ybx=ybx: sc.activation(
                    sl[:, mc * TB:(mc + 1) * TB], ybx[:], AF.Silu),
                  reads=[kyb], writes=[f"sl{mc}"])
                o("vector", lambda v, mc=mc: v.scalar_tensor_tensor(
                    yb[0] if mc % 2 == 0 else yb[1], sl[:, mc * TB:(mc + 1) * TB],
                    -1.0, sl[:, mc * TB:(mc + 1) * TB], ALU.mult, ALU.max),
                  reads=[f"sl{mc}"], writes=["yb0" if mc % 2 == 0 else "yb1"])
                o("vector", lambda v, mc=mc: v.tensor_tensor(
                    m1b[:], m1b[:], yb[0] if mc % 2 == 0 else yb[1], ALU.max),
                  reads=["yb0" if mc % 2 == 0 else "yb1", "m1b"], writes=["m1b"])
            o("tensor", lambda t: t.matmul(pS[:, 512:640], m1b[:], eye_f[:],
                                           start=True, stop=True),
              reads=["m1b", "eye_f"], writes=["pS1"])
            o("vector", lambda v: v.tensor_reduce(
                stt[:, 8:9], pS[:, 512:640], AX.X, ALU.max),
              reads=["pS1"], writes=["m2r"])
            o("vector", lambda v: v.tensor_scalar(
                stt[:, 10:11], stt[:, 8:9], 1e-5, None, ALU.max),
              reads=["m2r"], writes=["m2"])
            o("vector", lambda v: v.reciprocal(stt[:, 12:13], stt[:, 10:11]),
              reads=["m2"], writes=["im2"])
            o("vector", lambda v: v.tensor_scalar(
                stt[:, 12:13], stt[:, 12:13], 127.0, None, ALU.mult),
              reads=["im2"], writes=["f2"])
            o("vector", lambda v: v.tensor_tensor(
                stt[:, 14:15], stt[:, 10:11], mwr[:, 5:6], ALU.mult),
              reads=["m2", "mw2s"], writes=["cs2"])
            o("tensor", lambda t: t.matmul(pS[0:1, 512:640], stt[:, 12:13], eye_f[:],
                                           start=True, stop=True),
              reads=["f2", "eye_f"], writes=["pS1"])
            o("vector", lambda v: v.tensor_copy(f2r[0:1, 0:TB], pS[0:1, 512:640]),
              reads=["pS1"], writes=["f2r"])
            o("tensor", lambda t: t.matmul(pS[:, 512:640], ones_f[0:1, :],
                                           f2r[0:1, 0:TB], start=True, stop=True),
              reads=["f2r", "ones_f"], writes=["pS1"])
            o("vector", lambda v: v.tensor_copy(bc2[:], pS[:, 512:640]),
              reads=["pS1"], writes=["bc2"])
            for mc in range(32):
                o("vector", lambda v, mc=mc: v.tensor_tensor(
                    sl[:, mc * TB:(mc + 1) * TB], sl[:, mc * TB:(mc + 1) * TB],
                    bc2[:], ALU.mult),
                  reads=[f"sl{mc}", "bc2"], writes=[f"sl{mc}"])
                o("vector", lambda v, mc=mc: v.tensor_scalar(
                    xq2[:, mc * TB:(mc + 1) * TB], sl[:, mc * TB:(mc + 1) * TB],
                    MAGIC, MAGIC, ALU.add, ALU.subtract),
                  reads=[f"sl{mc}"], writes=[f"xq2{mc}"])
            xa, kx = xt[tb % 2], f"xt{tb % 2}"
            d("sync", xa[:], src[seq, t0:t0 + TB, :], writes=[kx], group=f"xin{tb % 2}")
            for dh in range(2):
                pb, kpb = (p6, "p6") if dh == 0 else (p7, "p7")
                for mc in range(32):
                    o("tensor", lambda t, mc=mc, dh=dh, pb=pb: t.matmul(
                        pb[:], xq2[:, mc * TB:(mc + 1) * TB],
                        w2q[mc][:, dh * 512:(dh + 1) * 512],
                        start=(mc == 0), stop=(mc == 31)),
                      reads=[f"xq2{mc}", f"w2q{mc}"],
                      writes=[kpb], inc=(mc == 31))
                o("vector", lambda v, pb=pb, dh=dh, xa=xa: v.scalar_tensor_tensor(
                    xa[:, dh * 512:(dh + 1) * 512], pb[:], stt[:, 14:15],
                    xa[:, dh * 512:(dh + 1) * 512], ALU.mult, ALU.add),
                  reads=[kpb, "cs2", kx], writes=[kx])
            d("sync", dst[seq, t0:t0 + TB, :], xa[:], reads=[kx], group="xout")
        arB.pop(); arF.pop()

    def stage_attn(seq):
        S_.barrier()
        arB.push(); arF.push()
        oall = [arB.alloc(S) for _ in range(8)]
        xnbuf = arB.alloc(D)
        whq8 = arB.alloc(2 * 8 * NBC)
        ut = [arF.alloc(D) for _ in range(2)]
        stt = arF.alloc(16)
        lrec = arF.alloc(QB)
        bcst = arF.alloc(QB)
        wtmp = arF.alloc(NBC)
        arB.push()
        xnT = [arB.alloc(S) for _ in range(8)]

        for tt in range(NT):
            xa, kx = ut[tt % 2], f"ut{tt % 2}"
            d("sync", xa[:], sx1[seq, tt * 128:(tt + 1) * 128, :],
              writes=[kx], group=f"bx{tt % 2}")
            o("vector", lambda v: v.bn_stats(stt[:, 0:6], xa[:, 0:512]),
              reads=[kx], writes=["st0"])
            o("vector", lambda v: v.bn_stats(stt[:, 6:12], xa[:, 512:1024]),
              reads=[kx], writes=["st1"])
            o("vector", lambda v: v.bn_aggr(stt[:, 12:14], stt[:, 0:12]),
              reads=["st0", "st1"], writes=["mu"])
            o("vector", lambda v: v.tensor_scalar(stt[:, 14:15], stt[:, 13:14],
                                                  1e-5, None, ALU.add),
              reads=["mu"], writes=["vpe"])
            o("scalar", lambda sc: sc.activation(stt[:, 15:16], stt[:, 14:15], AF.Sqrt),
              reads=["vpe"], writes=["sig"])
            o("vector", lambda v: v.reciprocal(stt[:, 14:15], stt[:, 15:16]),
              reads=["sig"], writes=["rstd"])
            o("vector", lambda v: v.tensor_scalar(
                xa[:], xa[:], stt[:, 12:13], stt[:, 14:15], ALU.subtract, ALU.mult),
              reads=[kx, "mu", "rstd"], writes=[kx])
            d("sync", su[seq, tt * 128:(tt + 1) * 128, :], xa[:], reads=[kx], group="su")
            if debug:
                d("sync", dbg["d_u"][seq, tt * 128:(tt + 1) * 128, :], xa[:],
                  reads=[kx], group="dbgu")
            o("vector", lambda v: v.bn_stats(stt[:, 0:6], xa[:, 0:512]),
              reads=[kx], writes=["st0"])
            o("vector", lambda v: v.bn_stats(stt[:, 6:12], xa[:, 512:1024]),
              reads=[kx], writes=["st1"])
            o("vector", lambda v: v.bn_aggr(stt[:, 12:14], stt[:, 0:12]),
              reads=["st0", "st1"], writes=["mu"])
            o("vector", lambda v: v.tensor_scalar(stt[:, 14:15], stt[:, 13:14],
                                                  1e-6, None, ALU.add),
              reads=["mu"], writes=["vpe"])
            o("scalar", lambda sc: sc.activation(stt[:, 15:16], stt[:, 14:15], AF.Sqrt),
              reads=["vpe"], writes=["sig"])
            o("vector", lambda v: v.reciprocal(stt[:, 14:15], stt[:, 15:16]),
              reads=["sig"], writes=["rstd"])
            o("vector", lambda v: v.scalar_tensor_tensor(
                xa[:], xa[:], stt[:, 12:13], g1r[:], ALU.subtract, ALU.mult),
              reads=[kx, "mu", "g1r"], writes=[kx])
            o("vector", lambda v: v.scalar_tensor_tensor(
                xnbuf[:], xa[:], stt[:, 14:15], b1r[:], ALU.mult, ALU.add),
              reads=[kx, "rstd", "b1r"], writes=["xnbuf"])
            for c in range(4):
                o("tensor", lambda t, c=c: t.matmul(
                    pS[:, 1024 + c * 128:1024 + (c + 1) * 128],
                    xnbuf[:, 2 * c * 128:(2 * c + 1) * 128], eye_b[:],
                    start=True, stop=True),
                  reads=["xnbuf", "eye_b"], writes=["pS2"], inc=(c == 3))
                o("tensor", lambda t, c=c: t.matmul(
                    pS[:, 1536 + c * 128:1536 + (c + 1) * 128],
                    xnbuf[:, (2 * c + 1) * 128:(2 * c + 2) * 128], eye_b[:],
                    start=True, stop=True),
                  reads=["xnbuf", "eye_b"], writes=["pS3"], inc=(c == 3))
            for c in range(4):
                o("vector", lambda v, c=c, tt=tt: v.tensor_copy(
                    xnT[2 * c][:, tt * 128:(tt + 1) * 128],
                    pS[:, 1024 + c * 128:1024 + (c + 1) * 128]),
                  reads=["pS2"], writes=[f"xnT{2 * c}:{tt}"], inc=(c == 3))
                o("scalar", lambda sc, c=c, tt=tt: sc.copy(
                    xnT[2 * c + 1][:, tt * 128:(tt + 1) * 128],
                    pS[:, 1536 + c * 128:1536 + (c + 1) * 128]),
                  reads=["pS3"], writes=[f"xnT{2 * c + 1}:{tt}"], inc=(c == 3))
        xnall = [f"xnT{i}:{j}" for i in range(8) for j in range(NT)]

        for qki, wdr in ((0, whqd), (1, whkd)):
            for p in range(8):
                for par in range(2):
                    h_ = 2 * p + par
                    d("sync", wtmp[par * 64:(par + 1) * 64, :],
                      wdr[h_ * DH:(h_ + 1) * DH, :], writes=[f"wtmp{par}"], group="whq")
                o("vector", lambda v, p=p, qki=qki: v.tensor_scalar(
                    whq8[:, (qki * 8 + p) * NBC:(qki * 8 + p + 1) * NBC],
                    wtmp[:, :], 8.0 if qki == 0 else 1.0, None, ALU.mult),
                  reads=["wtmp0", "wtmp1"], writes=[f"whq8:{qki}:{p}"])

        for half in range(2):
            arB.push(); arF.push()
            qsb = [arB.alloc(S) for _ in range(4)]
            ksb = [arB.alloc(S) for _ in range(4)]
            qbb = [arB.alloc(S) for _ in range(4)]
            kbb = [arB.alloc(S) for _ in range(4)]
            vsb = [arB.alloc(NKC * 65 + 1) for _ in range(8)]
            qbn = arB.alloc(128)
            eqf = arF.alloc(128)
            pair0 = half * 4

            for hh in range(8):
                o("vector", lambda v, hh=hh: v.memset(
                    vsb[hh][:, 0:NKC * 65].rearrange("p (a b) -> p a b", b=65)[:, :, 64:65], 1.0),
                  writes=[f"vones{hh}"])

            arB.push()
            wpr = [arB.alloc(8 * 128) for _ in range(2)]
            wvh = arB.alloc(8 * 512)
            for which, wdr, dstl, scl in (("q", wqd, qsb, 0.125), ("k", wkd, ksb, 1.0)):
                wb = wpr[0] if which == "q" else wpr[1]
                for pi in range(4):
                    p = pair0 + pi
                    kwb = f"wpr{which}"
                    for dc in range(8):
                        d("gpsimd", wb[:, dc * 128:(dc + 1) * 128],
                          wdr[dc * 128:(dc + 1) * 128, p * 128:(p + 1) * 128],
                          writes=[f"{kwb}:{dc}"], group=f"w{which}")
                    for nb_ in range(S // W512):
                        pb, kpb = (p4, "p4") if nb_ % 2 == 0 else (p5, "p5")
                        for dc in range(8):
                            o("tensor", lambda t, dc=dc, nb_=nb_, wb=wb, pb=pb: t.matmul(
                                pb[:, 0:W512], wb[:, dc * 128:(dc + 1) * 128],
                                xnT[dc][:, nb_ * W512:(nb_ + 1) * W512],
                                start=(dc == 0), stop=(dc == 7)),
                              reads=[f"{kwb}:{dc}"] + (xnall if dc == 0 else []),
                              writes=[kpb], inc=(dc == 7))
                        o("vector", lambda v, pb=pb, nb_=nb_, pi=pi, dstl=dstl, scl=scl: v.tensor_scalar(
                            dstl[pi][:, nb_ * W512:(nb_ + 1) * W512], pb[:, 0:W512],
                            scl, None, ALU.mult),
                          reads=[kpb], writes=[f"{which}sb{pi}:{nb_}"])
            for dc in range(8):
                d("gpsimd", wvh[:, dc * 512:(dc + 1) * 512],
                  wvd[dc * 128:(dc + 1) * 128, half * 512:(half + 1) * 512],
                  writes=[f"wvh:{dc}"], group="wv")
            for tt in range(NT):
                pb, kpb = (p6, "p6") if tt % 2 == 0 else (p7, "p7")
                for dc in range(8):
                    o("tensor", lambda t, dc=dc, tt=tt, pb=pb: t.matmul(
                        pb[:], xnT[dc][:, tt * 128:(tt + 1) * 128],
                        wvh[:, dc * 512:(dc + 1) * 512],
                        start=(dc == 0), stop=(dc == 7)),
                      reads=[f"wvh:{dc}"] + (xnall if dc == 0 else []),
                      writes=[kpb], inc=(dc == 7))
                for hh in range(8):
                    o("vector", lambda v, hh=hh, tt=tt, pb=pb: v.tensor_copy(
                        vsb[hh][:, tt * 65:tt * 65 + 64], pb[:, hh * 64:(hh + 1) * 64]),
                      reads=[kpb], writes=[f"v{hh}:{tt}"], inc=(hh == 7))
            arB.pop()
            ptb = [arB.alloc(4 * QB) for _ in range(2)]

            for qki, srcl, dstl, scl in ((0, qsb, qbb, 0.1), (1, ksb, kbb, 1.0)):
                sk = "q" if qki == 0 else "k"
                dk = "qb" if qki == 0 else "kb"
                for gg in range(2):
                    for tt in range(NT):
                        for j in range(4):
                            pi = 2 * gg + j // 2
                            par = j % 2
                            pb, kpb = (p6, "p6") if par == 0 else (p7, "p7")
                            o("tensor", lambda t, pi=pi, par=par, j=j, tt=tt, qki=qki, pb=pb, srcl=srcl: t.matmul(
                                pb[0:128, (j // 2) * 32:(j // 2) * 32 + 32],
                                srcl[pi][par * 64:(par + 1) * 64, tt * 128:(tt + 1) * 128],
                                whq8[par * 64:(par + 1) * 64,
                                     (qki * 8 + pair0 + pi) * NBC:(qki * 8 + pair0 + pi + 1) * NBC],
                                start=(j // 2 == 0), stop=(j // 2 == 1),
                                tile_position=(par * 64, 0)),
                              reads=[f"{sk}sb{pi}:{tt * 128 // W512}",
                                     f"whq8:{qki}:{pair0 + pi}"],
                              writes=[kpb], inc=(j // 2 == 1))
                        o("scalar", lambda sc: sc.activation(eqf[:, 0:64], p6[:, 0:64], AF.Exp),
                          reads=["p6"], writes=["eq0"])
                        o("scalar", lambda sc: sc.activation(eqf[:, 64:128], p7[:, 0:64], AF.Exp),
                          reads=["p7"], writes=["eq1"])
                        o("vector", lambda v: v.tensor_reduce(
                            stt[:, 0:4], eqf[:, 0:128].rearrange("p (a b) -> p a b", b=32),
                            AX.X, ALU.add),
                          reads=["eq0", "eq1"], writes=["eqs"])
                        o("vector", lambda v: v.reciprocal(stt[:, 4:8], stt[:, 0:4]),
                          reads=["eqs"], writes=["eqr"])
                        for ci, j in ((0, 0), (1, 2), (2, 1), (3, 3)):
                            o("vector", lambda v, ci=ci, scl=scl: v.tensor_scalar(
                                qbn[:, ci * 32:(ci + 1) * 32], eqf[:, ci * 32:(ci + 1) * 32],
                                stt[:, 4 + ci:5 + ci], scl, ALU.mult, ALU.mult),
                              reads=["eqr", "eq0", "eq1"], writes=["qbn"], inc=(ci == 3))
                        first = True
                        for ci, j in ((0, 0), (1, 2), (2, 1), (3, 3)):
                            par = j % 2
                            o("tensor", lambda t, ci=ci, par=par, first=first: t.matmul(
                                pS[par * 64:par * 64 + 32,
                                   512 + ci * 128:512 + (ci + 1) * 128],
                                qbn[:, ci * 32:(ci + 1) * 32], eye_b[:],
                                start=(ci in (0, 2)), stop=(ci in (1, 3)),
                                tile_position=(0, par * 64)),
                              reads=["qbn", "eye_b"], writes=["pS1"], inc=(ci == 3))
                            first = False
                        for ci, j in ((0, 0), (1, 2), (2, 1), (3, 3)):
                            par = j % 2
                            pi = 2 * gg + j // 2
                            o("vector", lambda v, ci=ci, par=par, pi=pi, tt=tt, dstl=dstl: v.tensor_copy(
                                dstl[pi][par * 64:par * 64 + 32, tt * 128:(tt + 1) * 128],
                                pS[par * 64:par * 64 + 32, 512 + ci * 128:512 + (ci + 1) * 128]),
                              reads=["pS1"], writes=[f"{dk}{pi}:{tt}"], inc=(ci == 3))

            qkeys = lambda pi, nb_: [f"qsb{pi}:{nb_}", f"qbb{pi}"]
            obank = [p4, p5, p6, p7]
            okeys = ["p4", "p5", "p6", "p7"]
            for gg in range(2):
                pA, pB_ = 2 * gg, 2 * gg + 1
                for qb_ in range(NQB):
                    q0 = qb_ * QB
                    nb_ = q0 // W512
                    for kc in range(NKC):
                        for jj, pi in ((0, pA), (1, pA), (2, pB_), (3, pB_)):
                            par = jj % 2
                            kqr = [f"qsb{pi}:{nb_}", f"ksb{pi}:{kc * 128 // W512}"]
                            o("tensor", lambda t, jj=jj, pi=pi, par=par, kc=kc, q0=q0: t.matmul(
                                pS[:, jj * 512:jj * 512 + QB],
                                ksb[pi][par * 64:(par + 1) * 64, kc * 128:(kc + 1) * 128],
                                qsb[pi][par * 64:(par + 1) * 64, q0:q0 + QB],
                                start=True, stop=False, tile_position=(par * 64, 0)),
                              reads=kqr, writes=[f"pS{jj}"], inc=False)
                            o("tensor", lambda t, jj=jj, pi=pi, par=par, kc=kc, q0=q0: t.matmul(
                                pS[:, jj * 512:jj * 512 + QB],
                                kbb[pi][par * 64:par * 64 + 32, kc * 128:(kc + 1) * 128],
                                qbb[pi][par * 64:par * 64 + 32, q0:q0 + QB],
                                start=False, stop=True, tile_position=(par * 64, 0)),
                              reads=[f"qb{pi}:{q0 // 128 + z}" for z in range(QB // 128)] + [f"kb{pi}:{kc}"],
                              writes=[f"pS{jj}"], inc=True)
                        kpt = f"ptb{kc % 2}"
                        o("scalar", lambda sc, kc=kc: sc.activation(
                            ptb[kc % 2][:].rearrange("p (a b) -> p a b", b=QB),
                            pS[:].rearrange("p (a b) -> p a b", b=512)[:, :, 0:QB],
                            AF.Exp, bias=nb20[:, 0:1]),
                          reads=["pS0", "pS1", "pS2", "pS3"], writes=[kpt])
                        for jj, pi in ((0, pA), (1, pA), (2, pB_), (3, pB_)):
                            par = jj % 2
                            hh = 2 * pi + par
                            ob, ok = obank[jj], okeys[jj]
                            if par == 0:
                                o("tensor", lambda t, jj=jj, hh=hh, kc=kc, ob=ob: t.matmul(
                                    ob[0:65, 0:QB], vsb[hh][:, kc * 65:kc * 65 + 65],
                                    ptb[kc % 2][:, jj * QB:(jj + 1) * QB],
                                    start=(kc == 0), stop=(kc == NKC - 1)),
                                  reads=[kpt, f"v{hh}:{kc}", f"vones{hh}"],
                                  writes=[ok], inc=True)
                            else:
                                o("tensor", lambda t, jj=jj, hh=hh, kc=kc, ob=ob: t.matmul(
                                    ob[64:128, 0:QB], vsb[hh][:, kc * 65:kc * 65 + 64],
                                    ptb[kc % 2][:, jj * QB:(jj + 1) * QB],
                                    start=(kc == 0), stop=(kc == NKC - 1),
                                    tile_position=(0, 64)),
                                  reads=[kpt, f"v{hh}:{kc}"],
                                  writes=[ok], inc=False)
                                o("tensor", lambda t, jj=jj, hh=hh, kc=kc, ob=ob: t.matmul(
                                    ob[32:33, 0:QB], vsb[hh][:, kc * 65 + 64:kc * 65 + 65],
                                    ptb[kc % 2][:, jj * QB:(jj + 1) * QB],
                                    start=(kc == 0), stop=(kc == NKC - 1),
                                    tile_position=(0, 32)),
                                  reads=[kpt, f"vones{hh}"], writes=[ok], inc=True)
                    for jj, pi in ((0, pA), (1, pA), (2, pB_), (3, pB_)):
                        par = jj % 2
                        ob, ok = obank[jj], okeys[jj]
                        pg = pair0 + pi
                        if par == 0:
                            o("vector", lambda v, ob=ob: v.reciprocal(
                                lrec[64:65, 0:QB], ob[64:65, 0:QB]),
                              reads=[ok], writes=["lrec"])
                            o("tensor", lambda t: t.matmul(
                                pS[0:64, 0:QB], ones_f[64:65, 0:64], lrec[64:65, 0:QB],
                                start=True, stop=True, tile_position=(64, 0)),
                              reads=["lrec", "ones_f"], writes=["pS0"])
                            o("scalar", lambda sc: sc.copy(bcst[0:64, 0:QB], pS[0:64, 0:QB]),
                              reads=["pS0"], writes=["bcst0"])
                            o("vector", lambda v, ob=ob, pg=pg, q0=q0: v.tensor_tensor(
                                oall[pg][0:64, q0:q0 + QB], ob[0:64, 0:QB],
                                bcst[0:64, 0:QB], ALU.mult),
                              reads=[ok, "bcst0"], writes=[f"oall{pg}:{q0}"])
                        else:
                            o("vector", lambda v, ob=ob: v.reciprocal(
                                lrec[32:33, 0:QB], ob[32:33, 0:QB]),
                              reads=[ok], writes=["lrec2"])
                            o("tensor", lambda t: t.matmul(
                                pS[64:128, 0:QB], ones_f[32:33, 0:64], lrec[32:33, 0:QB],
                                start=True, stop=True, tile_position=(32, 64)),
                              reads=["lrec2", "ones_f"], writes=["pS0"])
                            o("scalar", lambda sc: sc.copy(bcst[64:128, 0:QB], pS[64:128, 0:QB]),
                              reads=["pS0"], writes=["bcst1"])
                            o("vector", lambda v, ob=ob, pg=pg, q0=q0: v.tensor_tensor(
                                oall[pg][64:128, q0:q0 + QB], ob[64:128, 0:QB],
                                bcst[64:128, 0:QB], ALU.mult),
                              reads=[ok, "bcst1"], writes=[f"oall{pg}:{q0}"])
            S_.barrier()
            arB.pop(); arF.pop()
        arB.pop()

        arB.push(); arF.push()
        wob = arB.alloc(8 * D)
        w1b = [arB.alloc(8 * 128) for _ in range(2)]
        w2b = [arB.alloc(D) for _ in range(2)]
        h1 = arB.alloc(32 * MLPB)
        xmT = [arB.alloc(MLPB) for _ in range(8)]
        xmb = arB.alloc(D)
        x3b = arB.alloc(2 * (MLPB // 128) * D).bitcast(F32)
        x1t = arF.alloc(D)
        for pg in range(8):
            d("gpsimd", wob[:, pg * D:(pg + 1) * D], wod[pg, :, :],
              writes=[f"wob{pg}"], group="wo")
        oallk = [f"oall{pg}:{q0}" for pg in range(8) for q0 in range(0, S, QB)]

        for mb in range(NMB):
            tiles = [mb * (MLPB // 128) + c for c in range(MLPB // 128)]
            for ci, tt in enumerate(tiles):
                for dh in range(2):
                    pb, kpb = (p4, "p4") if dh == 0 else (p5, "p5")
                    for pg in range(8):
                        o("tensor", lambda t, pg=pg, dh=dh, tt=tt, pb=pb: t.matmul(
                            pb[:], oall[pg][:, tt * 128:(tt + 1) * 128],
                            wob[:, pg * D + dh * 512:pg * D + (dh + 1) * 512],
                            start=(pg == 0), stop=(pg == 7)),
                          reads=[f"wob{pg}"] + (oallk if pg == 0 else []),
                          writes=[kpb], inc=(pg == 7))
                kx, xu = f"ut{tt % 2}", ut[tt % 2]
                d("sync", xu[:], su[seq, tt * 128:(tt + 1) * 128, :],
                  writes=[kx], group=f"bx{tt % 2}")
                for dh in range(2):
                    pb, kpb = (p4, "p4") if dh == 0 else (p5, "p5")
                    o("vector", lambda v, pb=pb, dh=dh, ci=ci, xu=xu: v.tensor_tensor(
                        x3b[:, ci * D + dh * 512:ci * D + (dh + 1) * 512],
                        pb[:], xu[:, dh * 512:(dh + 1) * 512], ALU.add),
                      reads=[kpb, kx], writes=[f"x3:{ci}"])
                kx3 = f"x3:{ci}"
                o("vector", lambda v, ci=ci: v.bn_stats(
                    stt[:, 0:6], x3b[:, ci * D:ci * D + 512]),
                  reads=[kx3], writes=["st0"])
                o("vector", lambda v, ci=ci: v.bn_stats(
                    stt[:, 6:12], x3b[:, ci * D + 512:ci * D + 1024]),
                  reads=[kx3], writes=["st1"])
                o("vector", lambda v: v.bn_aggr(stt[:, 12:14], stt[:, 0:12]),
                  reads=["st0", "st1"], writes=["mu"])
                o("vector", lambda v: v.tensor_scalar(stt[:, 14:15], stt[:, 13:14],
                                                      1e-6, None, ALU.add),
                  reads=["mu"], writes=["vpe"])
                o("scalar", lambda sc: sc.activation(stt[:, 15:16], stt[:, 14:15], AF.Sqrt),
                  reads=["vpe"], writes=["sig"])
                o("vector", lambda v: v.reciprocal(stt[:, 14:15], stt[:, 15:16]),
                  reads=["sig"], writes=["rstd"])
                o("vector", lambda v, ci=ci: v.scalar_tensor_tensor(
                    xmb[:], x3b[:, ci * D:(ci + 1) * D], stt[:, 12:13], g2r[:],
                    ALU.subtract, ALU.mult),
                  reads=[kx3, "mu", "g2r"], writes=["xmb"])
                o("vector", lambda v: v.scalar_tensor_tensor(
                    xmb[:], xmb[:], stt[:, 14:15], b2lr[:], ALU.mult, ALU.add),
                  reads=["xmb", "rstd", "b2lr"], writes=["xmb"])
                for c in range(4):
                    o("tensor", lambda t, c=c: t.matmul(
                        pS[:, 1024 + c * 128:1024 + (c + 1) * 128],
                        xmb[:, 2 * c * 128:(2 * c + 1) * 128], eye_b[:],
                        start=True, stop=True),
                      reads=["xmb", "eye_b"], writes=["pS2"], inc=(c == 3))
                    o("tensor", lambda t, c=c: t.matmul(
                        pS[:, 1536 + c * 128:1536 + (c + 1) * 128],
                        xmb[:, (2 * c + 1) * 128:(2 * c + 2) * 128], eye_b[:],
                        start=True, stop=True),
                      reads=["xmb", "eye_b"], writes=["pS3"], inc=(c == 3))
                for c in range(4):
                    o("vector", lambda v, c=c, ci=ci: v.tensor_copy(
                        xmT[2 * c][:, ci * 128:(ci + 1) * 128],
                        pS[:, 1024 + c * 128:1024 + (c + 1) * 128]),
                      reads=["pS2"], writes=[f"xmT{2 * c}:{ci}"], inc=(c == 3))
                    o("scalar", lambda sc, c=c, ci=ci: sc.copy(
                        xmT[2 * c + 1][:, ci * 128:(ci + 1) * 128],
                        pS[:, 1536 + c * 128:1536 + (c + 1) * 128]),
                      reads=["pS3"], writes=[f"xmT{2 * c + 1}:{ci}"], inc=(c == 3))
            xmk = [f"xmT{i}:{j}" for i in range(8) for j in range(MLPB // 128)]
            for mc in range(32):
                wb, kwb = w1b[mc % 2], f"w1b{mc % 2}"
                for dc in range(8):
                    d("gpsimd", wb[:, dc * 128:(dc + 1) * 128],
                      wm1d[dc * 128:(dc + 1) * 128, mc * 128:(mc + 1) * 128],
                      writes=[f"{kwb}:{dc}"], group=f"wm1{mc % 2}")
                pb, kpb = (p6, "p6") if mc % 2 == 0 else (p7, "p7")
                for dc in range(8):
                    o("tensor", lambda t, dc=dc, mc=mc, wb=wb, pb=pb: t.matmul(
                        pb[:, 0:MLPB], wb[:, dc * 128:(dc + 1) * 128],
                        xmT[dc][:, 0:MLPB], start=(dc == 0), stop=(dc == 7)),
                      reads=[f"{kwb}:{dc}"] + (xmk if dc == 0 else []),
                      writes=[kpb], inc=(dc == 7))
                o("vector", lambda v, mc=mc, pb=pb: v.tensor_scalar(
                    h1[:, mc * MLPB:(mc + 1) * MLPB], pb[:, 0:MLPB],
                    b1cc[:, mc:mc + 1], 0.0, ALU.add, ALU.max),
                  reads=[kpb, "b1cc"], writes=[f"h1:{mc}"])
            ypb = [pS[:, 0:512], pS[:, 512:1024], pS[:, 1024:1536], pS[:, 1536:2048]]
            for mc in range(32):
                wb, kwb = w2b[mc % 2], f"w2b{mc % 2}"
                d("gpsimd", wb[:], wm2d[mc * 128:(mc + 1) * 128, :],
                  writes=[kwb], group=f"wm2{mc % 2}")
                for ci in range(MLPB // 128):
                    for dh in range(2):
                        bi = ci * 2 + dh
                        o("tensor", lambda t, mc=mc, ci=ci, dh=dh, wb=wb, bi=bi: t.matmul(
                            ypb[bi], h1[:, mc * MLPB + ci * 128:mc * MLPB + (ci + 1) * 128],
                            wb[:, dh * 512:(dh + 1) * 512],
                            start=(mc == 0), stop=(mc == 31)),
                          reads=[f"h1:{mc}", kwb],
                          writes=[f"pS{bi}"],
                          inc=(ci == MLPB // 128 - 1 and dh == 1))
            for ci, tt in enumerate(tiles):
                d("sync", x1t[:], sx1[seq, tt * 128:(tt + 1) * 128, :],
                  writes=["x1t"], group="bx1r")
                for dh in range(2):
                    bi = ci * 2 + dh
                    o("vector", lambda v, ci=ci, dh=dh, bi=bi: v.tensor_tensor(
                        x3b[:, ci * D + dh * 512:ci * D + (dh + 1) * 512],
                        x3b[:, ci * D + dh * 512:ci * D + (dh + 1) * 512],
                        ypb[bi], ALU.add),
                      reads=[f"pS{bi}", f"x3:{ci}"], writes=[f"x3:{ci}"])
                    o("vector", lambda v, ci=ci, dh=dh: v.tensor_tensor(
                        x3b[:, ci * D + dh * 512:ci * D + (dh + 1) * 512],
                        x3b[:, ci * D + dh * 512:ci * D + (dh + 1) * 512],
                        b2rr[:, dh * 512:(dh + 1) * 512], ALU.add),
                      reads=[f"x3:{ci}", "b2rr"], writes=[f"x3:{ci}"])
                    o("vector", lambda v, ci=ci, dh=dh: v.tensor_tensor(
                        x3b[:, ci * D + dh * 512:ci * D + (dh + 1) * 512],
                        x3b[:, ci * D + dh * 512:ci * D + (dh + 1) * 512],
                        x1t[:, dh * 512:(dh + 1) * 512], ALU.add),
                      reads=[f"x3:{ci}", "x1t"], writes=[f"x3:{ci}"])
                d("sync", sx4[seq, tt * 128:(tt + 1) * 128, :],
                  x3b[:, ci * D:(ci + 1) * D], reads=[f"x3:{ci}"], group="x4out")
                if debug:
                    d("sync", dbg["d_x4"][seq, tt * 128:(tt + 1) * 128, :],
                      x3b[:, ci * D:(ci + 1) * D], reads=[f"x3:{ci}"], group="dbg4")
        arB.pop(); arF.pop()
        arB.pop(); arF.pop()

    for seq in range(NSEQ):
        stage_ffn(seq, xin, sx1)
        if debug:
            S_.barrier()
            arF.push()
            tmp = arF.alloc(D)
            for tt in range(NT):
                dt_ = d("sync", tmp[:], sx1[seq, tt * 128:(tt + 1) * 128, :],
                        writes=["dtmp"], group="dbg1")
                d("sync", dbg["d_x1"][seq, tt * 128:(tt + 1) * 128, :], tmp[:],
                  reads=["dtmp"], group="dbg1b")
            arF.pop()
        stage_attn(seq)
        stage_ffn(seq, sx4, out)
    S_.emit("sync")
    return nc, st, dbg


def _host_inputs(inputs, S=2048):
    f32 = np.float32
    i = inputs
    shared = {
        "w1t": np.ascontiguousarray(i["L0_ffn_w1"].T.astype(f32)),
        "w2t": np.ascontiguousarray(i["L0_ffn_w2"].T.astype(f32)),
        "convw": np.ascontiguousarray(
            i["L0_ffn_conv_w"].reshape(32, 128, 3).transpose(1, 0, 2).reshape(128, 96).astype(f32)),
        "convb": np.ascontiguousarray(
            i["L0_ffn_conv_b"].reshape(32, 128).T.astype(f32)),
        "g1d": i["L0_attn_ln1_g"].reshape(1, D).astype(f32),
        "b1d": i["L0_attn_ln1_b"].reshape(1, D).astype(f32),
        "g2d": i["L0_attn_ln2_g"].reshape(1, D).astype(f32),
        "b2d": i["L0_attn_ln2_b"].reshape(1, D).astype(f32),
        "wqd": np.ascontiguousarray(i["L0_attn_Wq"].reshape(D, H * DH).astype(f32)),
        "wkd": np.ascontiguousarray(i["L0_attn_Wk"].reshape(D, H * DH).astype(f32)),
        "wvd": np.ascontiguousarray(i["L0_attn_Wv"].reshape(D, H * DH).astype(f32)),
        "whqd": np.ascontiguousarray(i["L0_attn_Whq"].reshape(H * DH, NBC).astype(f32)),
        "whkd": np.ascontiguousarray(i["L0_attn_Whk"].reshape(H * DH, NBC).astype(f32)),
        "wod": np.ascontiguousarray(
            i["L0_attn_Wo"].reshape(H // 2, 2 * DH, D).astype(f32)),
        "wm1d": np.ascontiguousarray(i["L0_attn_W1"].astype(f32)),
        "wm2d": np.ascontiguousarray(i["L0_attn_W2"].astype(f32)),
        "b1cd": np.ascontiguousarray(i["L0_attn_b1"].reshape(32, 128).T.astype(f32)),
        "b2rd": i["L0_attn_b2"].reshape(1, D).astype(f32),
        "eyebd": np.eye(128, dtype=f32).astype(bfdt),
        "eyefd": np.eye(128, dtype=f32),
        "onesfd": np.ones((128, 128), dtype=f32),
    }
    x = np.ascontiguousarray(i["L0_attn_inputs"].astype(f32))
    in_maps = []
    for c in range(NCORES):
        m = dict(shared)
        m["xin"] = np.ascontiguousarray(x[c * NSEQ:(c + 1) * NSEQ, :S, :])
        in_maps.append(m)
    return in_maps


_CACHE = {}


def kernel(**inputs):
    if "prog" not in _CACHE:
        nc, st, _ = build_program(S=2048, debug=False)
        _CACHE["prog"] = (nc, st)
    nc, st = _CACHE["prog"]
    in_maps = _host_inputs(inputs, S=2048)
    res = run_bass_kernel_spmd(nc, in_maps, core_ids=list(range(NCORES)))
    outp = np.concatenate([res.results[c]["out"] for c in range(NCORES)], axis=0)
    return outp.astype(np.float32)


# version 02
# speedup vs baseline: 1.6794x; 1.6794x over previous
import numpy as np
import ml_dtypes
from contextlib import ExitStack
from typing import NamedTuple

import concourse.bass as bass
import concourse.mybir as mybir
from concourse.bass_utils import run_bass_kernel_spmd

F32 = mybir.dt.float32
BF16 = mybir.dt.bfloat16
ALU = mybir.AluOpType
AF = mybir.ActivationFunctionType
AX = mybir.AxisListType
bfdt = ml_dtypes.bfloat16

MAGIC = 12582912.0
EXPB = -20.0
D, H, DH, NBC, FF = 1024, 16, 64, 32, 4096
NSEQ, NCORES = 2, 8
ENGINES = ("sync", "tensor", "vector", "scalar", "gpsimd")


class Tok(NamedTuple):
    key: str
    val: int


class _Rec:

    def __getattr__(self, name):
        def call(*a, **k):
            return (name, a, k)
        return call


class Sched:

    def __init__(self, nc, stack):
        self.nc = nc
        self.stack = stack
        self.sem = {e: stack.enter_context(nc.semaphore("S_" + e)) for e in ENGINES}
        self.cnt = {e: 0 for e in ENGINES}
        self.opl = {e: [] for e in ENGINES}
        self.water = {}
        self.dma_groups = {}
        self.pend_bar = {}
        self.wmap = {}
        self.rmap = {}
        self.pend_rw = {e: [] for e in ENGINES}

    def group(self, name):
        if name not in self.dma_groups:
            self.dma_groups[name] = self.stack.enter_context(
                self.nc.semaphore("G_" + name))
            self.cnt[name] = 0
        return name

    def barrier(self):
        toks = [Tok(k, v) for k, v in self.cnt.items() if v > 0]
        for e in ENGINES:
            self.pend_bar[e] = toks
        self.wmap = {}
        self.rmap = {}

    def _rw_deps(self, reads, writes):
        deps = []
        for k in reads:
            deps += self.wmap.get(k, [])
        for k in writes:
            deps += self.wmap.get(k, [])
            deps += self.rmap.get(k, [])
        return deps

    def _rw_commit(self, tok, reads, writes):
        for k in writes:
            self.wmap[k] = [tok]
            self.rmap[k] = []
        for k in reads:
            self.rmap.setdefault(k, []).append(tok)

    def _resolve(self, eng, deps):
        byk = {}
        for t in deps:
            if t is None:
                continue
            v = t.val
            if t.key in self.dma_groups:
                v = self.cnt[t.key]
            if byk.get(t.key, 0) < v:
                byk[t.key] = v
        waits = []
        for k, v in sorted(byk.items()):
            wk = (eng, k)
            if self.water.get(wk, 0) >= v:
                continue
            self.water[wk] = v
            waits.append((k, v))
        return waits

    def op(self, eng, fn, reads=(), writes=(), deps=(), inc=True):
        deps = list(deps) + self._rw_deps(reads, writes)
        if self.pend_bar.get(eng):
            deps += self.pend_bar[eng]
            self.pend_bar[eng] = None
        waits = self._resolve(eng, deps)
        tok, inckey = None, None
        if inc:
            self.cnt[eng] += 1
            tok = Tok(eng, self.cnt[eng])
            inckey = eng
        self.opl[eng].append((fn(_Rec()), waits, inckey, 1))
        if tok is not None:
            for (r2, w2) in self.pend_rw[eng]:
                self._rw_commit(tok, r2, w2)
            self.pend_rw[eng] = []
            self._rw_commit(tok, reads, writes)
        else:
            self.pend_rw[eng].append((list(reads), list(writes)))
        return tok

    def dma(self, eng, out, in_, reads=(), writes=(), deps=(), group="misc", **kw):
        g = self.group(group)
        deps = list(deps) + self._rw_deps(reads, writes)
        if self.pend_bar.get(eng):
            deps += self.pend_bar[eng]
            self.pend_bar[eng] = None
        waits = self._resolve(eng, deps)
        self.cnt[g] += 16
        tok = Tok(g, self.cnt[g])
        self.opl[eng].append((("dma_start", (out, in_), kw), waits, g, 16))
        self._rw_commit(tok, reads, writes)
        return tok

    def emit(self, final_eng="sync"):
        finals = [Tok(g, self.cnt[g]) for g in self.dma_groups if self.cnt[g] > 0]
        self.opl[final_eng].append((None, self._resolve(final_eng, finals), None, 0))
        with self.nc.Block() as block:
            for e in ENGINES:
                if not self.opl[e]:
                    continue
                def body(E, _ops=tuple(self.opl[e])):
                    for spec, waits, inckey, amt in _ops:
                        for (k, v) in waits:
                            E.wait_ge(self._sem_of(k), v)
                        if spec is not None:
                            name, a, kw = spec
                            inst = getattr(E, name)(*a, **kw)
                            if inckey is not None:
                                inst.then_inc(self._sem_of(inckey), amt)
                getattr(block, e)(body)

    def _sem_of(self, key):
        return self.sem[key] if key in self.sem else self.dma_groups[key]


class Arena:
    def __init__(self, tensor, cols):
        self.t, self.cols, self.off, self.marks = tensor, cols, 0, []

    def alloc(self, n):
        n = (n + 1) & ~1
        assert self.off + n <= self.cols, f"arena overflow {self.off}+{n}>{self.cols}"
        ap = self.t[:, self.off:self.off + n]
        self.off += n
        return ap

    def push(self):
        self.marks.append(self.off)

    def pop(self):
        self.off = self.marks.pop()


def build_program(S=2048, debug=False):
    TB = 128
    MLPB = 256
    NT = S // 128
    NKC = S // 128
    QB = min(512, S)
    NQB = S // QB
    NTB = S // TB
    NMB = S // MLPB
    W512 = min(512, S)

    nc = bass.Bass("TRN2", target_bir_lowering=False)
    st = ExitStack()
    S_ = Sched(nc, st)
    o, d = S_.op, S_.dma

    xin = nc.dram_tensor("xin", [NSEQ, S, D], F32, kind="ExternalInput")
    out = nc.dram_tensor("out", [NSEQ, S, D], F32, kind="ExternalOutput")
    w1t = nc.dram_tensor("w1t", [D, FF], F32, kind="ExternalInput")
    w2t = nc.dram_tensor("w2t", [FF, D], F32, kind="ExternalInput")
    convw = nc.dram_tensor("convw", [128, 3 * (FF // 128)], F32, kind="ExternalInput")
    convb = nc.dram_tensor("convb", [128, FF // 128], F32, kind="ExternalInput")
    g1d = nc.dram_tensor("g1d", [1, D], F32, kind="ExternalInput")
    b1d = nc.dram_tensor("b1d", [1, D], F32, kind="ExternalInput")
    g2d = nc.dram_tensor("g2d", [1, D], F32, kind="ExternalInput")
    b2d = nc.dram_tensor("b2d", [1, D], F32, kind="ExternalInput")
    wqd = nc.dram_tensor("wqd", [D, H * DH], F32, kind="ExternalInput")
    wkd = nc.dram_tensor("wkd", [D, H * DH], F32, kind="ExternalInput")
    wvd = nc.dram_tensor("wvd", [D, H * DH], F32, kind="ExternalInput")
    whqd = nc.dram_tensor("whqd", [H * DH, NBC], F32, kind="ExternalInput")
    whkd = nc.dram_tensor("whkd", [H * DH, NBC], F32, kind="ExternalInput")
    wod = nc.dram_tensor("wod", [H // 2, 2 * DH, D], F32, kind="ExternalInput")
    wm1d = nc.dram_tensor("wm1d", [D, FF], F32, kind="ExternalInput")
    wm2d = nc.dram_tensor("wm2d", [FF, D], F32, kind="ExternalInput")
    b1cd = nc.dram_tensor("b1cd", [128, FF // 128], F32, kind="ExternalInput")
    b2rd = nc.dram_tensor("b2rd", [1, D], F32, kind="ExternalInput")
    eyebd = nc.dram_tensor("eyebd", [128, 128], BF16, kind="ExternalInput")
    eyefd = nc.dram_tensor("eyefd", [128, 128], F32, kind="ExternalInput")
    onesfd = nc.dram_tensor("onesfd", [128, 128], F32, kind="ExternalInput")

    sx1 = nc.dram_tensor("sx1", [NSEQ, S, D], F32)
    su = nc.dram_tensor("su", [NSEQ, S, D], F32)
    sx4 = nc.dram_tensor("sx4", [NSEQ, S, D], F32)
    sq1 = nc.dram_tensor("sq1", [D, FF], BF16)
    sq2 = nc.dram_tensor("sq2", [FF, D], BF16)
    bwq = nc.dram_tensor("bwq", [D, H * DH], BF16)
    bwk = nc.dram_tensor("bwk", [D, H * DH], BF16)
    bwv = nc.dram_tensor("bwv", [D, H * DH], BF16)
    bwo = nc.dram_tensor("bwo", [H // 2, 2 * DH, D], BF16)
    bw1 = nc.dram_tensor("bw1", [D, FF], BF16)
    bw2 = nc.dram_tensor("bw2", [FF, D], BF16)
    dbg = {}
    if debug:
        for nm in ("d_x1", "d_u", "d_x4"):
            dbg[nm] = nc.dram_tensor(nm, [NSEQ, S, D], F32, kind="ExternalOutput")

    BCOLS = 80 * 1024
    FCOLS = 6592
    arB_t = st.enter_context(nc.sbuf_tensor("arB", [128, BCOLS], BF16))
    arF_t = st.enter_context(nc.sbuf_tensor("arF", [128, FCOLS], F32))
    eye_b = st.enter_context(nc.sbuf_tensor("eye_b", [128, 128], BF16))
    eye_f = st.enter_context(nc.sbuf_tensor("eye_f", [128, 128], F32))
    ones_f = st.enter_context(nc.sbuf_tensor("ones_f", [128, 128], F32))
    cvw = st.enter_context(nc.sbuf_tensor("cvw", [128, 3 * (FF // 128)], F32))
    cvb = st.enter_context(nc.sbuf_tensor("cvb", [128, FF // 128], F32))
    g1r = st.enter_context(nc.sbuf_tensor("g1r", [128, D], F32))
    b1r = st.enter_context(nc.sbuf_tensor("b1r", [128, D], F32))
    g2r = st.enter_context(nc.sbuf_tensor("g2r", [128, D], F32))
    b2lr = st.enter_context(nc.sbuf_tensor("b2lr", [128, D], F32))
    b2rr = st.enter_context(nc.sbuf_tensor("b2rr", [128, D], F32))
    b1cc = st.enter_context(nc.sbuf_tensor("b1cc", [128, FF // 128], F32))
    mwr = st.enter_context(nc.sbuf_tensor("mwr", [128, 8], F32))
    nb20 = st.enter_context(nc.sbuf_tensor("nb20", [128, 1], F32))
    pS = st.enter_context(nc.psum_tensor("pS", [128, 2048], F32))
    p4 = st.enter_context(nc.psum_tensor("p4", [128, 512], F32))
    p5 = st.enter_context(nc.psum_tensor("p5", [128, 512], F32))
    p6 = st.enter_context(nc.psum_tensor("p6", [128, 512], F32))
    p7 = st.enter_context(nc.psum_tensor("p7", [128, 512], F32))
    arB = Arena(arB_t, BCOLS)
    arF = Arena(arF_t, FCOLS)

    for t_, src_ in ((eye_b, eyebd), (eye_f, eyefd), (ones_f, onesfd), (cvw, convw),
                     (cvb, convb), (b1cc, b1cd)):
        d("sync", t_[:], src_[:], writes=[t_.name], group="cst")
    o("vector", lambda v: v.memset(nb20[:], EXPB), writes=["nb20"])
    for t_, src_ in ((g1r, g1d), (b1r, b1d), (g2r, g2d), (b2lr, b2d), (b2rr, b2rd)):
        d("sync", t_[:], src_[0:1, :].partition_broadcast(128), writes=[t_.name],
          group="cst")

    def quantize_weight(wdram, sqdram, nrow, ncol, mcol):
        nstrip = nrow // 128
        arF.push(); arB.push()
        wstrip = arF.alloc(ncol)
        qs = arB.alloc(ncol)
        o("vector", lambda v: v.memset(mwr[:, 7:8], 0.0), writes=["acc"])
        for i in range(nstrip):
            d("sync", wstrip[:], wdram[i * 128:(i + 1) * 128, :],
              writes=["wstrip"], group="wqld")
            o("vector", lambda v: v.tensor_reduce(
                mwr[:, 6:7], wstrip[:], AX.X, ALU.add, apply_absolute_value=True),
              reads=["wstrip"], writes=["red"])
            o("vector", lambda v: v.tensor_tensor(
                mwr[:, 7:8], mwr[:, 7:8], mwr[:, 6:7], ALU.add),
              reads=["red", "acc"], writes=["acc"])
        o("tensor", lambda t: t.matmul(p4[0:1, 0:1], ones_f[:, 0:1], mwr[:, 7:8],
                                       start=True, stop=True),
          reads=["acc", "ones_f"], writes=["p4"])
        o("vector", lambda v: v.tensor_scalar(
            mwr[0:1, 6:7], p4[0:1, 0:1], 1.0 / (nrow * ncol), 1e-5, ALU.mult, ALU.max),
          reads=["p4"], writes=["mean1"])
        o("tensor", lambda t: t.matmul(p4[:, 1:2], ones_f[0:1, :], mwr[0:1, 6:7],
                                       start=True, stop=True),
          reads=["mean1", "ones_f"], writes=["p4"])
        o("vector", lambda v: v.tensor_copy(mwr[:, mcol:mcol + 1], p4[:, 1:2]),
          reads=["p4"], writes=[f"mw{mcol}"])
        o("vector", lambda v: v.reciprocal(mwr[:, mcol + 1:mcol + 2],
                                           mwr[:, mcol:mcol + 1]),
          reads=[f"mw{mcol}"], writes=[f"mwi{mcol}"])
        for i in range(nstrip):
            d("sync", wstrip[:], wdram[i * 128:(i + 1) * 128, :],
              writes=["wstrip"], group="wqld")
            o("vector", lambda v: v.tensor_scalar(
                wstrip[:], wstrip[:], mwr[:, mcol + 1:mcol + 2], MAGIC,
                ALU.mult, ALU.add),
              reads=["wstrip", f"mwi{mcol}"], writes=["wstrip"])
            o("vector", lambda v: v.tensor_scalar(
                wstrip[:], wstrip[:], MAGIC, None, ALU.subtract),
              reads=["wstrip"], writes=["wstrip"])
            o("vector", lambda v: v.tensor_scalar(
                qs[:], wstrip[:], -1.0, 1.0, ALU.max, ALU.min),
              reads=["wstrip"], writes=["qs"])
            d("sync", sqdram[i * 128:(i + 1) * 128, :], qs[:],
              reads=["qs"], group="wqst")
        arF.pop(); arB.pop()

    for dst_, src_, nr in ((bwq, wqd, D), (bwk, wkd, D), (bwv, wvd, D),
                           (bw1, wm1d, D)):
        for c_ in range(2):
            d("gpsimd", dst_[c_ * nr // 2:(c_ + 1) * nr // 2, :],
              src_[c_ * nr // 2:(c_ + 1) * nr // 2, :], group="wcast")
    for c_ in range(8):
        d("gpsimd", bw2[c_ * FF // 8:(c_ + 1) * FF // 8, :],
          wm2d[c_ * FF // 8:(c_ + 1) * FF // 8, :], group="wcast")
    for pg_ in range(8):
        d("gpsimd", bwo[pg_, :, :], wod[pg_, :, :], group="wcast")
    quantize_weight(w1t, sq1, D, FF, 0)
    quantize_weight(w2t, sq2, FF, D, 2)
    o("vector", lambda v: v.tensor_scalar(mwr[:, 4:5], mwr[:, 0:1],
                                          1.0 / 127.0, None, ALU.mult),
      reads=["mw0"], writes=["mw1s"])
    o("vector", lambda v: v.tensor_scalar(mwr[:, 5:6], mwr[:, 2:3],
                                          0.5 / 127.0, None, ALU.mult),
      reads=["mw2"], writes=["mw2s"])

    def stage_ffn(seq, src, dst):
        S_.barrier()
        arB.push(); arF.push()
        w1q = [arB.alloc(FF) for _ in range(8)]
        w2q = [arB.alloc(D) for _ in range(32)]
        xqw = [[arB.alloc(TB + 2) for _ in range(8)] for _ in range(2)]
        xq2 = arB.alloc(32 * TB)
        q1b = arB.alloc(D)
        csr = arB.alloc(2 * S).bitcast(F32)
        f2r = arB.alloc(2 * TB).bitcast(F32)
        xt = [arF.alloc(D) for _ in range(2)]
        sl = arF.alloc(32 * TB)
        hb = [arB.alloc(2 * (TB + 2)).bitcast(F32) for _ in range(2)]
        yb = [arB.alloc(2 * TB).bitcast(F32) for _ in range(2)]
        ab = [arB.alloc(2 * TB).bitcast(F32) for _ in range(2)]
        bcc = arF.alloc(TB + 2)
        bc2 = arF.alloc(TB)
        m1b = arF.alloc(TB)
        stt = arF.alloc(16)

        o("vector", lambda v: v.memset(bcc[:], 0.0), writes=["bcc"])
        for i in range(8):
            d("sync", w1q[i][:], sq1[i * 128:(i + 1) * 128, :],
              writes=[f"w1q{i}"], group="wst")
        for i in range(32):
            d("sync", w2q[i][:], sq2[i * 128:(i + 1) * 128, :],
              writes=[f"w2q{i}"], group="wst")

        def quant_tile(tt):
            xa, kx = xt[tt % 2], f"xt{tt % 2}"
            d("sync", xa[:], src[seq, tt * 128:(tt + 1) * 128, :],
              writes=[kx], group=f"xin{tt % 2}")
            o("vector", lambda v: v.bn_stats(stt[:, 0:6], xa[:, 0:512]),
              reads=[kx], writes=["st0"])
            o("vector", lambda v: v.bn_stats(stt[:, 6:12], xa[:, 512:1024]),
              reads=[kx], writes=["st1"])
            o("vector", lambda v: v.bn_aggr(stt[:, 12:14], stt[:, 0:12]),
              reads=["st0", "st1"], writes=["mu"])
            o("vector", lambda v: v.tensor_scalar(stt[:, 14:15], stt[:, 13:14],
                                                  1e-5, None, ALU.add),
              reads=["mu"], writes=["vpe"])
            o("scalar", lambda sc: sc.activation(stt[:, 15:16], stt[:, 14:15], AF.Sqrt),
              reads=["vpe"], writes=["sig"])
            o("vector", lambda v: v.reciprocal(stt[:, 14:15], stt[:, 15:16]),
              reads=["sig"], writes=["rstd"])
            o("vector", lambda v: v.tensor_scalar(
                xa[:], xa[:], stt[:, 12:13], None, ALU.subtract),
              reads=[kx, "mu"], writes=[kx])
            o("vector", lambda v: v.tensor_reduce(
                stt[:, 0:1], xa[:], AX.X, ALU.max, apply_absolute_value=True),
              reads=[kx], writes=["am"])
            o("vector", lambda v: v.tensor_scalar(
                stt[:, 1:2], stt[:, 0:1], stt[:, 14:15], 1e-5, ALU.mult, ALU.max),
              reads=["am", "rstd"], writes=["u2"])
            o("vector", lambda v: v.reciprocal(stt[:, 2:3], stt[:, 1:2]),
              reads=["u2"], writes=["iu2"])
            o("vector", lambda v: v.tensor_scalar(
                stt[:, 3:4], stt[:, 2:3], 127.0, None, ALU.mult),
              reads=["iu2"], writes=["f0"])
            o("vector", lambda v: v.tensor_tensor(
                stt[:, 3:4], stt[:, 3:4], stt[:, 14:15], ALU.mult),
              reads=["f0", "rstd"], writes=["f"])
            o("vector", lambda v: v.tensor_tensor(
                stt[:, 4:5], stt[:, 1:2], mwr[:, 4:5], ALU.mult),
              reads=["u2", "mw1s"], writes=["cs"])
            o("vector", lambda v: v.tensor_scalar(
                xa[:], xa[:], stt[:, 3:4], MAGIC, ALU.mult, ALU.add),
              reads=[kx, "f"], writes=[kx])
            o("vector", lambda v: v.tensor_scalar(
                q1b[:], xa[:], MAGIC, None, ALU.subtract),
              reads=[kx], writes=["q1b"])
            o("tensor", lambda t: t.matmul(pS[0:1, 0:128], stt[:, 4:5], eye_f[:],
                                           start=True, stop=True),
              reads=["cs", "eye_f"], writes=["pS0"])
            o("vector", lambda v: v.tensor_copy(
                csr[0:1, tt * 128:(tt + 1) * 128], pS[0:1, 0:128]),
              reads=["pS0"], writes=[f"csr{tt}"])
            w = tt % 2
            for c in range(4):
                o("tensor", lambda t, c=c: t.matmul(
                    pS[:, 1024 + c * 128:1024 + (c + 1) * 128],
                    q1b[:, 2 * c * 128:(2 * c + 1) * 128], eye_b[:],
                    start=True, stop=True),
                  reads=["q1b", "eye_b"], writes=["pS2"], inc=(c == 3))
                o("tensor", lambda t, c=c: t.matmul(
                    pS[:, 1536 + c * 128:1536 + (c + 1) * 128],
                    q1b[:, (2 * c + 1) * 128:(2 * c + 2) * 128], eye_b[:],
                    start=True, stop=True),
                  reads=["q1b", "eye_b"], writes=["pS3"], inc=(c == 3))
            for c in range(4):
                o("vector", lambda v, c=c, w=w: v.tensor_copy(
                    xqw[w][2 * c][:, 1:129],
                    pS[:, 1024 + c * 128:1024 + (c + 1) * 128]),
                  reads=["pS2"], writes=[f"win{w}:{2 * c}"], inc=(c == 3))
                o("scalar", lambda sc, c=c, w=w: sc.copy(
                    xqw[w][2 * c + 1][:, 1:129],
                    pS[:, 1536 + c * 128:1536 + (c + 1) * 128]),
                  reads=["pS3"], writes=[f"win{w}:{2 * c + 1}"], inc=(c == 3))
            if tt == 0:
                for dc in range(8):
                    o("vector", lambda v, dc=dc: v.memset(xqw[0][dc][:, 0:1], 0.0),
                      reads=[], writes=[f"win0:{dc}"], inc=(dc == 7))
            else:
                pw = (tt - 1) % 2
                for dc in range(8):
                    o("vector", lambda v, dc=dc, w=w, pw=pw: v.tensor_copy(
                        xqw[w][dc][:, 0:1], xqw[pw][dc][:, 128:129]),
                      reads=[f"win{pw}:{dc}"], writes=[f"win{w}:{dc}"], inc=(dc == 7))
                for dc in range(8):
                    o("vector", lambda v, dc=dc, w=w, pw=pw: v.tensor_copy(
                        xqw[pw][dc][:, 129:130], xqw[w][dc][:, 1:2]),
                      reads=[f"win{w}:{dc}"], writes=[f"win{pw}:{dc}"], inc=(dc == 7))
            if tt == NT - 1:
                for dc in range(8):
                    o("vector", lambda v, dc=dc, w=w: v.memset(xqw[w][dc][:, 129:130], 0.0),
                      reads=[], writes=[f"win{w}:{dc}"], inc=(dc == 7))

        quant_tile(0)
        for tb in range(NTB):
            if tb + 1 < NT:
                quant_tile(tb + 1)
            w = tb % 2
            t0 = tb * TB
            lo, hi = max(0, t0 - 1), min(S, t0 + TB + 1)
            rt = [f"csr{i}" for i in range(max(0, tb - 1), min(NT, tb + 2))]
            o("tensor", lambda t: t.matmul(
                pS[:, 0:hi - lo], ones_f[0:1, :], csr[0:1, lo:hi],
                start=True, stop=True),
              reads=rt + ["ones_f"], writes=["pS0"])
            o("vector", lambda v: v.tensor_copy(
                bcc[:, lo - (t0 - 1):hi - (t0 - 1)], pS[:, 0:hi - lo]),
              reads=["pS0"], writes=["bcc"])
            if t0 == 0:
                o("vector", lambda v: v.memset(bcc[:, 0:1], 0.0), writes=["bcc"])
            if t0 + TB == S:
                o("vector", lambda v: v.memset(bcc[:, TB + 1:TB + 2], 0.0), writes=["bcc"])
            o("vector", lambda v: v.memset(m1b[:], 0.0), writes=["m1b"])

            winkeys = [f"win{w}:{dc}" for dc in range(8)]
            for mc in range(32):
                pb, kpb = (p4, "p4") if mc % 2 == 0 else (p5, "p5")
                for dc in range(8):
                    o("tensor", lambda t, mc=mc, dc=dc, pb=pb, w=w: t.matmul(
                        pb[:, 0:TB + 2], w1q[dc][:, mc * 128:(mc + 1) * 128],
                        xqw[w][dc][:, 0:TB + 2],
                        start=(dc == 0), stop=(dc == 7)),
                      reads=[f"w1q{dc}"] + (winkeys if dc == 0 else []),
                      writes=[kpb], inc=(dc == 7))
                hbx, khb = (hb[0], "hb0") if mc % 2 == 0 else (hb[1], "hb1")
                o("vector", lambda v, pb=pb, hbx=hbx: v.tensor_tensor(
                    hbx[:], pb[:, 0:TB + 2], bcc[:], ALU.mult),
                  reads=[kpb, "bcc"], writes=[khb])
                ybx, kyb = (yb[0], "yb0") if mc % 2 == 0 else (yb[1], "yb1")
                o("vector", lambda v, hbx=hbx, mc=mc, ybx=ybx: v.tensor_scalar(
                    ybx[:], hbx[:, 1:TB + 1], cvw[:, 3 * mc + 1:3 * mc + 2],
                    cvb[:, mc:mc + 1], ALU.mult, ALU.add),
                  reads=[khb, "cvw", "cvb"], writes=[kyb])
                o("vector", lambda v, hbx=hbx, mc=mc, ybx=ybx: v.scalar_tensor_tensor(
                    ybx[:], hbx[:, 0:TB], cvw[:, 3 * mc:3 * mc + 1],
                    ybx[:], ALU.mult, ALU.add),
                  reads=[khb, kyb, "cvw"], writes=[kyb])
                o("vector", lambda v, hbx=hbx, mc=mc, ybx=ybx: v.scalar_tensor_tensor(
                    ybx[:], hbx[:, 2:TB + 2], cvw[:, 3 * mc + 2:3 * mc + 3],
                    ybx[:], ALU.mult, ALU.add),
                  reads=[khb, kyb, "cvw"], writes=[kyb])
                o("scalar", lambda sc, mc=mc, ybx=ybx: sc.activation(
                    sl[:, mc * TB:(mc + 1) * TB], ybx[:], AF.Silu),
                  reads=[kyb], writes=[f"sl{mc}"])
            for mc in range(32):
                ka = "ab0" if mc % 2 == 0 else "ab1"
                abx = ab[0] if mc % 2 == 0 else ab[1]
                o("vector", lambda v, mc=mc, abx=abx: v.scalar_tensor_tensor(
                    abx[:], sl[:, mc * TB:(mc + 1) * TB],
                    -1.0, sl[:, mc * TB:(mc + 1) * TB], ALU.mult, ALU.max),
                  reads=[f"sl{mc}"], writes=[ka])
                o("vector", lambda v, mc=mc, abx=abx: v.tensor_tensor(
                    m1b[:], m1b[:], abx[:], ALU.max),
                  reads=[ka, "m1b"], writes=["m1b"])
            o("tensor", lambda t: t.matmul(pS[:, 512:640], m1b[:], eye_f[:],
                                           start=True, stop=True),
              reads=["m1b", "eye_f"], writes=["pS1"])
            o("vector", lambda v: v.tensor_reduce(
                stt[:, 8:9], pS[:, 512:640], AX.X, ALU.max),
              reads=["pS1"], writes=["m2r"])
            o("vector", lambda v: v.tensor_scalar(
                stt[:, 10:11], stt[:, 8:9], 1e-5, None, ALU.max),
              reads=["m2r"], writes=["m2"])
            o("vector", lambda v: v.reciprocal(stt[:, 12:13], stt[:, 10:11]),
              reads=["m2"], writes=["im2"])
            o("vector", lambda v: v.tensor_scalar(
                stt[:, 12:13], stt[:, 12:13], 127.0, None, ALU.mult),
              reads=["im2"], writes=["f2"])
            o("vector", lambda v: v.tensor_tensor(
                stt[:, 14:15], stt[:, 10:11], mwr[:, 5:6], ALU.mult),
              reads=["m2", "mw2s"], writes=["cs2"])
            o("tensor", lambda t: t.matmul(pS[0:1, 512:640], stt[:, 12:13], eye_f[:],
                                           start=True, stop=True),
              reads=["f2", "eye_f"], writes=["pS1"])
            o("vector", lambda v: v.tensor_copy(f2r[0:1, 0:TB], pS[0:1, 512:640]),
              reads=["pS1"], writes=["f2r"])
            o("tensor", lambda t: t.matmul(pS[:, 512:640], ones_f[0:1, :],
                                           f2r[0:1, 0:TB], start=True, stop=True),
              reads=["f2r", "ones_f"], writes=["pS1"])
            o("vector", lambda v: v.tensor_copy(bc2[:], pS[:, 512:640]),
              reads=["pS1"], writes=["bc2"])
            for mc in range(32):
                o("vector", lambda v, mc=mc: v.tensor_tensor(
                    sl[:, mc * TB:(mc + 1) * TB], sl[:, mc * TB:(mc + 1) * TB],
                    bc2[:], ALU.mult),
                  reads=[f"sl{mc}", "bc2"], writes=[f"sl{mc}"])
                o("vector", lambda v, mc=mc: v.tensor_scalar(
                    xq2[:, mc * TB:(mc + 1) * TB], sl[:, mc * TB:(mc + 1) * TB],
                    MAGIC, MAGIC, ALU.add, ALU.subtract),
                  reads=[f"sl{mc}"], writes=[f"xq2{mc}"])
            xa, kx = xt[tb % 2], f"xt{tb % 2}"
            d("sync", xa[:], src[seq, t0:t0 + TB, :], writes=[kx], group=f"xin{tb % 2}")
            for dh in range(2):
                pb, kpb = (p6, "p6") if dh == 0 else (p7, "p7")
                for mc in range(32):
                    o("tensor", lambda t, mc=mc, dh=dh, pb=pb: t.matmul(
                        pb[:], xq2[:, mc * TB:(mc + 1) * TB],
                        w2q[mc][:, dh * 512:(dh + 1) * 512],
                        start=(mc == 0), stop=(mc == 31)),
                      reads=[f"xq2{mc}", f"w2q{mc}"],
                      writes=[kpb], inc=(mc == 31))
                o("vector", lambda v, pb=pb, dh=dh, xa=xa: v.scalar_tensor_tensor(
                    xa[:, dh * 512:(dh + 1) * 512], pb[:], stt[:, 14:15],
                    xa[:, dh * 512:(dh + 1) * 512], ALU.mult, ALU.add),
                  reads=[kpb, "cs2", kx], writes=[kx])
            d("sync", dst[seq, t0:t0 + TB, :], xa[:], reads=[kx], group="xout")
        arB.pop(); arF.pop()

    def stage_attn(seq):
        S_.barrier()
        arB.push(); arF.push()
        oall = [arB.alloc(S) for _ in range(8)]
        xnbuf = arB.alloc(D)
        whq8 = arB.alloc(2 * 8 * NBC)
        ut = [arF.alloc(D) for _ in range(2)]
        stt = arF.alloc(16)
        lrec = arF.alloc(QB)
        bcst = arF.alloc(QB)
        wtmp = arF.alloc(NBC)
        arB.push()
        xnT = [arB.alloc(S) for _ in range(8)]

        for tt in range(NT):
            xa, kx = ut[tt % 2], f"ut{tt % 2}"
            d("sync", xa[:], sx1[seq, tt * 128:(tt + 1) * 128, :],
              writes=[kx], group=f"bx{tt % 2}")
            o("vector", lambda v: v.bn_stats(stt[:, 0:6], xa[:, 0:512]),
              reads=[kx], writes=["st0"])
            o("vector", lambda v: v.bn_stats(stt[:, 6:12], xa[:, 512:1024]),
              reads=[kx], writes=["st1"])
            o("vector", lambda v: v.bn_aggr(stt[:, 12:14], stt[:, 0:12]),
              reads=["st0", "st1"], writes=["mu"])
            o("vector", lambda v: v.tensor_scalar(stt[:, 14:15], stt[:, 13:14],
                                                  1e-5, None, ALU.add),
              reads=["mu"], writes=["vpe"])
            o("scalar", lambda sc: sc.activation(stt[:, 15:16], stt[:, 14:15], AF.Sqrt),
              reads=["vpe"], writes=["sig"])
            o("vector", lambda v: v.reciprocal(stt[:, 14:15], stt[:, 15:16]),
              reads=["sig"], writes=["rstd"])
            o("vector", lambda v: v.tensor_scalar(
                xa[:], xa[:], stt[:, 12:13], stt[:, 14:15], ALU.subtract, ALU.mult),
              reads=[kx, "mu", "rstd"], writes=[kx])
            d("sync", su[seq, tt * 128:(tt + 1) * 128, :], xa[:], reads=[kx], group="su")
            if debug:
                d("sync", dbg["d_u"][seq, tt * 128:(tt + 1) * 128, :], xa[:],
                  reads=[kx], group="dbgu")
            o("vector", lambda v: v.bn_stats(stt[:, 0:6], xa[:, 0:512]),
              reads=[kx], writes=["st0"])
            o("vector", lambda v: v.bn_stats(stt[:, 6:12], xa[:, 512:1024]),
              reads=[kx], writes=["st1"])
            o("vector", lambda v: v.bn_aggr(stt[:, 12:14], stt[:, 0:12]),
              reads=["st0", "st1"], writes=["mu"])
            o("vector", lambda v: v.tensor_scalar(stt[:, 14:15], stt[:, 13:14],
                                                  1e-6, None, ALU.add),
              reads=["mu"], writes=["vpe"])
            o("scalar", lambda sc: sc.activation(stt[:, 15:16], stt[:, 14:15], AF.Sqrt),
              reads=["vpe"], writes=["sig"])
            o("vector", lambda v: v.reciprocal(stt[:, 14:15], stt[:, 15:16]),
              reads=["sig"], writes=["rstd"])
            o("vector", lambda v: v.scalar_tensor_tensor(
                xa[:], xa[:], stt[:, 12:13], g1r[:], ALU.subtract, ALU.mult),
              reads=[kx, "mu", "g1r"], writes=[kx])
            o("vector", lambda v: v.scalar_tensor_tensor(
                xnbuf[:], xa[:], stt[:, 14:15], b1r[:], ALU.mult, ALU.add),
              reads=[kx, "rstd", "b1r"], writes=["xnbuf"])
            for c in range(4):
                o("tensor", lambda t, c=c: t.matmul(
                    pS[:, 1024 + c * 128:1024 + (c + 1) * 128],
                    xnbuf[:, 2 * c * 128:(2 * c + 1) * 128], eye_b[:],
                    start=True, stop=True),
                  reads=["xnbuf", "eye_b"], writes=["pS2"], inc=(c == 3))
                o("tensor", lambda t, c=c: t.matmul(
                    pS[:, 1536 + c * 128:1536 + (c + 1) * 128],
                    xnbuf[:, (2 * c + 1) * 128:(2 * c + 2) * 128], eye_b[:],
                    start=True, stop=True),
                  reads=["xnbuf", "eye_b"], writes=["pS3"], inc=(c == 3))
            for c in range(4):
                o("vector", lambda v, c=c, tt=tt: v.tensor_copy(
                    xnT[2 * c][:, tt * 128:(tt + 1) * 128],
                    pS[:, 1024 + c * 128:1024 + (c + 1) * 128]),
                  reads=["pS2"], writes=[f"xnT{2 * c}:{tt}"], inc=(c == 3))
                o("scalar", lambda sc, c=c, tt=tt: sc.copy(
                    xnT[2 * c + 1][:, tt * 128:(tt + 1) * 128],
                    pS[:, 1536 + c * 128:1536 + (c + 1) * 128]),
                  reads=["pS3"], writes=[f"xnT{2 * c + 1}:{tt}"], inc=(c == 3))
        xnall = [f"xnT{i}:{j}" for i in range(8) for j in range(NT)]

        for qki, wdr in ((0, whqd), (1, whkd)):
            for p in range(8):
                for par in range(2):
                    h_ = 2 * p + par
                    d("sync", wtmp[par * 64:(par + 1) * 64, :],
                      wdr[h_ * DH:(h_ + 1) * DH, :], writes=[f"wtmp{par}"], group="whq")
                o("vector", lambda v, p=p, qki=qki: v.tensor_scalar(
                    whq8[:, (qki * 8 + p) * NBC:(qki * 8 + p + 1) * NBC],
                    wtmp[:, :], 8.0 if qki == 0 else 1.0, None, ALU.mult),
                  reads=["wtmp0", "wtmp1"], writes=[f"whq8:{qki}:{p}"])

        for half in range(2):
            arB.push(); arF.push()
            qsb = [arB.alloc(S) for _ in range(4)]
            ksb = [arB.alloc(S) for _ in range(4)]
            qbb = [arB.alloc(S) for _ in range(4)]
            kbb = [arB.alloc(S) for _ in range(4)]
            vsb = [arB.alloc(NKC * 65 + 1) for _ in range(8)]
            qbn = arB.alloc(128)
            eqf = arF.alloc(128)
            pair0 = half * 4

            for hh in range(8):
                o("vector", lambda v, hh=hh: v.memset(
                    vsb[hh][:, 0:NKC * 65].rearrange("p (a b) -> p a b", b=65)[:, :, 64:65], 1.0),
                  writes=[f"vones{hh}"])

            arB.push()
            wpr = [arB.alloc(8 * 128) for _ in range(2)]
            wvh = arB.alloc(8 * 512)
            for which, wdr, dstl, scl in (("q", wqd, qsb, 0.125), ("k", wkd, ksb, 1.0)):
                wb = wpr[0] if which == "q" else wpr[1]
                for pi in range(4):
                    p = pair0 + pi
                    kwb = f"wpr{which}"
                    bdr = bwq if which == "q" else bwk
                    d("sync", wb[:].rearrange("p (a m) -> p a m", a=8),
                      bdr[:, p * 128:(p + 1) * 128].rearrange(
                          "(a p) m -> p a m", p=128),
                      writes=[f"{kwb}:{dc2}" for dc2 in range(8)], group=f"w{which}")
                    for nb_ in range(S // W512):
                        pb, kpb = (p4, "p4") if nb_ % 2 == 0 else (p5, "p5")
                        for dc in range(8):
                            o("tensor", lambda t, dc=dc, nb_=nb_, wb=wb, pb=pb: t.matmul(
                                pb[:, 0:W512], wb[:, dc * 128:(dc + 1) * 128],
                                xnT[dc][:, nb_ * W512:(nb_ + 1) * W512],
                                start=(dc == 0), stop=(dc == 7)),
                              reads=[f"{kwb}:{dc}"] + (xnall if dc == 0 else []),
                              writes=[kpb], inc=(dc == 7))
                        o("vector", lambda v, pb=pb, nb_=nb_, pi=pi, dstl=dstl, scl=scl: v.tensor_scalar(
                            dstl[pi][:, nb_ * W512:(nb_ + 1) * W512], pb[:, 0:W512],
                            scl, None, ALU.mult),
                          reads=[kpb], writes=[f"{which}sb{pi}:{nb_}"])
            d("sync", wvh[:].rearrange("p (a m) -> p a m", a=8),
              bwv[:, half * 512:(half + 1) * 512].rearrange(
                  "(a p) m -> p a m", p=128),
              writes=[f"wvh:{dc2}" for dc2 in range(8)], group="wv")
            for tt in range(NT):
                pb, kpb = (p6, "p6") if tt % 2 == 0 else (p7, "p7")
                for dc in range(8):
                    o("tensor", lambda t, dc=dc, tt=tt, pb=pb: t.matmul(
                        pb[:], xnT[dc][:, tt * 128:(tt + 1) * 128],
                        wvh[:, dc * 512:(dc + 1) * 512],
                        start=(dc == 0), stop=(dc == 7)),
                      reads=[f"wvh:{dc}"] + (xnall if dc == 0 else []),
                      writes=[kpb], inc=(dc == 7))
                for hh in range(8):
                    o("vector", lambda v, hh=hh, tt=tt, pb=pb: v.tensor_copy(
                        vsb[hh][:, tt * 65:tt * 65 + 64], pb[:, hh * 64:(hh + 1) * 64]),
                      reads=[kpb], writes=[f"v{hh}:{tt}"], inc=(hh == 7))
            arB.pop()
            ptb = [arB.alloc(4 * QB) for _ in range(2)]

            for qki, srcl, dstl, scl in ((0, qsb, qbb, 0.1), (1, ksb, kbb, 1.0)):
                sk = "q" if qki == 0 else "k"
                dk = "qb" if qki == 0 else "kb"
                for gg in range(2):
                    for tt in range(NT):
                        for j in range(4):
                            pi = 2 * gg + j // 2
                            par = j % 2
                            pb, kpb = (p6, "p6") if par == 0 else (p7, "p7")
                            o("tensor", lambda t, pi=pi, par=par, j=j, tt=tt, qki=qki, pb=pb, srcl=srcl: t.matmul(
                                pb[0:128, (j // 2) * 32:(j // 2) * 32 + 32],
                                srcl[pi][par * 64:(par + 1) * 64, tt * 128:(tt + 1) * 128],
                                whq8[par * 64:(par + 1) * 64,
                                     (qki * 8 + pair0 + pi) * NBC:(qki * 8 + pair0 + pi + 1) * NBC],
                                start=(j // 2 == 0), stop=(j // 2 == 1),
                                tile_position=(par * 64, 0)),
                              reads=[f"{sk}sb{pi}:{tt * 128 // W512}",
                                     f"whq8:{qki}:{pair0 + pi}"],
                              writes=[kpb], inc=(j // 2 == 1))
                        o("scalar", lambda sc: sc.activation(eqf[:, 0:64], p6[:, 0:64], AF.Exp),
                          reads=["p6"], writes=["eq0"])
                        o("scalar", lambda sc: sc.activation(eqf[:, 64:128], p7[:, 0:64], AF.Exp),
                          reads=["p7"], writes=["eq1"])
                        o("vector", lambda v: v.tensor_reduce(
                            stt[:, 0:4], eqf[:, 0:128].rearrange("p (a b) -> p a b", b=32),
                            AX.X, ALU.add),
                          reads=["eq0", "eq1"], writes=["eqs"])
                        o("vector", lambda v: v.reciprocal(stt[:, 4:8], stt[:, 0:4]),
                          reads=["eqs"], writes=["eqr"])
                        for ci, j in ((0, 0), (1, 2), (2, 1), (3, 3)):
                            o("vector", lambda v, ci=ci, scl=scl: v.tensor_scalar(
                                qbn[:, ci * 32:(ci + 1) * 32], eqf[:, ci * 32:(ci + 1) * 32],
                                stt[:, 4 + ci:5 + ci], scl, ALU.mult, ALU.mult),
                              reads=["eqr", "eq0", "eq1"], writes=["qbn"], inc=(ci == 3))
                        first = True
                        for ci, j in ((0, 0), (1, 2), (2, 1), (3, 3)):
                            par = j % 2
                            o("tensor", lambda t, ci=ci, par=par, first=first: t.matmul(
                                pS[par * 64:par * 64 + 32,
                                   512 + ci * 128:512 + (ci + 1) * 128],
                                qbn[:, ci * 32:(ci + 1) * 32], eye_b[:],
                                start=(ci in (0, 2)), stop=(ci in (1, 3)),
                                tile_position=(0, par * 64)),
                              reads=["qbn", "eye_b"], writes=["pS1"], inc=(ci == 3))
                            first = False
                        for ci, j in ((0, 0), (1, 2), (2, 1), (3, 3)):
                            par = j % 2
                            pi = 2 * gg + j // 2
                            o("vector", lambda v, ci=ci, par=par, pi=pi, tt=tt, dstl=dstl: v.tensor_copy(
                                dstl[pi][par * 64:par * 64 + 32, tt * 128:(tt + 1) * 128],
                                pS[par * 64:par * 64 + 32, 512 + ci * 128:512 + (ci + 1) * 128]),
                              reads=["pS1"], writes=[f"{dk}{pi}:{tt}"], inc=(ci == 3))

            qkeys = lambda pi, nb_: [f"qsb{pi}:{nb_}", f"qbb{pi}"]
            obank = [p4, p5, p6, p7]
            okeys = ["p4", "p5", "p6", "p7"]
            for gg in range(2):
                pA, pB_ = 2 * gg, 2 * gg + 1
                for qb_ in range(NQB):
                    q0 = qb_ * QB
                    nb_ = q0 // W512
                    for kc in range(NKC):
                        for jj, pi in ((0, pA), (1, pA), (2, pB_), (3, pB_)):
                            par = jj % 2
                            kqr = [f"qsb{pi}:{nb_}", f"ksb{pi}:{kc * 128 // W512}"]
                            o("tensor", lambda t, jj=jj, pi=pi, par=par, kc=kc, q0=q0: t.matmul(
                                pS[:, jj * 512:jj * 512 + QB],
                                ksb[pi][par * 64:(par + 1) * 64, kc * 128:(kc + 1) * 128],
                                qsb[pi][par * 64:(par + 1) * 64, q0:q0 + QB],
                                start=True, stop=False, tile_position=(par * 64, 0)),
                              reads=kqr, writes=[f"pS{jj}"], inc=False)
                            o("tensor", lambda t, jj=jj, pi=pi, par=par, kc=kc, q0=q0: t.matmul(
                                pS[:, jj * 512:jj * 512 + QB],
                                kbb[pi][par * 64:par * 64 + 32, kc * 128:(kc + 1) * 128],
                                qbb[pi][par * 64:par * 64 + 32, q0:q0 + QB],
                                start=False, stop=True, tile_position=(par * 64, 0)),
                              reads=[f"qb{pi}:{q0 // 128 + z}" for z in range(QB // 128)] + [f"kb{pi}:{kc}"],
                              writes=[f"pS{jj}"], inc=True)
                        kpt = f"ptb{kc % 2}"
                        o("scalar", lambda sc, kc=kc: sc.activation(
                            ptb[kc % 2][:].rearrange("p (a b) -> p a b", b=QB),
                            pS[:].rearrange("p (a b) -> p a b", b=512)[:, :, 0:QB],
                            AF.Exp, bias=nb20[:, 0:1]),
                          reads=["pS0", "pS1", "pS2", "pS3"], writes=[kpt])
                        for jj, pi in ((0, pA), (1, pA), (2, pB_), (3, pB_)):
                            par = jj % 2
                            hh = 2 * pi + par
                            ob, ok = obank[jj], okeys[jj]
                            if par == 0:
                                o("tensor", lambda t, jj=jj, hh=hh, kc=kc, ob=ob: t.matmul(
                                    ob[0:65, 0:QB], vsb[hh][:, kc * 65:kc * 65 + 65],
                                    ptb[kc % 2][:, jj * QB:(jj + 1) * QB],
                                    start=(kc == 0), stop=(kc == NKC - 1)),
                                  reads=[kpt, f"v{hh}:{kc}", f"vones{hh}"],
                                  writes=[ok], inc=True)
                            else:
                                o("tensor", lambda t, jj=jj, hh=hh, kc=kc, ob=ob: t.matmul(
                                    ob[64:128, 0:QB], vsb[hh][:, kc * 65:kc * 65 + 64],
                                    ptb[kc % 2][:, jj * QB:(jj + 1) * QB],
                                    start=(kc == 0), stop=(kc == NKC - 1),
                                    tile_position=(0, 64)),
                                  reads=[kpt, f"v{hh}:{kc}"],
                                  writes=[ok], inc=False)
                                o("tensor", lambda t, jj=jj, hh=hh, kc=kc, ob=ob: t.matmul(
                                    ob[32:33, 0:QB], vsb[hh][:, kc * 65 + 64:kc * 65 + 65],
                                    ptb[kc % 2][:, jj * QB:(jj + 1) * QB],
                                    start=(kc == 0), stop=(kc == NKC - 1),
                                    tile_position=(0, 32)),
                                  reads=[kpt, f"vones{hh}"], writes=[ok], inc=True)
                    for jj, pi in ((0, pA), (1, pA), (2, pB_), (3, pB_)):
                        par = jj % 2
                        ob, ok = obank[jj], okeys[jj]
                        pg = pair0 + pi
                        if par == 0:
                            o("vector", lambda v, ob=ob: v.reciprocal(
                                lrec[64:65, 0:QB], ob[64:65, 0:QB]),
                              reads=[ok], writes=["lrec"])
                            o("tensor", lambda t: t.matmul(
                                pS[0:64, 0:QB], ones_f[64:65, 0:64], lrec[64:65, 0:QB],
                                start=True, stop=True, tile_position=(64, 0)),
                              reads=["lrec", "ones_f"], writes=["pS0"])
                            o("scalar", lambda sc: sc.copy(bcst[0:64, 0:QB], pS[0:64, 0:QB]),
                              reads=["pS0"], writes=["bcst0"])
                            o("vector", lambda v, ob=ob, pg=pg, q0=q0: v.tensor_tensor(
                                oall[pg][0:64, q0:q0 + QB], ob[0:64, 0:QB],
                                bcst[0:64, 0:QB], ALU.mult),
                              reads=[ok, "bcst0"], writes=[f"oall{pg}:{q0}"])
                        else:
                            o("vector", lambda v, ob=ob: v.reciprocal(
                                lrec[32:33, 0:QB], ob[32:33, 0:QB]),
                              reads=[ok], writes=["lrec2"])
                            o("tensor", lambda t: t.matmul(
                                pS[64:128, 0:QB], ones_f[32:33, 0:64], lrec[32:33, 0:QB],
                                start=True, stop=True, tile_position=(32, 64)),
                              reads=["lrec2", "ones_f"], writes=["pS0"])
                            o("scalar", lambda sc: sc.copy(bcst[64:128, 0:QB], pS[64:128, 0:QB]),
                              reads=["pS0"], writes=["bcst1"])
                            o("vector", lambda v, ob=ob, pg=pg, q0=q0: v.tensor_tensor(
                                oall[pg][64:128, q0:q0 + QB], ob[64:128, 0:QB],
                                bcst[64:128, 0:QB], ALU.mult),
                              reads=[ok, "bcst1"], writes=[f"oall{pg}:{q0}"])
            S_.barrier()
            arB.pop(); arF.pop()
        arB.pop()

        arB.push(); arF.push()
        wob = arB.alloc(8 * D)
        w1b = [arB.alloc(8 * 128) for _ in range(4)]
        w2b = [arB.alloc(D) for _ in range(4)]
        h1 = arB.alloc(32 * MLPB)
        xmT = [arB.alloc(MLPB) for _ in range(8)]
        xmb = arB.alloc(D)
        x3b = arB.alloc(2 * (MLPB // 128) * D).bitcast(F32)
        x1t = arF.alloc(D)
        for pg in range(8):
            d("sync", wob[:, pg * D:(pg + 1) * D], bwo[pg, :, :],
              writes=[f"wob{pg}"], group="wo")
        oallk = [f"oall{pg}:{q0}" for pg in range(8) for q0 in range(0, S, QB)]

        for mb in range(NMB):
            tiles = [mb * (MLPB // 128) + c for c in range(MLPB // 128)]
            for ci, tt in enumerate(tiles):
                for dh in range(2):
                    pb, kpb = (p4, "p4") if dh == 0 else (p5, "p5")
                    for pg in range(8):
                        o("tensor", lambda t, pg=pg, dh=dh, tt=tt, pb=pb: t.matmul(
                            pb[:], oall[pg][:, tt * 128:(tt + 1) * 128],
                            wob[:, pg * D + dh * 512:pg * D + (dh + 1) * 512],
                            start=(pg == 0), stop=(pg == 7)),
                          reads=[f"wob{pg}"] + (oallk if pg == 0 else []),
                          writes=[kpb], inc=(pg == 7))
                kx, xu = f"ut{tt % 2}", ut[tt % 2]
                d("sync", xu[:], su[seq, tt * 128:(tt + 1) * 128, :],
                  writes=[kx], group=f"bx{tt % 2}")
                for dh in range(2):
                    pb, kpb = (p4, "p4") if dh == 0 else (p5, "p5")
                    o("vector", lambda v, pb=pb, dh=dh, ci=ci, xu=xu: v.tensor_tensor(
                        x3b[:, ci * D + dh * 512:ci * D + (dh + 1) * 512],
                        pb[:], xu[:, dh * 512:(dh + 1) * 512], ALU.add),
                      reads=[kpb, kx], writes=[f"x3:{ci}"])
                kx3 = f"x3:{ci}"
                o("vector", lambda v, ci=ci: v.bn_stats(
                    stt[:, 0:6], x3b[:, ci * D:ci * D + 512]),
                  reads=[kx3], writes=["st0"])
                o("vector", lambda v, ci=ci: v.bn_stats(
                    stt[:, 6:12], x3b[:, ci * D + 512:ci * D + 1024]),
                  reads=[kx3], writes=["st1"])
                o("vector", lambda v: v.bn_aggr(stt[:, 12:14], stt[:, 0:12]),
                  reads=["st0", "st1"], writes=["mu"])
                o("vector", lambda v: v.tensor_scalar(stt[:, 14:15], stt[:, 13:14],
                                                      1e-6, None, ALU.add),
                  reads=["mu"], writes=["vpe"])
                o("scalar", lambda sc: sc.activation(stt[:, 15:16], stt[:, 14:15], AF.Sqrt),
                  reads=["vpe"], writes=["sig"])
                o("vector", lambda v: v.reciprocal(stt[:, 14:15], stt[:, 15:16]),
                  reads=["sig"], writes=["rstd"])
                o("vector", lambda v, ci=ci: v.scalar_tensor_tensor(
                    xmb[:], x3b[:, ci * D:(ci + 1) * D], stt[:, 12:13], g2r[:],
                    ALU.subtract, ALU.mult),
                  reads=[kx3, "mu", "g2r"], writes=["xmb"])
                o("vector", lambda v: v.scalar_tensor_tensor(
                    xmb[:], xmb[:], stt[:, 14:15], b2lr[:], ALU.mult, ALU.add),
                  reads=["xmb", "rstd", "b2lr"], writes=["xmb"])
                for c in range(4):
                    o("tensor", lambda t, c=c: t.matmul(
                        pS[:, 1024 + c * 128:1024 + (c + 1) * 128],
                        xmb[:, 2 * c * 128:(2 * c + 1) * 128], eye_b[:],
                        start=True, stop=True),
                      reads=["xmb", "eye_b"], writes=["pS2"], inc=(c == 3))
                    o("tensor", lambda t, c=c: t.matmul(
                        pS[:, 1536 + c * 128:1536 + (c + 1) * 128],
                        xmb[:, (2 * c + 1) * 128:(2 * c + 2) * 128], eye_b[:],
                        start=True, stop=True),
                      reads=["xmb", "eye_b"], writes=["pS3"], inc=(c == 3))
                for c in range(4):
                    o("vector", lambda v, c=c, ci=ci: v.tensor_copy(
                        xmT[2 * c][:, ci * 128:(ci + 1) * 128],
                        pS[:, 1024 + c * 128:1024 + (c + 1) * 128]),
                      reads=["pS2"], writes=[f"xmT{2 * c}:{ci}"], inc=(c == 3))
                    o("scalar", lambda sc, c=c, ci=ci: sc.copy(
                        xmT[2 * c + 1][:, ci * 128:(ci + 1) * 128],
                        pS[:, 1536 + c * 128:1536 + (c + 1) * 128]),
                      reads=["pS3"], writes=[f"xmT{2 * c + 1}:{ci}"], inc=(c == 3))
            xmk = [f"xmT{i}:{j}" for i in range(8) for j in range(MLPB // 128)]
            for mc in range(32):
                wb, kwb = w1b[mc % 4], f"w1b{mc % 4}"
                d("sync", wb[:].rearrange("p (a m) -> p a m", a=8),
                  bw1[:, mc * 128:(mc + 1) * 128].rearrange(
                      "(a p) m -> p a m", p=128),
                  writes=[f"{kwb}:{dc2}" for dc2 in range(8)], group=f"wm1{mc % 4}")
                pb, kpb = (p6, "p6") if mc % 2 == 0 else (p7, "p7")
                for dc in range(8):
                    o("tensor", lambda t, dc=dc, mc=mc, wb=wb, pb=pb: t.matmul(
                        pb[:, 0:MLPB], wb[:, dc * 128:(dc + 1) * 128],
                        xmT[dc][:, 0:MLPB], start=(dc == 0), stop=(dc == 7)),
                      reads=[f"{kwb}:{dc}"] + (xmk if dc == 0 else []),
                      writes=[kpb], inc=(dc == 7))
                o("vector", lambda v, mc=mc, pb=pb: v.tensor_scalar(
                    h1[:, mc * MLPB:(mc + 1) * MLPB], pb[:, 0:MLPB],
                    b1cc[:, mc:mc + 1], 0.0, ALU.add, ALU.max),
                  reads=[kpb, "b1cc"], writes=[f"h1:{mc}"])
            ypb = [pS[:, 0:512], pS[:, 512:1024], pS[:, 1024:1536], pS[:, 1536:2048]]
            for mc in range(32):
                wb, kwb = w2b[mc % 4], f"w2b{mc % 4}"
                d("sync", wb[:], bw2[mc * 128:(mc + 1) * 128, :],
                  writes=[kwb], group=f"wm2{mc % 4}")
                for ci in range(MLPB // 128):
                    for dh in range(2):
                        bi = ci * 2 + dh
                        o("tensor", lambda t, mc=mc, ci=ci, dh=dh, wb=wb, bi=bi: t.matmul(
                            ypb[bi], h1[:, mc * MLPB + ci * 128:mc * MLPB + (ci + 1) * 128],
                            wb[:, dh * 512:(dh + 1) * 512],
                            start=(mc == 0), stop=(mc == 31)),
                          reads=[f"h1:{mc}", kwb],
                          writes=[f"pS{bi}"],
                          inc=(ci == MLPB // 128 - 1 and dh == 1))
            for ci, tt in enumerate(tiles):
                d("sync", x1t[:], sx1[seq, tt * 128:(tt + 1) * 128, :],
                  writes=["x1t"], group="bx1r")
                for dh in range(2):
                    bi = ci * 2 + dh
                    o("vector", lambda v, ci=ci, dh=dh, bi=bi: v.tensor_tensor(
                        x3b[:, ci * D + dh * 512:ci * D + (dh + 1) * 512],
                        x3b[:, ci * D + dh * 512:ci * D + (dh + 1) * 512],
                        ypb[bi], ALU.add),
                      reads=[f"pS{bi}", f"x3:{ci}"], writes=[f"x3:{ci}"])
                    o("vector", lambda v, ci=ci, dh=dh: v.tensor_tensor(
                        x3b[:, ci * D + dh * 512:ci * D + (dh + 1) * 512],
                        x3b[:, ci * D + dh * 512:ci * D + (dh + 1) * 512],
                        b2rr[:, dh * 512:(dh + 1) * 512], ALU.add),
                      reads=[f"x3:{ci}", "b2rr"], writes=[f"x3:{ci}"])
                    o("vector", lambda v, ci=ci, dh=dh: v.tensor_tensor(
                        x3b[:, ci * D + dh * 512:ci * D + (dh + 1) * 512],
                        x3b[:, ci * D + dh * 512:ci * D + (dh + 1) * 512],
                        x1t[:, dh * 512:(dh + 1) * 512], ALU.add),
                      reads=[f"x3:{ci}", "x1t"], writes=[f"x3:{ci}"])
                d("sync", sx4[seq, tt * 128:(tt + 1) * 128, :],
                  x3b[:, ci * D:(ci + 1) * D], reads=[f"x3:{ci}"], group="x4out")
                if debug:
                    d("sync", dbg["d_x4"][seq, tt * 128:(tt + 1) * 128, :],
                      x3b[:, ci * D:(ci + 1) * D], reads=[f"x3:{ci}"], group="dbg4")
        arB.pop(); arF.pop()
        arB.pop(); arF.pop()

    for seq in range(NSEQ):
        stage_ffn(seq, xin, sx1)
        if debug:
            S_.barrier()
            arF.push()
            tmp = arF.alloc(D)
            for tt in range(NT):
                dt_ = d("sync", tmp[:], sx1[seq, tt * 128:(tt + 1) * 128, :],
                        writes=["dtmp"], group="dbg1")
                d("sync", dbg["d_x1"][seq, tt * 128:(tt + 1) * 128, :], tmp[:],
                  reads=["dtmp"], group="dbg1b")
            arF.pop()
        stage_attn(seq)
        stage_ffn(seq, sx4, out)
    S_.emit("sync")
    return nc, st, dbg


def _host_inputs(inputs, S=2048):
    f32 = np.float32
    i = inputs
    shared = {
        "w1t": np.ascontiguousarray(i["L0_ffn_w1"].T.astype(f32)),
        "w2t": np.ascontiguousarray(i["L0_ffn_w2"].T.astype(f32)),
        "convw": np.ascontiguousarray(
            i["L0_ffn_conv_w"].reshape(32, 128, 3).transpose(1, 0, 2).reshape(128, 96).astype(f32)),
        "convb": np.ascontiguousarray(
            i["L0_ffn_conv_b"].reshape(32, 128).T.astype(f32)),
        "g1d": i["L0_attn_ln1_g"].reshape(1, D).astype(f32),
        "b1d": i["L0_attn_ln1_b"].reshape(1, D).astype(f32),
        "g2d": i["L0_attn_ln2_g"].reshape(1, D).astype(f32),
        "b2d": i["L0_attn_ln2_b"].reshape(1, D).astype(f32),
        "wqd": np.ascontiguousarray(i["L0_attn_Wq"].reshape(D, H * DH).astype(f32)),
        "wkd": np.ascontiguousarray(i["L0_attn_Wk"].reshape(D, H * DH).astype(f32)),
        "wvd": np.ascontiguousarray(i["L0_attn_Wv"].reshape(D, H * DH).astype(f32)),
        "whqd": np.ascontiguousarray(i["L0_attn_Whq"].reshape(H * DH, NBC).astype(f32)),
        "whkd": np.ascontiguousarray(i["L0_attn_Whk"].reshape(H * DH, NBC).astype(f32)),
        "wod": np.ascontiguousarray(
            i["L0_attn_Wo"].reshape(H // 2, 2 * DH, D).astype(f32)),
        "wm1d": np.ascontiguousarray(i["L0_attn_W1"].astype(f32)),
        "wm2d": np.ascontiguousarray(i["L0_attn_W2"].astype(f32)),
        "b1cd": np.ascontiguousarray(i["L0_attn_b1"].reshape(32, 128).T.astype(f32)),
        "b2rd": i["L0_attn_b2"].reshape(1, D).astype(f32),
        "eyebd": np.eye(128, dtype=f32).astype(bfdt),
        "eyefd": np.eye(128, dtype=f32),
        "onesfd": np.ones((128, 128), dtype=f32),
    }
    x = np.ascontiguousarray(i["L0_attn_inputs"].astype(f32))
    in_maps = []
    for c in range(NCORES):
        m = dict(shared)
        m["xin"] = np.ascontiguousarray(x[c * NSEQ:(c + 1) * NSEQ, :S, :])
        in_maps.append(m)
    return in_maps


_CACHE = {}


def kernel(**inputs):
    if "prog" not in _CACHE:
        nc, st, _ = build_program(S=2048, debug=False)
        _CACHE["prog"] = (nc, st)
    nc, st = _CACHE["prog"]
    in_maps = _host_inputs(inputs, S=2048)
    res = run_bass_kernel_spmd(nc, in_maps, core_ids=list(range(NCORES)))
    outp = np.concatenate([res.results[c]["out"] for c in range(NCORES)], axis=0)
    return outp.astype(np.float32)


# version 06
# speedup vs baseline: 1.7137x; 1.0204x over previous
import numpy as np
import ml_dtypes
from contextlib import ExitStack
from typing import NamedTuple

import concourse.bass as bass
import concourse.mybir as mybir
from concourse.bass_utils import run_bass_kernel_spmd

F32 = mybir.dt.float32
BF16 = mybir.dt.bfloat16
ALU = mybir.AluOpType
AF = mybir.ActivationFunctionType
AX = mybir.AxisListType
bfdt = ml_dtypes.bfloat16

MAGIC = 12582912.0
EXPB = -20.0
D, H, DH, NBC, FF = 1024, 16, 64, 32, 4096
NSEQ, NCORES = 2, 8
ENGINES = ("sync", "tensor", "vector", "scalar", "gpsimd")


class Tok(NamedTuple):
    key: str
    val: int


class _Rec:

    def __getattr__(self, name):
        def call(*a, **k):
            return (name, a, k)
        return call


class Sched:

    def __init__(self, nc, stack):
        self.nc = nc
        self.stack = stack
        self.sem = {e: stack.enter_context(nc.semaphore("S_" + e)) for e in ENGINES}
        self.cnt = {e: 0 for e in ENGINES}
        self.opl = {e: [] for e in ENGINES}
        self.water = {}
        self.dma_groups = {}
        self.pend_bar = {}
        self.wmap = {}
        self.rmap = {}
        self.pend_rw = {e: [] for e in ENGINES}

    def group(self, name):
        if name not in self.dma_groups:
            self.dma_groups[name] = self.stack.enter_context(
                self.nc.semaphore("G_" + name))
            self.cnt[name] = 0
        return name

    def barrier(self):
        toks = [Tok(k, v) for k, v in self.cnt.items() if v > 0]
        for e in ENGINES:
            self.pend_bar[e] = toks
        self.wmap = {}
        self.rmap = {}

    def _rw_deps(self, reads, writes):
        deps = []
        for k in reads:
            deps += self.wmap.get(k, [])
        for k in writes:
            deps += self.wmap.get(k, [])
            deps += self.rmap.get(k, [])
        return deps

    def _rw_commit(self, tok, reads, writes):
        for k in writes:
            self.wmap[k] = [tok]
            self.rmap[k] = []
        for k in reads:
            self.rmap.setdefault(k, []).append(tok)

    def _resolve(self, eng, deps):
        byk = {}
        for t in deps:
            if t is None:
                continue
            v = t.val
            if t.key in self.dma_groups:
                v = self.cnt[t.key]
            if byk.get(t.key, 0) < v:
                byk[t.key] = v
        waits = []
        for k, v in sorted(byk.items()):
            wk = (eng, k)
            if self.water.get(wk, 0) >= v:
                continue
            self.water[wk] = v
            waits.append((k, v))
        return waits

    def op(self, eng, fn, reads=(), writes=(), deps=(), inc=True):
        deps = list(deps) + self._rw_deps(reads, writes)
        if self.pend_bar.get(eng):
            deps += self.pend_bar[eng]
            self.pend_bar[eng] = None
        waits = self._resolve(eng, deps)
        tok, inckey = None, None
        if inc:
            self.cnt[eng] += 1
            tok = Tok(eng, self.cnt[eng])
            inckey = eng
        self.opl[eng].append((fn(_Rec()), waits, inckey, 1))
        if tok is not None:
            for (r2, w2) in self.pend_rw[eng]:
                self._rw_commit(tok, r2, w2)
            self.pend_rw[eng] = []
            self._rw_commit(tok, reads, writes)
        else:
            self.pend_rw[eng].append((list(reads), list(writes)))
        return tok

    def dma(self, eng, out, in_, reads=(), writes=(), deps=(), group="misc", **kw):
        g = self.group(group)
        deps = list(deps) + self._rw_deps(reads, writes)
        if self.pend_bar.get(eng):
            deps += self.pend_bar[eng]
            self.pend_bar[eng] = None
        waits = self._resolve(eng, deps)
        self.cnt[g] += 16
        tok = Tok(g, self.cnt[g])
        self.opl[eng].append((("dma_start", (out, in_), kw), waits, g, 16))
        self._rw_commit(tok, reads, writes)
        return tok

    def emit(self, final_eng="sync"):
        finals = [Tok(g, self.cnt[g]) for g in self.dma_groups if self.cnt[g] > 0]
        self.opl[final_eng].append((None, self._resolve(final_eng, finals), None, 0))
        with self.nc.Block() as block:
            for e in ENGINES:
                if not self.opl[e]:
                    continue
                def body(E, _ops=tuple(self.opl[e])):
                    for spec, waits, inckey, amt in _ops:
                        for (k, v) in waits:
                            E.wait_ge(self._sem_of(k), v)
                        if spec is not None:
                            name, a, kw = spec
                            inst = getattr(E, name)(*a, **kw)
                            if inckey is not None:
                                inst.then_inc(self._sem_of(inckey), amt)
                getattr(block, e)(body)

    def _sem_of(self, key):
        return self.sem[key] if key in self.sem else self.dma_groups[key]


class Arena:
    def __init__(self, tensor, cols):
        self.t, self.cols, self.off, self.marks = tensor, cols, 0, []

    def alloc(self, n):
        n = (n + 1) & ~1
        assert self.off + n <= self.cols, f"arena overflow {self.off}+{n}>{self.cols}"
        ap = self.t[:, self.off:self.off + n]
        self.off += n
        return ap

    def push(self):
        self.marks.append(self.off)

    def pop(self):
        self.off = self.marks.pop()


def build_program(S=2048, debug=False):
    TB = 128
    MLPB = min(512, S)
    NT = S // 128
    NKC = S // 128
    QB = min(512, S)
    NQB = S // QB
    NTB = S // TB
    NMB = S // MLPB
    W512 = min(512, S)

    nc = bass.Bass("TRN2", target_bir_lowering=False)
    st = ExitStack()
    S_ = Sched(nc, st)
    o, d = S_.op, S_.dma

    xin = nc.dram_tensor("xin", [NSEQ, S, D], F32, kind="ExternalInput")
    out = nc.dram_tensor("out", [NSEQ, S, D], F32, kind="ExternalOutput")
    w1t = nc.dram_tensor("w1t", [D, FF], F32, kind="ExternalInput")
    w2t = nc.dram_tensor("w2t", [FF, D], F32, kind="ExternalInput")
    convw = nc.dram_tensor("convw", [128, 3 * (FF // 128)], F32, kind="ExternalInput")
    convb = nc.dram_tensor("convb", [128, FF // 128], F32, kind="ExternalInput")
    g1d = nc.dram_tensor("g1d", [1, D], F32, kind="ExternalInput")
    b1d = nc.dram_tensor("b1d", [1, D], F32, kind="ExternalInput")
    g2d = nc.dram_tensor("g2d", [1, D], F32, kind="ExternalInput")
    b2d = nc.dram_tensor("b2d", [1, D], F32, kind="ExternalInput")
    wqd = nc.dram_tensor("wqd", [D, H * DH], F32, kind="ExternalInput")
    wkd = nc.dram_tensor("wkd", [D, H * DH], F32, kind="ExternalInput")
    wvd = nc.dram_tensor("wvd", [D, H * DH], F32, kind="ExternalInput")
    whqd = nc.dram_tensor("whqd", [H * DH, NBC], F32, kind="ExternalInput")
    whkd = nc.dram_tensor("whkd", [H * DH, NBC], F32, kind="ExternalInput")
    wod = nc.dram_tensor("wod", [H // 2, 2 * DH, D], F32, kind="ExternalInput")
    wm1d = nc.dram_tensor("wm1d", [D, FF], F32, kind="ExternalInput")
    wm2d = nc.dram_tensor("wm2d", [FF, D], F32, kind="ExternalInput")
    b1cd = nc.dram_tensor("b1cd", [128, FF // 128], F32, kind="ExternalInput")
    b2rd = nc.dram_tensor("b2rd", [1, D], F32, kind="ExternalInput")
    eyebd = nc.dram_tensor("eyebd", [128, 128], BF16, kind="ExternalInput")
    eyefd = nc.dram_tensor("eyefd", [128, 128], F32, kind="ExternalInput")
    onesfd = nc.dram_tensor("onesfd", [128, 128], F32, kind="ExternalInput")

    sx1 = nc.dram_tensor("sx1", [NSEQ, S, D], F32)
    su = nc.dram_tensor("su", [NSEQ, S, D], F32)
    sx4 = nc.dram_tensor("sx4", [NSEQ, S, D], F32)
    sq1 = nc.dram_tensor("sq1", [D, FF], BF16)
    sq2 = nc.dram_tensor("sq2", [FF, D], BF16)
    bwq = nc.dram_tensor("bwq", [D, H * DH], BF16)
    bwk = nc.dram_tensor("bwk", [D, H * DH], BF16)
    bwv = nc.dram_tensor("bwv", [D, H * DH], BF16)
    bwo = nc.dram_tensor("bwo", [H // 2, 2 * DH, D], BF16)
    bw1 = nc.dram_tensor("bw1", [D, FF], BF16)
    bw2 = nc.dram_tensor("bw2", [FF, D], BF16)
    dbg = {}
    if debug:
        for nm in ("d_x1", "d_u", "d_x4"):
            dbg[nm] = nc.dram_tensor(nm, [NSEQ, S, D], F32, kind="ExternalOutput")

    BCOLS = 80 * 1024
    FCOLS = 6592
    arB_t = st.enter_context(nc.sbuf_tensor("arB", [128, BCOLS], BF16))
    arF_t = st.enter_context(nc.sbuf_tensor("arF", [128, FCOLS], F32))
    eye_b = st.enter_context(nc.sbuf_tensor("eye_b", [128, 128], BF16))
    eye_f = st.enter_context(nc.sbuf_tensor("eye_f", [128, 128], F32))
    ones_f = st.enter_context(nc.sbuf_tensor("ones_f", [128, 128], F32))
    cvw = st.enter_context(nc.sbuf_tensor("cvw", [128, 3 * (FF // 128)], F32))
    cvb = st.enter_context(nc.sbuf_tensor("cvb", [128, FF // 128], F32))
    g1r = st.enter_context(nc.sbuf_tensor("g1r", [128, D], F32))
    b1r = st.enter_context(nc.sbuf_tensor("b1r", [128, D], F32))
    g2r = st.enter_context(nc.sbuf_tensor("g2r", [128, D], F32))
    b2lr = st.enter_context(nc.sbuf_tensor("b2lr", [128, D], F32))
    b2rr = st.enter_context(nc.sbuf_tensor("b2rr", [128, D], F32))
    b1cc = st.enter_context(nc.sbuf_tensor("b1cc", [128, FF // 128], F32))
    mwr = st.enter_context(nc.sbuf_tensor("mwr", [128, 8], F32))
    nb20 = st.enter_context(nc.sbuf_tensor("nb20", [128, 1], F32))
    pS = st.enter_context(nc.psum_tensor("pS", [128, 2048], F32))
    p4 = st.enter_context(nc.psum_tensor("p4", [128, 512], F32))
    p5 = st.enter_context(nc.psum_tensor("p5", [128, 512], F32))
    p6 = st.enter_context(nc.psum_tensor("p6", [128, 512], F32))
    p7 = st.enter_context(nc.psum_tensor("p7", [128, 512], F32))
    arB = Arena(arB_t, BCOLS)
    arF = Arena(arF_t, FCOLS)

    for t_, src_ in ((eye_b, eyebd), (eye_f, eyefd), (ones_f, onesfd), (cvw, convw),
                     (cvb, convb), (b1cc, b1cd)):
        d("sync", t_[:], src_[:], writes=[t_.name], group="cst")
    o("vector", lambda v: v.memset(nb20[:], EXPB), writes=["nb20"])
    for t_, src_ in ((g1r, g1d), (b1r, b1d), (g2r, g2d), (b2lr, b2d), (b2rr, b2rd)):
        d("sync", t_[:], src_[0:1, :].partition_broadcast(128), writes=[t_.name],
          group="cst")

    def quantize_weight(wdram, sqdram, nrow, ncol, mcol):
        nstrip = nrow // 128
        arF.push(); arB.push()
        wstrip = arF.alloc(ncol)
        qs = arB.alloc(ncol)
        o("vector", lambda v: v.memset(mwr[:, 7:8], 0.0), writes=["acc"])
        for i in range(nstrip):
            d("sync", wstrip[:], wdram[i * 128:(i + 1) * 128, :],
              writes=["wstrip"], group="wqld")
            o("vector", lambda v: v.tensor_reduce(
                mwr[:, 6:7], wstrip[:], AX.X, ALU.add, apply_absolute_value=True),
              reads=["wstrip"], writes=["red"])
            o("vector", lambda v: v.tensor_tensor(
                mwr[:, 7:8], mwr[:, 7:8], mwr[:, 6:7], ALU.add),
              reads=["red", "acc"], writes=["acc"])
        o("tensor", lambda t: t.matmul(p4[0:1, 0:1], ones_f[:, 0:1], mwr[:, 7:8],
                                       start=True, stop=True),
          reads=["acc", "ones_f"], writes=["p4"])
        o("vector", lambda v: v.tensor_scalar(
            mwr[0:1, 6:7], p4[0:1, 0:1], 1.0 / (nrow * ncol), 1e-5, ALU.mult, ALU.max),
          reads=["p4"], writes=["mean1"])
        o("tensor", lambda t: t.matmul(p4[:, 1:2], ones_f[0:1, :], mwr[0:1, 6:7],
                                       start=True, stop=True),
          reads=["mean1", "ones_f"], writes=["p4"])
        o("vector", lambda v: v.tensor_copy(mwr[:, mcol:mcol + 1], p4[:, 1:2]),
          reads=["p4"], writes=[f"mw{mcol}"])
        o("vector", lambda v: v.reciprocal(mwr[:, mcol + 1:mcol + 2],
                                           mwr[:, mcol:mcol + 1]),
          reads=[f"mw{mcol}"], writes=[f"mwi{mcol}"])
        for i in range(nstrip):
            d("sync", wstrip[:], wdram[i * 128:(i + 1) * 128, :],
              writes=["wstrip"], group="wqld")
            o("vector", lambda v: v.tensor_scalar(
                wstrip[:], wstrip[:], mwr[:, mcol + 1:mcol + 2], MAGIC,
                ALU.mult, ALU.add),
              reads=["wstrip", f"mwi{mcol}"], writes=["wstrip"])
            o("vector", lambda v: v.tensor_scalar(
                wstrip[:], wstrip[:], MAGIC, None, ALU.subtract),
              reads=["wstrip"], writes=["wstrip"])
            o("vector", lambda v: v.tensor_scalar(
                qs[:], wstrip[:], -1.0, 1.0, ALU.max, ALU.min),
              reads=["wstrip"], writes=["qs"])
            d("sync", sqdram[i * 128:(i + 1) * 128, :], qs[:],
              reads=["qs"], group="wqst")
        arF.pop(); arB.pop()

    for dst_, src_, nr in ((bwq, wqd, D), (bwk, wkd, D), (bwv, wvd, D),
                           (bw1, wm1d, D)):
        for c_ in range(2):
            d("gpsimd", dst_[c_ * nr // 2:(c_ + 1) * nr // 2, :],
              src_[c_ * nr // 2:(c_ + 1) * nr // 2, :], group="wcast")
    for c_ in range(8):
        d("gpsimd", bw2[c_ * FF // 8:(c_ + 1) * FF // 8, :],
          wm2d[c_ * FF // 8:(c_ + 1) * FF // 8, :], group="wcast")
    for pg_ in range(8):
        d("gpsimd", bwo[pg_, :, :], wod[pg_, :, :], group="wcast")
    quantize_weight(w1t, sq1, D, FF, 0)
    quantize_weight(w2t, sq2, FF, D, 2)
    o("vector", lambda v: v.tensor_scalar(mwr[:, 4:5], mwr[:, 0:1],
                                          1.0 / 127.0, None, ALU.mult),
      reads=["mw0"], writes=["mw1s"])
    o("vector", lambda v: v.tensor_scalar(mwr[:, 5:6], mwr[:, 2:3],
                                          0.5 / 127.0, None, ALU.mult),
      reads=["mw2"], writes=["mw2s"])

    def stage_ffn(seq, src, dst):
        S_.barrier()
        arB.push(); arF.push()
        w1q = [arB.alloc(FF) for _ in range(8)]
        w2q = [arB.alloc(D) for _ in range(32)]
        xqw = [[arB.alloc(TB + 2) for _ in range(8)] for _ in range(2)]
        xq2 = arB.alloc(32 * TB)
        q1b = arB.alloc(D)
        csr = arB.alloc(2 * S).bitcast(F32)
        f2r = arB.alloc(2 * TB).bitcast(F32)
        xt = [arF.alloc(D) for _ in range(2)]
        sl = arF.alloc(32 * TB)
        hb = [arB.alloc(2 * (TB + 2)).bitcast(F32) for _ in range(2)]
        yb = [arB.alloc(2 * TB).bitcast(F32) for _ in range(2)]
        ab = [arB.alloc(2 * TB).bitcast(F32) for _ in range(2)]
        bcc = arF.alloc(TB + 2)
        bc2 = arF.alloc(TB)
        m1b = arF.alloc(TB)
        stt = arF.alloc(16)

        o("vector", lambda v: v.memset(bcc[:], 0.0), writes=["bcc"])
        for i in range(8):
            d("sync", w1q[i][:], sq1[i * 128:(i + 1) * 128, :],
              writes=[f"w1q{i}"], group="wst")
        for i in range(32):
            d("sync", w2q[i][:], sq2[i * 128:(i + 1) * 128, :],
              writes=[f"w2q{i}"], group="wst")

        def quant_tile(tt):
            xa, kx = xt[tt % 2], f"xt{tt % 2}"
            d("sync", xa[:], src[seq, tt * 128:(tt + 1) * 128, :],
              writes=[kx], group=f"xin{tt % 2}")
            o("vector", lambda v: v.bn_stats(stt[:, 0:6], xa[:, 0:512]),
              reads=[kx], writes=["st0"])
            o("vector", lambda v: v.bn_stats(stt[:, 6:12], xa[:, 512:1024]),
              reads=[kx], writes=["st1"])
            o("vector", lambda v: v.bn_aggr(stt[:, 12:14], stt[:, 0:12]),
              reads=["st0", "st1"], writes=["mu"])
            o("vector", lambda v: v.tensor_scalar(stt[:, 14:15], stt[:, 13:14],
                                                  1e-5, None, ALU.add),
              reads=["mu"], writes=["vpe"])
            o("scalar", lambda sc: sc.activation(stt[:, 15:16], stt[:, 14:15], AF.Sqrt),
              reads=["vpe"], writes=["sig"])
            o("vector", lambda v: v.reciprocal(stt[:, 14:15], stt[:, 15:16]),
              reads=["sig"], writes=["rstd"])
            o("vector", lambda v: v.tensor_scalar(
                xa[:], xa[:], stt[:, 12:13], None, ALU.subtract),
              reads=[kx, "mu"], writes=[kx])
            o("vector", lambda v: v.tensor_reduce(
                stt[:, 0:1], xa[:], AX.X, ALU.max, apply_absolute_value=True),
              reads=[kx], writes=["am"])
            o("vector", lambda v: v.tensor_scalar(
                stt[:, 1:2], stt[:, 0:1], stt[:, 14:15], 1e-5, ALU.mult, ALU.max),
              reads=["am", "rstd"], writes=["u2"])
            o("vector", lambda v: v.reciprocal(stt[:, 2:3], stt[:, 1:2]),
              reads=["u2"], writes=["iu2"])
            o("vector", lambda v: v.tensor_scalar(
                stt[:, 3:4], stt[:, 2:3], 127.0, None, ALU.mult),
              reads=["iu2"], writes=["f0"])
            o("vector", lambda v: v.tensor_tensor(
                stt[:, 3:4], stt[:, 3:4], stt[:, 14:15], ALU.mult),
              reads=["f0", "rstd"], writes=["f"])
            o("vector", lambda v: v.tensor_tensor(
                stt[:, 4:5], stt[:, 1:2], mwr[:, 4:5], ALU.mult),
              reads=["u2", "mw1s"], writes=["cs"])
            o("vector", lambda v: v.tensor_scalar(
                xa[:], xa[:], stt[:, 3:4], MAGIC, ALU.mult, ALU.add),
              reads=[kx, "f"], writes=[kx])
            o("vector", lambda v: v.tensor_scalar(
                q1b[:], xa[:], MAGIC, None, ALU.subtract),
              reads=[kx], writes=["q1b"])
            o("tensor", lambda t: t.matmul(pS[0:1, 0:128], stt[:, 4:5], eye_f[:],
                                           start=True, stop=True),
              reads=["cs", "eye_f"], writes=["pS0"])
            o("vector", lambda v: v.tensor_copy(
                csr[0:1, tt * 128:(tt + 1) * 128], pS[0:1, 0:128]),
              reads=["pS0"], writes=[f"csr{tt}"])
            w = tt % 2
            for c in range(4):
                o("tensor", lambda t, c=c: t.matmul(
                    pS[:, 1024 + c * 128:1024 + (c + 1) * 128],
                    q1b[:, 2 * c * 128:(2 * c + 1) * 128], eye_b[:],
                    start=True, stop=True),
                  reads=["q1b", "eye_b"], writes=["pS2"], inc=(c == 3))
                o("tensor", lambda t, c=c: t.matmul(
                    pS[:, 1536 + c * 128:1536 + (c + 1) * 128],
                    q1b[:, (2 * c + 1) * 128:(2 * c + 2) * 128], eye_b[:],
                    start=True, stop=True),
                  reads=["q1b", "eye_b"], writes=["pS3"], inc=(c == 3))
            for c in range(4):
                o("vector", lambda v, c=c, w=w: v.tensor_copy(
                    xqw[w][2 * c][:, 1:129],
                    pS[:, 1024 + c * 128:1024 + (c + 1) * 128]),
                  reads=["pS2"], writes=[f"win{w}:{2 * c}"], inc=(c == 3))
                o("scalar", lambda sc, c=c, w=w: sc.copy(
                    xqw[w][2 * c + 1][:, 1:129],
                    pS[:, 1536 + c * 128:1536 + (c + 1) * 128]),
                  reads=["pS3"], writes=[f"win{w}:{2 * c + 1}"], inc=(c == 3))
            if tt == 0:
                for dc in range(8):
                    o("vector", lambda v, dc=dc: v.memset(xqw[0][dc][:, 0:1], 0.0),
                      reads=[], writes=[f"win0:{dc}"], inc=(dc == 7))
            else:
                pw = (tt - 1) % 2
                for dc in range(8):
                    o("vector", lambda v, dc=dc, w=w, pw=pw: v.tensor_copy(
                        xqw[w][dc][:, 0:1], xqw[pw][dc][:, 128:129]),
                      reads=[f"win{pw}:{dc}"], writes=[f"win{w}:{dc}"], inc=(dc == 7))
                for dc in range(8):
                    o("vector", lambda v, dc=dc, w=w, pw=pw: v.tensor_copy(
                        xqw[pw][dc][:, 129:130], xqw[w][dc][:, 1:2]),
                      reads=[f"win{w}:{dc}"], writes=[f"win{pw}:{dc}"], inc=(dc == 7))
            if tt == NT - 1:
                for dc in range(8):
                    o("vector", lambda v, dc=dc, w=w: v.memset(xqw[w][dc][:, 129:130], 0.0),
                      reads=[], writes=[f"win{w}:{dc}"], inc=(dc == 7))

        quant_tile(0)
        for tb in range(NTB):
            if tb + 1 < NT:
                quant_tile(tb + 1)
            w = tb % 2
            t0 = tb * TB
            lo, hi = max(0, t0 - 1), min(S, t0 + TB + 1)
            rt = [f"csr{i}" for i in range(max(0, tb - 1), min(NT, tb + 2))]
            o("tensor", lambda t: t.matmul(
                pS[:, 0:hi - lo], ones_f[0:1, :], csr[0:1, lo:hi],
                start=True, stop=True),
              reads=rt + ["ones_f"], writes=["pS0"])
            o("vector", lambda v: v.tensor_copy(
                bcc[:, lo - (t0 - 1):hi - (t0 - 1)], pS[:, 0:hi - lo]),
              reads=["pS0"], writes=["bcc"])
            if t0 == 0:
                o("vector", lambda v: v.memset(bcc[:, 0:1], 0.0), writes=["bcc"])
            if t0 + TB == S:
                o("vector", lambda v: v.memset(bcc[:, TB + 1:TB + 2], 0.0), writes=["bcc"])
            o("vector", lambda v: v.memset(m1b[:], 0.0), writes=["m1b"])

            winkeys = [f"win{w}:{dc}" for dc in range(8)]
            for mc in range(32):
                pb, kpb = (p4, "p4") if mc % 2 == 0 else (p5, "p5")
                for dc in range(8):
                    o("tensor", lambda t, mc=mc, dc=dc, pb=pb, w=w: t.matmul(
                        pb[:, 0:TB + 2], w1q[dc][:, mc * 128:(mc + 1) * 128],
                        xqw[w][dc][:, 0:TB + 2],
                        start=(dc == 0), stop=(dc == 7)),
                      reads=[f"w1q{dc}"] + (winkeys if dc == 0 else []),
                      writes=[kpb], inc=(dc == 7))
                hbx, khb = (hb[0], "hb0") if mc % 2 == 0 else (hb[1], "hb1")
                o("vector", lambda v, pb=pb, hbx=hbx: v.tensor_tensor(
                    hbx[:], pb[:, 0:TB + 2], bcc[:], ALU.mult),
                  reads=[kpb, "bcc"], writes=[khb])
                ybx, kyb = (yb[0], "yb0") if mc % 2 == 0 else (yb[1], "yb1")
                o("vector", lambda v, hbx=hbx, mc=mc, ybx=ybx: v.tensor_scalar(
                    ybx[:], hbx[:, 1:TB + 1], cvw[:, 3 * mc + 1:3 * mc + 2],
                    cvb[:, mc:mc + 1], ALU.mult, ALU.add),
                  reads=[khb, "cvw", "cvb"], writes=[kyb])
                o("vector", lambda v, hbx=hbx, mc=mc, ybx=ybx: v.scalar_tensor_tensor(
                    ybx[:], hbx[:, 0:TB], cvw[:, 3 * mc:3 * mc + 1],
                    ybx[:], ALU.mult, ALU.add),
                  reads=[khb, kyb, "cvw"], writes=[kyb])
                o("vector", lambda v, hbx=hbx, mc=mc, ybx=ybx: v.scalar_tensor_tensor(
                    ybx[:], hbx[:, 2:TB + 2], cvw[:, 3 * mc + 2:3 * mc + 3],
                    ybx[:], ALU.mult, ALU.add),
                  reads=[khb, kyb, "cvw"], writes=[kyb])
                o("scalar", lambda sc, mc=mc, ybx=ybx: sc.activation(
                    sl[:, mc * TB:(mc + 1) * TB], ybx[:], AF.Silu),
                  reads=[kyb], writes=[f"sl{mc}"])
            for mc in range(32):
                ka = "ab0" if mc % 2 == 0 else "ab1"
                abx = ab[0] if mc % 2 == 0 else ab[1]
                o("vector", lambda v, mc=mc, abx=abx: v.scalar_tensor_tensor(
                    abx[:], sl[:, mc * TB:(mc + 1) * TB],
                    -1.0, sl[:, mc * TB:(mc + 1) * TB], ALU.mult, ALU.max),
                  reads=[f"sl{mc}"], writes=[ka])
                o("vector", lambda v, mc=mc, abx=abx: v.tensor_tensor(
                    m1b[:], m1b[:], abx[:], ALU.max),
                  reads=[ka, "m1b"], writes=["m1b"])
            o("tensor", lambda t: t.matmul(pS[:, 512:640], m1b[:], eye_f[:],
                                           start=True, stop=True),
              reads=["m1b", "eye_f"], writes=["pS1"])
            o("vector", lambda v: v.tensor_reduce(
                stt[:, 8:9], pS[:, 512:640], AX.X, ALU.max),
              reads=["pS1"], writes=["m2r"])
            o("vector", lambda v: v.tensor_scalar(
                stt[:, 10:11], stt[:, 8:9], 1e-5, None, ALU.max),
              reads=["m2r"], writes=["m2"])
            o("vector", lambda v: v.reciprocal(stt[:, 12:13], stt[:, 10:11]),
              reads=["m2"], writes=["im2"])
            o("vector", lambda v: v.tensor_scalar(
                stt[:, 12:13], stt[:, 12:13], 127.0, None, ALU.mult),
              reads=["im2"], writes=["f2"])
            o("vector", lambda v: v.tensor_tensor(
                stt[:, 14:15], stt[:, 10:11], mwr[:, 5:6], ALU.mult),
              reads=["m2", "mw2s"], writes=["cs2"])
            o("tensor", lambda t: t.matmul(pS[0:1, 512:640], stt[:, 12:13], eye_f[:],
                                           start=True, stop=True),
              reads=["f2", "eye_f"], writes=["pS1"])
            o("vector", lambda v: v.tensor_copy(f2r[0:1, 0:TB], pS[0:1, 512:640]),
              reads=["pS1"], writes=["f2r"])
            o("tensor", lambda t: t.matmul(pS[:, 512:640], ones_f[0:1, :],
                                           f2r[0:1, 0:TB], start=True, stop=True),
              reads=["f2r", "ones_f"], writes=["pS1"])
            o("vector", lambda v: v.tensor_copy(bc2[:], pS[:, 512:640]),
              reads=["pS1"], writes=["bc2"])
            for mc in range(32):
                o("vector", lambda v, mc=mc: v.tensor_tensor(
                    sl[:, mc * TB:(mc + 1) * TB], sl[:, mc * TB:(mc + 1) * TB],
                    bc2[:], ALU.mult),
                  reads=[f"sl{mc}", "bc2"], writes=[f"sl{mc}"])
                o("vector", lambda v, mc=mc: v.tensor_scalar(
                    xq2[:, mc * TB:(mc + 1) * TB], sl[:, mc * TB:(mc + 1) * TB],
                    MAGIC, MAGIC, ALU.add, ALU.subtract),
                  reads=[f"sl{mc}"], writes=[f"xq2{mc}"])
            xa, kx = xt[tb % 2], f"xt{tb % 2}"
            d("sync", xa[:], src[seq, t0:t0 + TB, :], writes=[kx], group=f"xin{tb % 2}")
            for dh in range(2):
                pb, kpb = (p6, "p6") if dh == 0 else (p7, "p7")
                for mc in range(32):
                    o("tensor", lambda t, mc=mc, dh=dh, pb=pb: t.matmul(
                        pb[:], xq2[:, mc * TB:(mc + 1) * TB],
                        w2q[mc][:, dh * 512:(dh + 1) * 512],
                        start=(mc == 0), stop=(mc == 31)),
                      reads=[f"xq2{mc}", f"w2q{mc}"],
                      writes=[kpb], inc=(mc == 31))
                o("vector", lambda v, pb=pb, dh=dh, xa=xa: v.scalar_tensor_tensor(
                    xa[:, dh * 512:(dh + 1) * 512], pb[:], stt[:, 14:15],
                    xa[:, dh * 512:(dh + 1) * 512], ALU.mult, ALU.add),
                  reads=[kpb, "cs2", kx], writes=[kx])
            d("sync", dst[seq, t0:t0 + TB, :], xa[:], reads=[kx], group="xout")
        arB.pop(); arF.pop()

    def stage_attn(seq):
        S_.barrier()
        arB.push(); arF.push()
        oall = [arB.alloc(S) for _ in range(8)]
        xnbuf = arB.alloc(D)
        whq8 = arB.alloc(2 * 8 * NBC)
        ut = [arF.alloc(D) for _ in range(2)]
        stt = arF.alloc(16)
        lrec = arF.alloc(QB)
        bcst = arF.alloc(QB)
        wtmp = arF.alloc(NBC)
        arB.push()
        xnT = [arB.alloc(S) for _ in range(8)]

        for tt in range(NT):
            xa, kx = ut[tt % 2], f"ut{tt % 2}"
            d("sync", xa[:], sx1[seq, tt * 128:(tt + 1) * 128, :],
              writes=[kx], group=f"bx{tt % 2}")
            o("vector", lambda v: v.bn_stats(stt[:, 0:6], xa[:, 0:512]),
              reads=[kx], writes=["st0"])
            o("vector", lambda v: v.bn_stats(stt[:, 6:12], xa[:, 512:1024]),
              reads=[kx], writes=["st1"])
            o("vector", lambda v: v.bn_aggr(stt[:, 12:14], stt[:, 0:12]),
              reads=["st0", "st1"], writes=["mu"])
            o("vector", lambda v: v.tensor_scalar(stt[:, 14:15], stt[:, 13:14],
                                                  1e-5, None, ALU.add),
              reads=["mu"], writes=["vpe"])
            o("scalar", lambda sc: sc.activation(stt[:, 15:16], stt[:, 14:15], AF.Sqrt),
              reads=["vpe"], writes=["sig"])
            o("vector", lambda v: v.reciprocal(stt[:, 14:15], stt[:, 15:16]),
              reads=["sig"], writes=["rstd"])
            o("vector", lambda v: v.tensor_scalar(
                xa[:], xa[:], stt[:, 12:13], stt[:, 14:15], ALU.subtract, ALU.mult),
              reads=[kx, "mu", "rstd"], writes=[kx])
            d("sync", su[seq, tt * 128:(tt + 1) * 128, :], xa[:], reads=[kx], group="su")
            if debug:
                d("sync", dbg["d_u"][seq, tt * 128:(tt + 1) * 128, :], xa[:],
                  reads=[kx], group="dbgu")
            o("vector", lambda v: v.bn_stats(stt[:, 0:6], xa[:, 0:512]),
              reads=[kx], writes=["st0"])
            o("vector", lambda v: v.bn_stats(stt[:, 6:12], xa[:, 512:1024]),
              reads=[kx], writes=["st1"])
            o("vector", lambda v: v.bn_aggr(stt[:, 12:14], stt[:, 0:12]),
              reads=["st0", "st1"], writes=["mu"])
            o("vector", lambda v: v.tensor_scalar(stt[:, 14:15], stt[:, 13:14],
                                                  1e-6, None, ALU.add),
              reads=["mu"], writes=["vpe"])
            o("scalar", lambda sc: sc.activation(stt[:, 15:16], stt[:, 14:15], AF.Sqrt),
              reads=["vpe"], writes=["sig"])
            o("vector", lambda v: v.reciprocal(stt[:, 14:15], stt[:, 15:16]),
              reads=["sig"], writes=["rstd"])
            o("vector", lambda v: v.scalar_tensor_tensor(
                xa[:], xa[:], stt[:, 12:13], g1r[:], ALU.subtract, ALU.mult),
              reads=[kx, "mu", "g1r"], writes=[kx])
            o("vector", lambda v: v.scalar_tensor_tensor(
                xnbuf[:], xa[:], stt[:, 14:15], b1r[:], ALU.mult, ALU.add),
              reads=[kx, "rstd", "b1r"], writes=["xnbuf"])
            for c in range(4):
                o("tensor", lambda t, c=c: t.matmul(
                    pS[:, 1024 + c * 128:1024 + (c + 1) * 128],
                    xnbuf[:, 2 * c * 128:(2 * c + 1) * 128], eye_b[:],
                    start=True, stop=True),
                  reads=["xnbuf", "eye_b"], writes=["pS2"], inc=(c == 3))
                o("tensor", lambda t, c=c: t.matmul(
                    pS[:, 1536 + c * 128:1536 + (c + 1) * 128],
                    xnbuf[:, (2 * c + 1) * 128:(2 * c + 2) * 128], eye_b[:],
                    start=True, stop=True),
                  reads=["xnbuf", "eye_b"], writes=["pS3"], inc=(c == 3))
            for c in range(4):
                o("vector", lambda v, c=c, tt=tt: v.tensor_copy(
                    xnT[2 * c][:, tt * 128:(tt + 1) * 128],
                    pS[:, 1024 + c * 128:1024 + (c + 1) * 128]),
                  reads=["pS2"], writes=[f"xnT{2 * c}:{tt}"], inc=(c == 3))
                o("scalar", lambda sc, c=c, tt=tt: sc.copy(
                    xnT[2 * c + 1][:, tt * 128:(tt + 1) * 128],
                    pS[:, 1536 + c * 128:1536 + (c + 1) * 128]),
                  reads=["pS3"], writes=[f"xnT{2 * c + 1}:{tt}"], inc=(c == 3))
        xnall = [f"xnT{i}:{j}" for i in range(8) for j in range(NT)]

        for qki, wdr in ((0, whqd), (1, whkd)):
            for p in range(8):
                for par in range(2):
                    h_ = 2 * p + par
                    d("sync", wtmp[par * 64:(par + 1) * 64, :],
                      wdr[h_ * DH:(h_ + 1) * DH, :], writes=[f"wtmp{par}"], group="whq")
                o("vector", lambda v, p=p, qki=qki: v.tensor_scalar(
                    whq8[:, (qki * 8 + p) * NBC:(qki * 8 + p + 1) * NBC],
                    wtmp[:, :], 8.0 if qki == 0 else 1.0, None, ALU.mult),
                  reads=["wtmp0", "wtmp1"], writes=[f"whq8:{qki}:{p}"])

        for half in range(2):
            arB.push(); arF.push()
            qsb = [arB.alloc(S) for _ in range(4)]
            ksb = [arB.alloc(S) for _ in range(4)]
            qbb = [arB.alloc(S) for _ in range(4)]
            kbb = [arB.alloc(S) for _ in range(4)]
            vsb = [arB.alloc(NKC * 65 + 1) for _ in range(8)]
            qbn = arB.alloc(128)
            eqf = arF.alloc(128)
            pair0 = half * 4

            for hh in range(8):
                o("vector", lambda v, hh=hh: v.memset(
                    vsb[hh][:, 0:NKC * 65].rearrange("p (a b) -> p a b", b=65)[:, :, 64:65], 1.0),
                  writes=[f"vones{hh}"])

            arB.push()
            wpr = [arB.alloc(8 * 128) for _ in range(2)]
            wvh = arB.alloc(8 * 512)
            for which, wdr, dstl, scl in (("q", wqd, qsb, 0.125), ("k", wkd, ksb, 1.0)):
                wb = wpr[0] if which == "q" else wpr[1]
                for pi in range(4):
                    p = pair0 + pi
                    kwb = f"wpr{which}"
                    bdr = bwq if which == "q" else bwk
                    d("sync", wb[:].rearrange("p (a m) -> p a m", a=8),
                      bdr[:, p * 128:(p + 1) * 128].rearrange(
                          "(a p) m -> p a m", p=128),
                      writes=[f"{kwb}:{dc2}" for dc2 in range(8)], group=f"w{which}")
                    for nb_ in range(S // W512):
                        pb, kpb = (p4, "p4") if nb_ % 2 == 0 else (p5, "p5")
                        for dc in range(8):
                            o("tensor", lambda t, dc=dc, nb_=nb_, wb=wb, pb=pb: t.matmul(
                                pb[:, 0:W512], wb[:, dc * 128:(dc + 1) * 128],
                                xnT[dc][:, nb_ * W512:(nb_ + 1) * W512],
                                start=(dc == 0), stop=(dc == 7)),
                              reads=[f"{kwb}:{dc}"] + (xnall if dc == 0 else []),
                              writes=[kpb], inc=(dc == 7))
                        o("vector", lambda v, pb=pb, nb_=nb_, pi=pi, dstl=dstl, scl=scl: v.tensor_scalar(
                            dstl[pi][:, nb_ * W512:(nb_ + 1) * W512], pb[:, 0:W512],
                            scl, None, ALU.mult),
                          reads=[kpb], writes=[f"{which}sb{pi}:{nb_}"])
            d("sync", wvh[:].rearrange("p (a m) -> p a m", a=8),
              bwv[:, half * 512:(half + 1) * 512].rearrange(
                  "(a p) m -> p a m", p=128),
              writes=[f"wvh:{dc2}" for dc2 in range(8)], group="wv")
            for tt in range(NT):
                pb, kpb = (p6, "p6") if tt % 2 == 0 else (p7, "p7")
                for dc in range(8):
                    o("tensor", lambda t, dc=dc, tt=tt, pb=pb: t.matmul(
                        pb[:], xnT[dc][:, tt * 128:(tt + 1) * 128],
                        wvh[:, dc * 512:(dc + 1) * 512],
                        start=(dc == 0), stop=(dc == 7)),
                      reads=[f"wvh:{dc}"] + (xnall if dc == 0 else []),
                      writes=[kpb], inc=(dc == 7))
                for hh in range(8):
                    o("vector", lambda v, hh=hh, tt=tt, pb=pb: v.tensor_copy(
                        vsb[hh][:, tt * 65:tt * 65 + 64], pb[:, hh * 64:(hh + 1) * 64]),
                      reads=[kpb], writes=[f"v{hh}:{tt}"], inc=(hh == 7))
            arB.pop()
            ptb = [arB.alloc(2 * QB) for _ in range(2)]

            for qki, srcl, dstl, scl in ((0, qsb, qbb, 0.1), (1, ksb, kbb, 1.0)):
                sk = "q" if qki == 0 else "k"
                dk = "qb" if qki == 0 else "kb"
                for gg in range(2):
                    for tt in range(NT):
                        for j in range(4):
                            pi = 2 * gg + j // 2
                            par = j % 2
                            pb, kpb = (p6, "p6") if par == 0 else (p7, "p7")
                            o("tensor", lambda t, pi=pi, par=par, j=j, tt=tt, qki=qki, pb=pb, srcl=srcl: t.matmul(
                                pb[0:128, (j // 2) * 32:(j // 2) * 32 + 32],
                                srcl[pi][par * 64:(par + 1) * 64, tt * 128:(tt + 1) * 128],
                                whq8[par * 64:(par + 1) * 64,
                                     (qki * 8 + pair0 + pi) * NBC:(qki * 8 + pair0 + pi + 1) * NBC],
                                start=(j // 2 == 0), stop=(j // 2 == 1),
                                tile_position=(par * 64, 0)),
                              reads=[f"{sk}sb{pi}:{tt * 128 // W512}",
                                     f"whq8:{qki}:{pair0 + pi}"],
                              writes=[kpb], inc=(j // 2 == 1))
                        o("scalar", lambda sc: sc.activation(eqf[:, 0:64], p6[:, 0:64], AF.Exp),
                          reads=["p6"], writes=["eq0"])
                        o("scalar", lambda sc: sc.activation(eqf[:, 64:128], p7[:, 0:64], AF.Exp),
                          reads=["p7"], writes=["eq1"])
                        o("vector", lambda v: v.tensor_reduce(
                            stt[:, 0:4], eqf[:, 0:128].rearrange("p (a b) -> p a b", b=32),
                            AX.X, ALU.add),
                          reads=["eq0", "eq1"], writes=["eqs"])
                        o("vector", lambda v: v.reciprocal(stt[:, 4:8], stt[:, 0:4]),
                          reads=["eqs"], writes=["eqr"])
                        for ci, j in ((0, 0), (1, 2), (2, 1), (3, 3)):
                            o("vector", lambda v, ci=ci, scl=scl: v.tensor_scalar(
                                qbn[:, ci * 32:(ci + 1) * 32], eqf[:, ci * 32:(ci + 1) * 32],
                                stt[:, 4 + ci:5 + ci], scl, ALU.mult, ALU.mult),
                              reads=["eqr", "eq0", "eq1"], writes=["qbn"], inc=(ci == 3))
                        first = True
                        for ci, j in ((0, 0), (1, 2), (2, 1), (3, 3)):
                            par = j % 2
                            o("tensor", lambda t, ci=ci, par=par, first=first: t.matmul(
                                pS[par * 64:par * 64 + 32,
                                   512 + ci * 128:512 + (ci + 1) * 128],
                                qbn[:, ci * 32:(ci + 1) * 32], eye_b[:],
                                start=(ci in (0, 2)), stop=(ci in (1, 3)),
                                tile_position=(0, par * 64)),
                              reads=["qbn", "eye_b"], writes=["pS1"], inc=(ci == 3))
                            first = False
                        for ci, j in ((0, 0), (1, 2), (2, 1), (3, 3)):
                            par = j % 2
                            pi = 2 * gg + j // 2
                            o("vector", lambda v, ci=ci, par=par, pi=pi, tt=tt, dstl=dstl: v.tensor_copy(
                                dstl[pi][par * 64:par * 64 + 32, tt * 128:(tt + 1) * 128],
                                pS[par * 64:par * 64 + 32, 512 + ci * 128:512 + (ci + 1) * 128]),
                              reads=["pS1"], writes=[f"{dk}{pi}:{tt}"], inc=(ci == 3))

            for pi in range(4):
                for qb_ in range(NQB):
                    q0 = qb_ * QB
                    nb_ = q0 // W512
                    obpar = (pi * NQB + qb_) % 2
                    obE, okE = (p4, "p4") if obpar == 0 else (p6, "p6")
                    obO, okO = (p5, "p5") if obpar == 0 else (p7, "p7")
                    hhE, hhO = 2 * pi, 2 * pi + 1
                    pg = pair0 + pi
                    for kc in range(NKC):
                        par = kc % 2
                        sE = pS[:, par * 1024:par * 1024 + QB]
                        sO = pS[:, par * 1024 + 512:par * 1024 + 512 + QB]
                        kE, kO = f"pS{2 * par}", f"pS{2 * par + 1}"
                        o("tensor", lambda t, kc=kc, q0=q0, sE=sE: t.matmul(
                            sE, ksb[pi][0:64, kc * 128:(kc + 1) * 128],
                            qsb[pi][0:64, q0:q0 + QB],
                            start=True, stop=False, tile_position=(0, 0)),
                          reads=[f"qsb{pi}:{nb_}", f"ksb{pi}:{kc * 128 // W512}"],
                          writes=[kE], inc=False)
                        o("tensor", lambda t, kc=kc, q0=q0, sO=sO: t.matmul(
                            sO, ksb[pi][64:128, kc * 128:(kc + 1) * 128],
                            qsb[pi][64:128, q0:q0 + QB],
                            start=True, stop=False, tile_position=(64, 0)),
                          reads=[f"qsb{pi}:{nb_}", f"ksb{pi}:{kc * 128 // W512}"],
                          writes=[kO], inc=False)
                        o("tensor", lambda t, kc=kc, q0=q0, sE=sE: t.matmul(
                            sE, kbb[pi][0:32, kc * 128:(kc + 1) * 128],
                            qbb[pi][0:32, q0:q0 + QB],
                            start=False, stop=True, tile_position=(0, 0)),
                          reads=[f"qb{pi}:{q0 // 128 + z}" for z in range(QB // 128)] + [f"kb{pi}:{kc}"],
                          writes=[kE], inc=True)
                        o("tensor", lambda t, kc=kc, q0=q0, sO=sO: t.matmul(
                            sO, kbb[pi][64:96, kc * 128:(kc + 1) * 128],
                            qbb[pi][64:96, q0:q0 + QB],
                            start=False, stop=True, tile_position=(64, 0)),
                          reads=[f"qb{pi}:{q0 // 128 + z}" for z in range(QB // 128)] + [f"kb{pi}:{kc}"],
                          writes=[kO], inc=True)
                        kpt = f"ptb{kc % 2}"
                        o("scalar", lambda sc, kc=kc, par=par: sc.activation(
                            ptb[kc % 2][:].rearrange("p (a b) -> p a b", b=QB),
                            pS[:, par * 1024:(par + 1) * 1024].rearrange(
                                "p (a b) -> p a b", b=512)[:, :, 0:QB],
                            AF.Exp, bias=nb20[:, 0:1]),
                          reads=[kE, kO], writes=[kpt])
                        o("tensor", lambda t, kc=kc, hhE=hhE, obE=obE: t.matmul(
                            obE[0:65, 0:QB], vsb[hhE][:, kc * 65:kc * 65 + 65],
                            ptb[kc % 2][:, 0:QB],
                            start=(kc == 0), stop=(kc == NKC - 1)),
                          reads=[kpt, f"v{hhE}:{kc}", f"vones{hhE}"],
                          writes=[okE], inc=True)
                        o("tensor", lambda t, kc=kc, hhO=hhO, obO=obO: t.matmul(
                            obO[64:128, 0:QB], vsb[hhO][:, kc * 65:kc * 65 + 64],
                            ptb[kc % 2][:, QB:2 * QB],
                            start=(kc == 0), stop=(kc == NKC - 1),
                            tile_position=(0, 64)),
                          reads=[kpt, f"v{hhO}:{kc}"], writes=[okO], inc=False)
                        o("tensor", lambda t, kc=kc, hhO=hhO, obO=obO: t.matmul(
                            obO[32:33, 0:QB], vsb[hhO][:, kc * 65 + 64:kc * 65 + 65],
                            ptb[kc % 2][:, QB:2 * QB],
                            start=(kc == 0), stop=(kc == NKC - 1),
                            tile_position=(0, 32)),
                          reads=[kpt, f"vones{hhO}"], writes=[okO], inc=True)
                    o("vector", lambda v, obE=obE: v.reciprocal(
                        lrec[64:65, 0:QB], obE[64:65, 0:QB]),
                      reads=[okE], writes=["lrec"])
                    o("tensor", lambda t: t.matmul(
                        pS[0:64, 0:QB], ones_f[64:65, 0:64], lrec[64:65, 0:QB],
                        start=True, stop=True, tile_position=(64, 0)),
                      reads=["lrec", "ones_f"], writes=["pS0"])
                    o("scalar", lambda sc: sc.copy(bcst[0:64, 0:QB], pS[0:64, 0:QB]),
                      reads=["pS0"], writes=["bcst0"])
                    o("vector", lambda v, obE=obE, pg=pg, q0=q0: v.tensor_tensor(
                        oall[pg][0:64, q0:q0 + QB], obE[0:64, 0:QB],
                        bcst[0:64, 0:QB], ALU.mult),
                      reads=[okE, "bcst0"], writes=[f"oall{pg}:{q0}"])
                    o("vector", lambda v, obO=obO: v.reciprocal(
                        lrec[32:33, 0:QB], obO[32:33, 0:QB]),
                      reads=[okO], writes=["lrec2"])
                    o("tensor", lambda t: t.matmul(
                        pS[64:128, 0:QB], ones_f[32:33, 0:64], lrec[32:33, 0:QB],
                        start=True, stop=True, tile_position=(32, 64)),
                      reads=["lrec2", "ones_f"], writes=["pS0"])
                    o("scalar", lambda sc: sc.copy(bcst[64:128, 0:QB], pS[64:128, 0:QB]),
                      reads=["pS0"], writes=["bcst1"])
                    o("vector", lambda v, obO=obO, pg=pg, q0=q0: v.tensor_tensor(
                        oall[pg][64:128, q0:q0 + QB], obO[64:128, 0:QB],
                        bcst[64:128, 0:QB], ALU.mult),
                      reads=[okO, "bcst1"], writes=[f"oall{pg}:{q0}"])
            S_.barrier()
            arB.pop(); arF.pop()
        arB.pop()

        arB.push(); arF.push()
        wob = arB.alloc(8 * D)
        w1b = [arB.alloc(8 * 128) for _ in range(4)]
        w2b = [arB.alloc(D) for _ in range(4)]
        h1 = arB.alloc(32 * MLPB)
        xmT = [arB.alloc(MLPB) for _ in range(8)]
        xmb = arB.alloc(D)
        x3b = arB.alloc(2 * (MLPB // 128) * D).bitcast(F32)
        x1t = arF.alloc(D)
        for pg in range(8):
            d("sync", wob[:, pg * D:(pg + 1) * D], bwo[pg, :, :],
              writes=[f"wob{pg}"], group="wo")
        oallk = [f"oall{pg}:{q0}" for pg in range(8) for q0 in range(0, S, QB)]

        for mb in range(NMB):
            tiles = [mb * (MLPB // 128) + c for c in range(MLPB // 128)]
            for ci, tt in enumerate(tiles):
                for dh in range(2):
                    pb, kpb = (p4, "p4") if dh == 0 else (p5, "p5")
                    for pg in range(8):
                        o("tensor", lambda t, pg=pg, dh=dh, tt=tt, pb=pb: t.matmul(
                            pb[:], oall[pg][:, tt * 128:(tt + 1) * 128],
                            wob[:, pg * D + dh * 512:pg * D + (dh + 1) * 512],
                            start=(pg == 0), stop=(pg == 7)),
                          reads=[f"wob{pg}"] + (oallk if pg == 0 else []),
                          writes=[kpb], inc=(pg == 7))
                kx, xu = f"ut{tt % 2}", ut[tt % 2]
                d("sync", xu[:], su[seq, tt * 128:(tt + 1) * 128, :],
                  writes=[kx], group=f"bx{tt % 2}")
                for dh in range(2):
                    pb, kpb = (p4, "p4") if dh == 0 else (p5, "p5")
                    o("vector", lambda v, pb=pb, dh=dh, ci=ci, xu=xu: v.tensor_tensor(
                        x3b[:, ci * D + dh * 512:ci * D + (dh + 1) * 512],
                        pb[:], xu[:, dh * 512:(dh + 1) * 512], ALU.add),
                      reads=[kpb, kx], writes=[f"x3:{ci}"])
                kx3 = f"x3:{ci}"
                o("vector", lambda v, ci=ci: v.bn_stats(
                    stt[:, 0:6], x3b[:, ci * D:ci * D + 512]),
                  reads=[kx3], writes=["st0"])
                o("vector", lambda v, ci=ci: v.bn_stats(
                    stt[:, 6:12], x3b[:, ci * D + 512:ci * D + 1024]),
                  reads=[kx3], writes=["st1"])
                o("vector", lambda v: v.bn_aggr(stt[:, 12:14], stt[:, 0:12]),
                  reads=["st0", "st1"], writes=["mu"])
                o("vector", lambda v: v.tensor_scalar(stt[:, 14:15], stt[:, 13:14],
                                                      1e-6, None, ALU.add),
                  reads=["mu"], writes=["vpe"])
                o("scalar", lambda sc: sc.activation(stt[:, 15:16], stt[:, 14:15], AF.Sqrt),
                  reads=["vpe"], writes=["sig"])
                o("vector", lambda v: v.reciprocal(stt[:, 14:15], stt[:, 15:16]),
                  reads=["sig"], writes=["rstd"])
                o("vector", lambda v, ci=ci: v.scalar_tensor_tensor(
                    xmb[:], x3b[:, ci * D:(ci + 1) * D], stt[:, 12:13], g2r[:],
                    ALU.subtract, ALU.mult),
                  reads=[kx3, "mu", "g2r"], writes=["xmb"])
                o("vector", lambda v: v.scalar_tensor_tensor(
                    xmb[:], xmb[:], stt[:, 14:15], b2lr[:], ALU.mult, ALU.add),
                  reads=["xmb", "rstd", "b2lr"], writes=["xmb"])
                for c in range(4):
                    o("tensor", lambda t, c=c: t.matmul(
                        pS[:, 1024 + c * 128:1024 + (c + 1) * 128],
                        xmb[:, 2 * c * 128:(2 * c + 1) * 128], eye_b[:],
                        start=True, stop=True),
                      reads=["xmb", "eye_b"], writes=["pS2"], inc=(c == 3))
                    o("tensor", lambda t, c=c: t.matmul(
                        pS[:, 1536 + c * 128:1536 + (c + 1) * 128],
                        xmb[:, (2 * c + 1) * 128:(2 * c + 2) * 128], eye_b[:],
                        start=True, stop=True),
                      reads=["xmb", "eye_b"], writes=["pS3"], inc=(c == 3))
                for c in range(4):
                    o("vector", lambda v, c=c, ci=ci: v.tensor_copy(
                        xmT[2 * c][:, ci * 128:(ci + 1) * 128],
                        pS[:, 1024 + c * 128:1024 + (c + 1) * 128]),
                      reads=["pS2"], writes=[f"xmT{2 * c}:{ci}"], inc=(c == 3))
                    o("scalar", lambda sc, c=c, ci=ci: sc.copy(
                        xmT[2 * c + 1][:, ci * 128:(ci + 1) * 128],
                        pS[:, 1536 + c * 128:1536 + (c + 1) * 128]),
                      reads=["pS3"], writes=[f"xmT{2 * c + 1}:{ci}"], inc=(c == 3))
            xmk = [f"xmT{i}:{j}" for i in range(8) for j in range(MLPB // 128)]
            for mc in range(32):
                wb, kwb = w1b[mc % 4], f"w1b{mc % 4}"
                d("sync", wb[:].rearrange("p (a m) -> p a m", a=8),
                  bw1[:, mc * 128:(mc + 1) * 128].rearrange(
                      "(a p) m -> p a m", p=128),
                  writes=[f"{kwb}:{dc2}" for dc2 in range(8)], group=f"wm1{mc % 4}")
                pb, kpb = (p6, "p6") if mc % 2 == 0 else (p7, "p7")
                for dc in range(8):
                    o("tensor", lambda t, dc=dc, mc=mc, wb=wb, pb=pb: t.matmul(
                        pb[:, 0:MLPB], wb[:, dc * 128:(dc + 1) * 128],
                        xmT[dc][:, 0:MLPB], start=(dc == 0), stop=(dc == 7)),
                      reads=[f"{kwb}:{dc}"] + (xmk if dc == 0 else []),
                      writes=[kpb], inc=(dc == 7))
                o("vector", lambda v, mc=mc, pb=pb: v.tensor_scalar(
                    h1[:, mc * MLPB:(mc + 1) * MLPB], pb[:, 0:MLPB],
                    b1cc[:, mc:mc + 1], 0.0, ALU.add, ALU.max),
                  reads=[kpb, "b1cc"], writes=[f"h1:{mc}"])
            ypb = [pS[:, 0:512], pS[:, 512:1024], pS[:, 1024:1536],
                   pS[:, 1536:2048], p4[:], p5[:], p6[:], p7[:]]
            ypk = ["pS0", "pS1", "pS2", "pS3", "p4", "p5", "p6", "p7"]
            for mc in range(32):
                wb, kwb = w2b[mc % 4], f"w2b{mc % 4}"
                d("sync", wb[:], bw2[mc * 128:(mc + 1) * 128, :],
                  writes=[kwb], group=f"wm2{mc % 4}")
                for ci in range(MLPB // 128):
                    for dh in range(2):
                        bi = ci * 2 + dh
                        o("tensor", lambda t, mc=mc, ci=ci, dh=dh, wb=wb, bi=bi: t.matmul(
                            ypb[bi], h1[:, mc * MLPB + ci * 128:mc * MLPB + (ci + 1) * 128],
                            wb[:, dh * 512:(dh + 1) * 512],
                            start=(mc == 0), stop=(mc == 31)),
                          reads=[f"h1:{mc}", kwb],
                          writes=[ypk[bi]],
                          inc=(ci == MLPB // 128 - 1 and dh == 1))
            for ci, tt in enumerate(tiles):
                d("sync", x1t[:], sx1[seq, tt * 128:(tt + 1) * 128, :],
                  writes=["x1t"], group="bx1r")
                for dh in range(2):
                    bi = ci * 2 + dh
                    o("vector", lambda v, ci=ci, dh=dh, bi=bi: v.tensor_tensor(
                        x3b[:, ci * D + dh * 512:ci * D + (dh + 1) * 512],
                        x3b[:, ci * D + dh * 512:ci * D + (dh + 1) * 512],
                        ypb[bi], ALU.add),
                      reads=[ypk[bi], f"x3:{ci}"], writes=[f"x3:{ci}"])
                    o("vector", lambda v, ci=ci, dh=dh: v.tensor_tensor(
                        x3b[:, ci * D + dh * 512:ci * D + (dh + 1) * 512],
                        x3b[:, ci * D + dh * 512:ci * D + (dh + 1) * 512],
                        b2rr[:, dh * 512:(dh + 1) * 512], ALU.add),
                      reads=[f"x3:{ci}", "b2rr"], writes=[f"x3:{ci}"])
                    o("vector", lambda v, ci=ci, dh=dh: v.tensor_tensor(
                        x3b[:, ci * D + dh * 512:ci * D + (dh + 1) * 512],
                        x3b[:, ci * D + dh * 512:ci * D + (dh + 1) * 512],
                        x1t[:, dh * 512:(dh + 1) * 512], ALU.add),
                      reads=[f"x3:{ci}", "x1t"], writes=[f"x3:{ci}"])
                d("sync", sx4[seq, tt * 128:(tt + 1) * 128, :],
                  x3b[:, ci * D:(ci + 1) * D], reads=[f"x3:{ci}"], group="x4out")
                if debug:
                    d("sync", dbg["d_x4"][seq, tt * 128:(tt + 1) * 128, :],
                      x3b[:, ci * D:(ci + 1) * D], reads=[f"x3:{ci}"], group="dbg4")
        arB.pop(); arF.pop()
        arB.pop(); arF.pop()

    for seq in range(NSEQ):
        stage_ffn(seq, xin, sx1)
        if debug:
            S_.barrier()
            arF.push()
            tmp = arF.alloc(D)
            for tt in range(NT):
                dt_ = d("sync", tmp[:], sx1[seq, tt * 128:(tt + 1) * 128, :],
                        writes=["dtmp"], group="dbg1")
                d("sync", dbg["d_x1"][seq, tt * 128:(tt + 1) * 128, :], tmp[:],
                  reads=["dtmp"], group="dbg1b")
            arF.pop()
        stage_attn(seq)
        stage_ffn(seq, sx4, out)
    S_.emit("sync")
    return nc, st, dbg


def _host_inputs(inputs, S=2048):
    f32 = np.float32
    i = inputs
    shared = {
        "w1t": np.ascontiguousarray(i["L0_ffn_w1"].T.astype(f32)),
        "w2t": np.ascontiguousarray(i["L0_ffn_w2"].T.astype(f32)),
        "convw": np.ascontiguousarray(
            i["L0_ffn_conv_w"].reshape(32, 128, 3).transpose(1, 0, 2).reshape(128, 96).astype(f32)),
        "convb": np.ascontiguousarray(
            i["L0_ffn_conv_b"].reshape(32, 128).T.astype(f32)),
        "g1d": i["L0_attn_ln1_g"].reshape(1, D).astype(f32),
        "b1d": i["L0_attn_ln1_b"].reshape(1, D).astype(f32),
        "g2d": i["L0_attn_ln2_g"].reshape(1, D).astype(f32),
        "b2d": i["L0_attn_ln2_b"].reshape(1, D).astype(f32),
        "wqd": np.ascontiguousarray(i["L0_attn_Wq"].reshape(D, H * DH).astype(f32)),
        "wkd": np.ascontiguousarray(i["L0_attn_Wk"].reshape(D, H * DH).astype(f32)),
        "wvd": np.ascontiguousarray(i["L0_attn_Wv"].reshape(D, H * DH).astype(f32)),
        "whqd": np.ascontiguousarray(i["L0_attn_Whq"].reshape(H * DH, NBC).astype(f32)),
        "whkd": np.ascontiguousarray(i["L0_attn_Whk"].reshape(H * DH, NBC).astype(f32)),
        "wod": np.ascontiguousarray(
            i["L0_attn_Wo"].reshape(H // 2, 2 * DH, D).astype(f32)),
        "wm1d": np.ascontiguousarray(i["L0_attn_W1"].astype(f32)),
        "wm2d": np.ascontiguousarray(i["L0_attn_W2"].astype(f32)),
        "b1cd": np.ascontiguousarray(i["L0_attn_b1"].reshape(32, 128).T.astype(f32)),
        "b2rd": i["L0_attn_b2"].reshape(1, D).astype(f32),
        "eyebd": np.eye(128, dtype=f32).astype(bfdt),
        "eyefd": np.eye(128, dtype=f32),
        "onesfd": np.ones((128, 128), dtype=f32),
    }
    x = np.ascontiguousarray(i["L0_attn_inputs"].astype(f32))
    in_maps = []
    for c in range(NCORES):
        m = dict(shared)
        m["xin"] = np.ascontiguousarray(x[c * NSEQ:(c + 1) * NSEQ, :S, :])
        in_maps.append(m)
    return in_maps


_CACHE = {}


def kernel(**inputs):
    if "prog" not in _CACHE:
        nc, st, _ = build_program(S=2048, debug=False)
        _CACHE["prog"] = (nc, st)
    nc, st = _CACHE["prog"]
    in_maps = _host_inputs(inputs, S=2048)
    res = run_bass_kernel_spmd(nc, in_maps, core_ids=list(range(NCORES)))
    outp = np.concatenate([res.results[c]["out"] for c in range(NCORES)], axis=0)
    return outp.astype(np.float32)


# version 12
# speedup vs baseline: 1.7804x; 1.0389x over previous
import numpy as np
import ml_dtypes
from contextlib import ExitStack
from typing import NamedTuple

import concourse.bass as bass
import concourse.mybir as mybir
from concourse.bass_utils import run_bass_kernel_spmd

F32 = mybir.dt.float32
BF16 = mybir.dt.bfloat16
ALU = mybir.AluOpType
AF = mybir.ActivationFunctionType
AX = mybir.AxisListType
bfdt = ml_dtypes.bfloat16

MAGIC = 12582912.0
EXPB = -20.0
D, H, DH, NBC, FF = 1024, 16, 64, 32, 4096
NSEQ, NCORES = 2, 8
ENGINES = ("sync", "tensor", "vector", "scalar", "gpsimd")


class Tok(NamedTuple):
    key: str
    val: int


class _Rec:

    def __getattr__(self, name):
        def call(*a, **k):
            return (name, a, k)
        return call


class Sched:

    def __init__(self, nc, stack):
        self.nc = nc
        self.stack = stack
        self.sem = {e: stack.enter_context(nc.semaphore("S_" + e)) for e in ENGINES}
        self.cnt = {e: 0 for e in ENGINES}
        self.opl = {e: [] for e in ENGINES}
        self.water = {}
        self.dma_groups = {}
        self.pend_bar = {}
        self.wmap = {}
        self.rmap = {}
        self.pend_rw = {e: [] for e in ENGINES}

    def group(self, name):
        if name not in self.dma_groups:
            self.dma_groups[name] = self.stack.enter_context(
                self.nc.semaphore("G_" + name))
            self.cnt[name] = 0
        return name

    def barrier(self):
        toks = [Tok(k, v) for k, v in self.cnt.items() if v > 0]
        for e in ENGINES:
            self.pend_bar[e] = toks
        self.wmap = {}
        self.rmap = {}

    def _rw_deps(self, reads, writes):
        deps = []
        for k in reads:
            deps += self.wmap.get(k, [])
        for k in writes:
            deps += self.wmap.get(k, [])
            deps += self.rmap.get(k, [])
        return deps

    def _rw_commit(self, tok, reads, writes):
        for k in writes:
            self.wmap[k] = [tok]
            self.rmap[k] = []
        for k in reads:
            self.rmap.setdefault(k, []).append(tok)

    def _resolve(self, eng, deps):
        byk = {}
        for t in deps:
            if t is None:
                continue
            v = t.val
            if t.key in self.dma_groups:
                v = self.cnt[t.key]
            if byk.get(t.key, 0) < v:
                byk[t.key] = v
        waits = []
        for k, v in sorted(byk.items()):
            wk = (eng, k)
            if self.water.get(wk, 0) >= v:
                continue
            self.water[wk] = v
            waits.append((k, v))
        return waits

    def op(self, eng, fn, reads=(), writes=(), deps=(), inc=True):
        deps = list(deps) + self._rw_deps(reads, writes)
        if self.pend_bar.get(eng):
            deps += self.pend_bar[eng]
            self.pend_bar[eng] = None
        waits = self._resolve(eng, deps)
        tok, inckey = None, None
        if inc:
            self.cnt[eng] += 1
            tok = Tok(eng, self.cnt[eng])
            inckey = eng
        self.opl[eng].append((fn(_Rec()), waits, inckey, 1))
        if tok is not None:
            for (r2, w2) in self.pend_rw[eng]:
                self._rw_commit(tok, r2, w2)
            self.pend_rw[eng] = []
            self._rw_commit(tok, reads, writes)
        else:
            self.pend_rw[eng].append((list(reads), list(writes)))
        return tok

    def dma(self, eng, out, in_, reads=(), writes=(), deps=(), group="misc", **kw):
        g = self.group(group)
        deps = list(deps) + self._rw_deps(reads, writes)
        if self.pend_bar.get(eng):
            deps += self.pend_bar[eng]
            self.pend_bar[eng] = None
        waits = self._resolve(eng, deps)
        self.cnt[g] += 16
        tok = Tok(g, self.cnt[g])
        self.opl[eng].append((("dma_start", (out, in_), kw), waits, g, 16))
        self._rw_commit(tok, reads, writes)
        return tok

    def emit(self, final_eng="sync"):
        finals = [Tok(g, self.cnt[g]) for g in self.dma_groups if self.cnt[g] > 0]
        self.opl[final_eng].append((None, self._resolve(final_eng, finals), None, 0))
        with self.nc.Block() as block:
            for e in ENGINES:
                if not self.opl[e]:
                    continue
                def body(E, _ops=tuple(self.opl[e])):
                    for spec, waits, inckey, amt in _ops:
                        for (k, v) in waits:
                            E.wait_ge(self._sem_of(k), v)
                        if spec is not None:
                            name, a, kw = spec
                            inst = getattr(E, name)(*a, **kw)
                            if inckey is not None:
                                inst.then_inc(self._sem_of(inckey), amt)
                getattr(block, e)(body)

    def _sem_of(self, key):
        return self.sem[key] if key in self.sem else self.dma_groups[key]


class Arena:
    def __init__(self, tensor, cols):
        self.t, self.cols, self.off, self.marks = tensor, cols, 0, []

    def alloc(self, n):
        n = (n + 1) & ~1
        assert self.off + n <= self.cols, f"arena overflow {self.off}+{n}>{self.cols}"
        ap = self.t[:, self.off:self.off + n]
        self.off += n
        return ap

    def push(self):
        self.marks.append(self.off)

    def pop(self):
        self.off = self.marks.pop()


def build_program(S=2048, debug=False):
    TB = 128
    MLPB = min(512, S)
    NT = S // 128
    NKC = S // 128
    QB = min(512, S)
    NQB = S // QB
    NTB = S // TB
    NMB = S // MLPB
    W512 = min(512, S)

    nc = bass.Bass("TRN2", target_bir_lowering=False)
    st = ExitStack()
    S_ = Sched(nc, st)
    o, d = S_.op, S_.dma

    xin = nc.dram_tensor("xin", [NSEQ, S, D], F32, kind="ExternalInput")
    out = nc.dram_tensor("out", [NSEQ, S, D], F32, kind="ExternalOutput")
    w1t = nc.dram_tensor("w1t", [D, FF], F32, kind="ExternalInput")
    w2t = nc.dram_tensor("w2t", [FF, D], F32, kind="ExternalInput")
    convw = nc.dram_tensor("convw", [128, 3 * (FF // 128)], F32, kind="ExternalInput")
    convb = nc.dram_tensor("convb", [128, FF // 128], F32, kind="ExternalInput")
    g1d = nc.dram_tensor("g1d", [1, D], F32, kind="ExternalInput")
    b1d = nc.dram_tensor("b1d", [1, D], F32, kind="ExternalInput")
    g2d = nc.dram_tensor("g2d", [1, D], F32, kind="ExternalInput")
    b2d = nc.dram_tensor("b2d", [1, D], F32, kind="ExternalInput")
    wqd = nc.dram_tensor("wqd", [D, H * DH], F32, kind="ExternalInput")
    wkd = nc.dram_tensor("wkd", [D, H * DH], F32, kind="ExternalInput")
    wvd = nc.dram_tensor("wvd", [D, H * DH], F32, kind="ExternalInput")
    whqd = nc.dram_tensor("whqd", [H * DH, NBC], F32, kind="ExternalInput")
    whkd = nc.dram_tensor("whkd", [H * DH, NBC], F32, kind="ExternalInput")
    wod = nc.dram_tensor("wod", [H // 2, 2 * DH, D], F32, kind="ExternalInput")
    wm1d = nc.dram_tensor("wm1d", [D, FF], F32, kind="ExternalInput")
    wm2d = nc.dram_tensor("wm2d", [FF, D], F32, kind="ExternalInput")
    b1cd = nc.dram_tensor("b1cd", [128, FF // 128], F32, kind="ExternalInput")
    b2rd = nc.dram_tensor("b2rd", [1, D], F32, kind="ExternalInput")
    eyebd = nc.dram_tensor("eyebd", [128, 128], BF16, kind="ExternalInput")
    eyefd = nc.dram_tensor("eyefd", [128, 128], F32, kind="ExternalInput")
    onesfd = nc.dram_tensor("onesfd", [128, 128], F32, kind="ExternalInput")

    sx1 = nc.dram_tensor("sx1", [NSEQ, S, D], F32)
    su = nc.dram_tensor("su", [NSEQ, S, D], F32)
    sx4 = nc.dram_tensor("sx4", [NSEQ, S, D], F32)
    sq1 = nc.dram_tensor("sq1", [D, FF], BF16)
    sq2 = nc.dram_tensor("sq2", [FF, D], BF16)
    bwq = nc.dram_tensor("bwq", [D, H * DH], BF16)
    bwk = nc.dram_tensor("bwk", [D, H * DH], BF16)
    bwv = nc.dram_tensor("bwv", [D, H * DH], BF16)
    bwo = nc.dram_tensor("bwo", [H // 2, 2 * DH, D], BF16)
    bw1 = nc.dram_tensor("bw1", [D, FF], BF16)
    bw2 = nc.dram_tensor("bw2", [FF, D], BF16)
    dbg = {}
    if debug:
        for nm in ("d_x1", "d_u", "d_x4"):
            dbg[nm] = nc.dram_tensor(nm, [NSEQ, S, D], F32, kind="ExternalOutput")

    BCOLS = 84 * 1024
    FCOLS = 6592
    arB_t = st.enter_context(nc.sbuf_tensor("arB", [128, BCOLS], BF16))
    arF_t = st.enter_context(nc.sbuf_tensor("arF", [128, FCOLS], F32))
    eye_b = st.enter_context(nc.sbuf_tensor("eye_b", [128, 128], BF16))
    eye_f = st.enter_context(nc.sbuf_tensor("eye_f", [128, 128], F32))
    ones_f = st.enter_context(nc.sbuf_tensor("ones_f", [128, 128], F32))
    cvw = st.enter_context(nc.sbuf_tensor("cvw", [128, 3 * (FF // 128)], F32))
    cvb = st.enter_context(nc.sbuf_tensor("cvb", [128, FF // 128], F32))
    g1r = st.enter_context(nc.sbuf_tensor("g1r", [128, D], BF16))
    b1r = st.enter_context(nc.sbuf_tensor("b1r", [128, D], BF16))
    g2r = st.enter_context(nc.sbuf_tensor("g2r", [128, D], BF16))
    b2lr = st.enter_context(nc.sbuf_tensor("b2lr", [128, D], BF16))
    b2rr = st.enter_context(nc.sbuf_tensor("b2rr", [128, D], F32))
    b1cc = st.enter_context(nc.sbuf_tensor("b1cc", [128, FF // 128], F32))
    mwr = st.enter_context(nc.sbuf_tensor("mwr", [128, 8], F32))
    nb20 = st.enter_context(nc.sbuf_tensor("nb20", [128, 1], F32))
    pS = st.enter_context(nc.psum_tensor("pS", [128, 2048], F32))
    p4 = st.enter_context(nc.psum_tensor("p4", [128, 512], F32))
    p5 = st.enter_context(nc.psum_tensor("p5", [128, 512], F32))
    p6 = st.enter_context(nc.psum_tensor("p6", [128, 512], F32))
    p7 = st.enter_context(nc.psum_tensor("p7", [128, 512], F32))
    arB = Arena(arB_t, BCOLS)
    arF = Arena(arF_t, FCOLS)

    for t_, src_ in ((eye_b, eyebd), (eye_f, eyefd), (ones_f, onesfd), (cvw, convw),
                     (cvb, convb), (b1cc, b1cd)):
        d("sync", t_[:], src_[:], writes=[t_.name], group="cst")
    o("vector", lambda v: v.memset(nb20[:], EXPB), writes=["nb20"])
    for t_, src_ in ((g1r, g1d), (b1r, b1d), (g2r, g2d), (b2lr, b2d)):
        d("gpsimd", t_[:], src_[0:1, :].partition_broadcast(128), writes=[t_.name],
          group="cst2")
    d("sync", b2rr[:], b2rd[0:1, :].partition_broadcast(128), writes=[b2rr.name],
      group="cst")

    def quantize_weight(wdram, sqdram, nrow, ncol, mcol):
        nstrip = nrow // 128
        arF.push(); arB.push()
        wstrip = arF.alloc(ncol)
        qs = arB.alloc(ncol)
        o("vector", lambda v: v.memset(mwr[:, 7:8], 0.0), writes=["acc"])
        for i in range(nstrip):
            d("sync", wstrip[:], wdram[i * 128:(i + 1) * 128, :],
              writes=["wstrip"], group="wqld")
            o("vector", lambda v: v.tensor_reduce(
                mwr[:, 6:7], wstrip[:], AX.X, ALU.add, apply_absolute_value=True),
              reads=["wstrip"], writes=["red"])
            o("vector", lambda v: v.tensor_tensor(
                mwr[:, 7:8], mwr[:, 7:8], mwr[:, 6:7], ALU.add),
              reads=["red", "acc"], writes=["acc"])
        o("tensor", lambda t: t.matmul(p4[0:1, 0:1], ones_f[:, 0:1], mwr[:, 7:8],
                                       start=True, stop=True),
          reads=["acc", "ones_f"], writes=["p4"])
        o("vector", lambda v: v.tensor_scalar(
            mwr[0:1, 6:7], p4[0:1, 0:1], 1.0 / (nrow * ncol), 1e-5, ALU.mult, ALU.max),
          reads=["p4"], writes=["mean1"])
        o("tensor", lambda t: t.matmul(p4[:, 1:2], ones_f[0:1, :], mwr[0:1, 6:7],
                                       start=True, stop=True),
          reads=["mean1", "ones_f"], writes=["p4"])
        o("vector", lambda v: v.tensor_copy(mwr[:, mcol:mcol + 1], p4[:, 1:2]),
          reads=["p4"], writes=[f"mw{mcol}"])
        o("vector", lambda v: v.reciprocal(mwr[:, mcol + 1:mcol + 2],
                                           mwr[:, mcol:mcol + 1]),
          reads=[f"mw{mcol}"], writes=[f"mwi{mcol}"])
        for i in range(nstrip):
            d("sync", wstrip[:], wdram[i * 128:(i + 1) * 128, :],
              writes=["wstrip"], group="wqld")
            o("vector", lambda v: v.tensor_scalar(
                wstrip[:], wstrip[:], mwr[:, mcol + 1:mcol + 2], MAGIC,
                ALU.mult, ALU.add),
              reads=["wstrip", f"mwi{mcol}"], writes=["wstrip"])
            o("vector", lambda v: v.tensor_scalar(
                wstrip[:], wstrip[:], MAGIC, None, ALU.subtract),
              reads=["wstrip"], writes=["wstrip"])
            o("vector", lambda v: v.tensor_scalar(
                qs[:], wstrip[:], -1.0, 1.0, ALU.max, ALU.min),
              reads=["wstrip"], writes=["qs"])
            d("sync", sqdram[i * 128:(i + 1) * 128, :], qs[:],
              reads=["qs"], group="wqst")
        arF.pop(); arB.pop()

    for dst_, src_, nr in ((bwq, wqd, D), (bwk, wkd, D), (bwv, wvd, D),
                           (bw1, wm1d, D)):
        for c_ in range(2):
            d("gpsimd", dst_[c_ * nr // 2:(c_ + 1) * nr // 2, :],
              src_[c_ * nr // 2:(c_ + 1) * nr // 2, :], group="wcast")
    for c_ in range(8):
        d("gpsimd", bw2[c_ * FF // 8:(c_ + 1) * FF // 8, :],
          wm2d[c_ * FF // 8:(c_ + 1) * FF // 8, :], group="wcast")
    for pg_ in range(8):
        d("gpsimd", bwo[pg_, :, :], wod[pg_, :, :], group="wcast")
    quantize_weight(w1t, sq1, D, FF, 0)
    quantize_weight(w2t, sq2, FF, D, 2)
    o("vector", lambda v: v.tensor_scalar(mwr[:, 4:5], mwr[:, 0:1],
                                          1.0 / 127.0, None, ALU.mult),
      reads=["mw0"], writes=["mw1s"])
    o("vector", lambda v: v.tensor_scalar(mwr[:, 5:6], mwr[:, 2:3],
                                          0.5 / 127.0, None, ALU.mult),
      reads=["mw2"], writes=["mw2s"])

    def stage_ffn(seq, src, dst):
        S_.barrier()
        arB.push(); arF.push()
        w1q = [arB.alloc(FF) for _ in range(8)]
        w2q = [arB.alloc(D) for _ in range(32)]
        xqw = [[arB.alloc(TB + 2) for _ in range(8)] for _ in range(3)]
        xq2 = [arB.alloc(32 * TB) for _ in range(2)]
        q1b = arB.alloc(D)
        csr = arB.alloc(2 * S).bitcast(F32)
        f2r = arB.alloc(2 * TB).bitcast(F32)
        xt = [arF.alloc(D) for _ in range(2)]
        sl = arF.alloc(32 * TB)
        hb = [arB.alloc(2 * (TB + 2)).bitcast(F32) for _ in range(2)]
        yb = [arB.alloc(2 * TB).bitcast(F32) for _ in range(2)]
        ab = [arB.alloc(2 * TB).bitcast(F32) for _ in range(2)]
        bcc = arF.alloc(TB + 2)
        bc2 = arF.alloc(TB)
        m1b = arF.alloc(TB)
        stt = arF.alloc(40)

        o("vector", lambda v: v.memset(bcc[:], 0.0), writes=["bcc"])
        for i in range(8):
            d("sync", w1q[i][:], sq1[i * 128:(i + 1) * 128, :],
              writes=[f"w1q{i}"], group="wst")
        for i in range(32):
            d("sync", w2q[i][:], sq2[i * 128:(i + 1) * 128, :],
              writes=[f"w2q{i}"], group="wst")

        def quant_tile(tt):
            xa, kx = xt[tt % 2], f"xt{tt % 2}"
            pq = 16 * (tt % 2)
            kp = f"q{tt % 2}"
            w = tt % 3
            d("sync", xa[:], src[seq, tt * 128:(tt + 1) * 128, :],
              writes=[kx], group=f"xin{tt % 2}")
            o("vector", lambda v: v.bn_stats(stt[:, pq:pq + 6], xa[:, 0:512]),
              reads=[kx], writes=[kp + "st0"])
            o("vector", lambda v: v.bn_stats(stt[:, pq + 6:pq + 12], xa[:, 512:1024]),
              reads=[kx], writes=[kp + "st1"])
            o("vector", lambda v: v.bn_aggr(stt[:, pq + 12:pq + 14], stt[:, pq:pq + 12]),
              reads=[kp + "st0", kp + "st1"], writes=[kp + "mu"])
            o("vector", lambda v: v.tensor_scalar(
                xa[:], xa[:], stt[:, pq + 12:pq + 13], None, ALU.subtract),
              reads=[kx, kp + "mu"], writes=[kx])
            o("vector", lambda v: v.tensor_reduce(
                stt[:, pq:pq + 1], xa[:], AX.X, ALU.max, apply_absolute_value=True),
              reads=[kx], writes=[kp + "am"])
            o("vector", lambda v: v.reciprocal(stt[:, pq + 1:pq + 2], stt[:, pq:pq + 1]),
              reads=[kp + "am"], writes=[kp + "iam"])
            o("vector", lambda v: v.tensor_scalar(
                stt[:, pq + 1:pq + 2], stt[:, pq + 1:pq + 2], 127.0, None, ALU.mult),
              reads=[kp + "iam"], writes=[kp + "f"])
            o("vector", lambda v: v.tensor_scalar(
                xa[:], xa[:], stt[:, pq + 1:pq + 2], MAGIC, ALU.mult, ALU.add),
              reads=[kx, kp + "f"], writes=[kx])
            o("vector", lambda v: v.tensor_scalar(
                q1b[:], xa[:], MAGIC, None, ALU.subtract),
              reads=[kx], writes=["q1b"])
            o("vector", lambda v: v.tensor_scalar(
                stt[:, pq + 2:pq + 3], stt[:, pq + 13:pq + 14], 1e-5, None, ALU.add),
              reads=[kp + "mu"], writes=[kp + "vpe"])
            o("scalar", lambda sc: sc.activation(
                stt[:, pq + 3:pq + 4], stt[:, pq + 2:pq + 3], AF.Sqrt),
              reads=[kp + "vpe"], writes=[kp + "sig"])
            o("vector", lambda v: v.reciprocal(stt[:, pq + 4:pq + 5], stt[:, pq + 3:pq + 4]),
              reads=[kp + "sig"], writes=[kp + "rstd"])
            o("vector", lambda v: v.tensor_scalar(
                stt[:, pq + 5:pq + 6], stt[:, pq:pq + 1], stt[:, pq + 4:pq + 5],
                1e-5, ALU.mult, ALU.max),
              reads=[kp + "am", kp + "rstd"], writes=[kp + "u2"])
            o("vector", lambda v: v.tensor_tensor(
                stt[:, pq + 6:pq + 7], stt[:, pq + 5:pq + 6], mwr[:, 4:5], ALU.mult),
              reads=[kp + "u2", "mw1s"], writes=[kp + "cs"])
            o("tensor", lambda t: t.matmul(pS[0:1, 0:128], stt[:, pq + 6:pq + 7],
                                           eye_f[:], start=True, stop=True),
              reads=[kp + "cs", "eye_f"], writes=["pS0"])
            o("vector", lambda v: v.tensor_copy(
                csr[0:1, tt * 128:(tt + 1) * 128], pS[0:1, 0:128]),
              reads=["pS0"], writes=[f"csr{tt}"])
            for c in range(4):
                o("tensor", lambda t, c=c: t.matmul(
                    pS[:, 1024 + c * 128:1024 + (c + 1) * 128],
                    q1b[:, 2 * c * 128:(2 * c + 1) * 128], eye_b[:],
                    start=True, stop=True),
                  reads=["q1b", "eye_b"], writes=["pS2"], inc=(c == 3))
                o("tensor", lambda t, c=c: t.matmul(
                    pS[:, 1536 + c * 128:1536 + (c + 1) * 128],
                    q1b[:, (2 * c + 1) * 128:(2 * c + 2) * 128], eye_b[:],
                    start=True, stop=True),
                  reads=["q1b", "eye_b"], writes=["pS3"], inc=(c == 3))
            for c in range(4):
                o("vector", lambda v, c=c, w=w: v.tensor_copy(
                    xqw[w][2 * c][:, 1:129],
                    pS[:, 1024 + c * 128:1024 + (c + 1) * 128]),
                  reads=["pS2"], writes=[f"win{w}:{2 * c}"], inc=(c == 3))
                o("scalar", lambda sc, c=c, w=w: sc.copy(
                    xqw[w][2 * c + 1][:, 1:129],
                    pS[:, 1536 + c * 128:1536 + (c + 1) * 128]),
                  reads=["pS3"], writes=[f"win{w}:{2 * c + 1}"], inc=(c == 3))
            if tt == 0:
                for dc in range(8):
                    o("vector", lambda v, dc=dc: v.memset(xqw[0][dc][:, 0:1], 0.0),
                      reads=[], writes=[f"win0:{dc}"], inc=(dc == 7))
            else:
                pw = (tt - 1) % 3
                for dc in range(8):
                    o("vector", lambda v, dc=dc, w=w, pw=pw: v.tensor_copy(
                        xqw[w][dc][:, 0:1], xqw[pw][dc][:, 128:129]),
                      reads=[f"win{pw}:{dc}"], writes=[f"win{w}:{dc}"], inc=(dc == 7))
                for dc in range(8):
                    o("vector", lambda v, dc=dc, w=w, pw=pw: v.tensor_copy(
                        xqw[pw][dc][:, 129:130], xqw[w][dc][:, 1:2]),
                      reads=[f"win{w}:{dc}"], writes=[f"win{pw}:{dc}"], inc=(dc == 7))
            if tt == NT - 1:
                for dc in range(8):
                    o("vector", lambda v, dc=dc, w=w: v.memset(xqw[w][dc][:, 129:130], 0.0),
                      reads=[], writes=[f"win{w}:{dc}"], inc=(dc == 7))

        def do_mm2(tb):
            t0 = tb * TB
            xq = xq2[tb % 2]
            kcs2 = f"cs2:{tb % 2}"
            pcs2 = 32 + 4 * (tb % 2)
            xa, kx = xt[tb % 2], f"xt{tb % 2}"
            d("sync", xa[:], src[seq, t0:t0 + TB, :], writes=[kx], group=f"xin{tb % 2}")
            for dh in range(2):
                pb, kpb = (p6, "p6") if dh == 0 else (p7, "p7")
                for mc in range(32):
                    o("tensor", lambda t, mc=mc, dh=dh, pb=pb, xq=xq: t.matmul(
                        pb[:], xq[:, mc * TB:(mc + 1) * TB],
                        w2q[mc][:, dh * 512:(dh + 1) * 512],
                        start=(mc == 0), stop=(mc == 31)),
                      reads=[f"xq2{tb % 2}:{mc}", f"w2q{mc}"],
                      writes=[kpb], inc=(mc == 31))
                o("vector", lambda v, pb=pb, dh=dh, xa=xa: v.scalar_tensor_tensor(
                    xa[:, dh * 512:(dh + 1) * 512], pb[:], stt[:, pcs2:pcs2 + 1],
                    xa[:, dh * 512:(dh + 1) * 512], ALU.mult, ALU.add),
                  reads=[kpb, kcs2, kx], writes=[kx])
            d("sync", dst[seq, t0:t0 + TB, :], xa[:], reads=[kx], group="xout")

        quant_tile(0)
        quant_tile(1)
        for tb in range(NTB):
            w = tb % 3
            t0 = tb * TB
            lo, hi = max(0, t0 - 1), min(S, t0 + TB + 1)
            rt = [f"csr{i}" for i in range(max(0, tb - 1), min(NT, tb + 2))]
            o("tensor", lambda t: t.matmul(
                pS[:, 0:hi - lo], ones_f[0:1, :], csr[0:1, lo:hi],
                start=True, stop=True),
              reads=rt + ["ones_f"], writes=["pS0"])
            o("vector", lambda v: v.tensor_copy(
                bcc[:, lo - (t0 - 1):hi - (t0 - 1)], pS[:, 0:hi - lo]),
              reads=["pS0"], writes=["bcc"])
            if t0 == 0:
                o("vector", lambda v: v.memset(bcc[:, 0:1], 0.0), writes=["bcc"])
            if t0 + TB == S:
                o("vector", lambda v: v.memset(bcc[:, TB + 1:TB + 2], 0.0), writes=["bcc"])
            o("vector", lambda v: v.memset(m1b[:], 0.0), writes=["m1b"])

            winkeys = [f"win{w}:{dc}" for dc in range(8)]
            for mc in range(32):
                pb, kpb = (p4, "p4") if mc % 2 == 0 else (p5, "p5")
                for dc in range(8):
                    o("tensor", lambda t, mc=mc, dc=dc, pb=pb, w=w: t.matmul(
                        pb[:, 0:TB + 2], w1q[dc][:, mc * 128:(mc + 1) * 128],
                        xqw[w][dc][:, 0:TB + 2],
                        start=(dc == 0), stop=(dc == 7)),
                      reads=[f"w1q{dc}"] + (winkeys if dc == 0 else []),
                      writes=[kpb], inc=(dc == 7))
                hbx, khb = (hb[0], "hb0") if mc % 2 == 0 else (hb[1], "hb1")
                o("vector", lambda v, pb=pb, hbx=hbx: v.tensor_tensor(
                    hbx[:], pb[:, 0:TB + 2], bcc[:], ALU.mult),
                  reads=[kpb, "bcc"], writes=[khb])
                ybx, kyb = (yb[0], "yb0") if mc % 2 == 0 else (yb[1], "yb1")
                o("vector", lambda v, hbx=hbx, mc=mc, ybx=ybx: v.tensor_scalar(
                    ybx[:], hbx[:, 1:TB + 1], cvw[:, 3 * mc + 1:3 * mc + 2],
                    cvb[:, mc:mc + 1], ALU.mult, ALU.add),
                  reads=[khb, "cvw", "cvb"], writes=[kyb])
                o("vector", lambda v, hbx=hbx, mc=mc, ybx=ybx: v.scalar_tensor_tensor(
                    ybx[:], hbx[:, 0:TB], cvw[:, 3 * mc:3 * mc + 1],
                    ybx[:], ALU.mult, ALU.add),
                  reads=[khb, kyb, "cvw"], writes=[kyb])
                o("vector", lambda v, hbx=hbx, mc=mc, ybx=ybx: v.scalar_tensor_tensor(
                    ybx[:], hbx[:, 2:TB + 2], cvw[:, 3 * mc + 2:3 * mc + 3],
                    ybx[:], ALU.mult, ALU.add),
                  reads=[khb, kyb, "cvw"], writes=[kyb])
                o("scalar", lambda sc, mc=mc, ybx=ybx: sc.activation(
                    sl[:, mc * TB:(mc + 1) * TB], ybx[:], AF.Silu),
                  reads=[kyb], writes=[f"sl{mc}"])
                abx, kab = (ab[0], "ab0") if mc % 2 == 0 else (ab[1], "ab1")
                o("scalar", lambda sc, mc=mc, abx=abx: sc.activation(
                    abx[:], sl[:, mc * TB:(mc + 1) * TB], AF.Abs),
                  reads=[f"sl{mc}"], writes=[kab])
                if mc > 0:
                    pab, kpab = (ab[0], "ab0") if (mc - 1) % 2 == 0 else (ab[1], "ab1")
                    o("vector", lambda v, pab=pab: v.tensor_tensor(
                        m1b[:], m1b[:], pab[:], ALU.max),
                      reads=[kpab, "m1b"], writes=["m1b"])
            o("vector", lambda v: v.tensor_tensor(
                m1b[:], m1b[:], ab[1][:], ALU.max),
              reads=["ab1", "m1b"], writes=["m1b"])
            if tb >= 1:
                do_mm2(tb - 1)
            if tb + 2 < NT:
                quant_tile(tb + 2)
            pcs2 = 32 + 4 * (tb % 2)
            kcs2 = f"cs2:{tb % 2}"
            o("tensor", lambda t: t.matmul(pS[:, 512:640], m1b[:], eye_f[:],
                                           start=True, stop=True),
              reads=["m1b", "eye_f"], writes=["pS1"])
            o("vector", lambda v: v.tensor_reduce(
                stt[:, pcs2 + 1:pcs2 + 2], pS[:, 512:640], AX.X, ALU.max),
              reads=["pS1"], writes=["m2r"])
            o("vector", lambda v: v.tensor_scalar(
                stt[:, pcs2 + 1:pcs2 + 2], stt[:, pcs2 + 1:pcs2 + 2],
                1e-5, None, ALU.max),
              reads=["m2r"], writes=["m2"])
            o("vector", lambda v: v.reciprocal(
                stt[:, pcs2 + 2:pcs2 + 3], stt[:, pcs2 + 1:pcs2 + 2]),
              reads=["m2"], writes=["im2"])
            o("vector", lambda v: v.tensor_scalar(
                stt[:, pcs2 + 2:pcs2 + 3], stt[:, pcs2 + 2:pcs2 + 3],
                127.0, None, ALU.mult),
              reads=["im2"], writes=["f2"])
            o("vector", lambda v: v.tensor_tensor(
                stt[:, pcs2:pcs2 + 1], stt[:, pcs2 + 1:pcs2 + 2], mwr[:, 5:6],
                ALU.mult),
              reads=["m2", "mw2s"], writes=[kcs2])
            o("tensor", lambda t: t.matmul(
                pS[0:1, 512:640], stt[:, pcs2 + 2:pcs2 + 3], eye_f[:],
                start=True, stop=True),
              reads=["f2", "eye_f"], writes=["pS1"])
            o("vector", lambda v: v.tensor_copy(f2r[0:1, 0:TB], pS[0:1, 512:640]),
              reads=["pS1"], writes=["f2r"])
            o("tensor", lambda t: t.matmul(pS[:, 512:640], ones_f[0:1, :],
                                           f2r[0:1, 0:TB], start=True, stop=True),
              reads=["f2r", "ones_f"], writes=["pS1"])
            o("vector", lambda v: v.tensor_copy(bc2[:], pS[:, 512:640]),
              reads=["pS1"], writes=["bc2"])
            xq = xq2[tb % 2]
            for mc in range(32):
                o("vector", lambda v, mc=mc: v.tensor_tensor(
                    sl[:, mc * TB:(mc + 1) * TB], sl[:, mc * TB:(mc + 1) * TB],
                    bc2[:], ALU.mult),
                  reads=[f"sl{mc}", "bc2"], writes=[f"sl{mc}"])
                o("vector", lambda v, mc=mc, xq=xq: v.tensor_scalar(
                    xq[:, mc * TB:(mc + 1) * TB], sl[:, mc * TB:(mc + 1) * TB],
                    MAGIC, MAGIC, ALU.add, ALU.subtract),
                  reads=[f"sl{mc}"], writes=[f"xq2{tb % 2}:{mc}"])
        do_mm2(NTB - 1)
        arB.pop(); arF.pop()

    def stage_attn(seq):
        S_.barrier()
        arB.push(); arF.push()
        oall = [arB.alloc(S) for _ in range(8)]
        xnbuf = arB.alloc(D)
        whq8 = arB.alloc(2 * 8 * NBC)
        ut = [arF.alloc(D) for _ in range(2)]
        stt = arF.alloc(16)
        lrec = arF.alloc(QB)
        bcst = arF.alloc(QB)
        wtmp = arF.alloc(NBC)
        arB.push()
        xnT = [arB.alloc(S) for _ in range(8)]

        for tt in range(NT):
            xa, kx = ut[tt % 2], f"ut{tt % 2}"
            d("sync", xa[:], sx1[seq, tt * 128:(tt + 1) * 128, :],
              writes=[kx], group=f"bx{tt % 2}")
            o("vector", lambda v: v.bn_stats(stt[:, 0:6], xa[:, 0:512]),
              reads=[kx], writes=["st0"])
            o("vector", lambda v: v.bn_stats(stt[:, 6:12], xa[:, 512:1024]),
              reads=[kx], writes=["st1"])
            o("vector", lambda v: v.bn_aggr(stt[:, 12:14], stt[:, 0:12]),
              reads=["st0", "st1"], writes=["mu"])
            o("vector", lambda v: v.tensor_scalar(stt[:, 14:15], stt[:, 13:14],
                                                  1e-5, None, ALU.add),
              reads=["mu"], writes=["vpe"])
            o("scalar", lambda sc: sc.activation(stt[:, 15:16], stt[:, 14:15], AF.Sqrt),
              reads=["vpe"], writes=["sig"])
            o("vector", lambda v: v.reciprocal(stt[:, 14:15], stt[:, 15:16]),
              reads=["sig"], writes=["rstd"])
            o("vector", lambda v: v.tensor_scalar(
                xa[:], xa[:], stt[:, 12:13], stt[:, 14:15], ALU.subtract, ALU.mult),
              reads=[kx, "mu", "rstd"], writes=[kx])
            d("sync", su[seq, tt * 128:(tt + 1) * 128, :], xa[:], reads=[kx], group="su")
            if debug:
                d("sync", dbg["d_u"][seq, tt * 128:(tt + 1) * 128, :], xa[:],
                  reads=[kx], group="dbgu")
            o("vector", lambda v: v.bn_stats(stt[:, 0:6], xa[:, 0:512]),
              reads=[kx], writes=["st0"])
            o("vector", lambda v: v.bn_stats(stt[:, 6:12], xa[:, 512:1024]),
              reads=[kx], writes=["st1"])
            o("vector", lambda v: v.bn_aggr(stt[:, 12:14], stt[:, 0:12]),
              reads=["st0", "st1"], writes=["mu"])
            o("vector", lambda v: v.tensor_scalar(stt[:, 14:15], stt[:, 13:14],
                                                  1e-6, None, ALU.add),
              reads=["mu"], writes=["vpe"])
            o("scalar", lambda sc: sc.activation(stt[:, 15:16], stt[:, 14:15], AF.Sqrt),
              reads=["vpe"], writes=["sig"])
            o("vector", lambda v: v.reciprocal(stt[:, 14:15], stt[:, 15:16]),
              reads=["sig"], writes=["rstd"])
            o("vector", lambda v: v.scalar_tensor_tensor(
                xa[:], xa[:], stt[:, 12:13], g1r[:], ALU.subtract, ALU.mult),
              reads=[kx, "mu", "g1r"], writes=[kx])
            o("vector", lambda v: v.scalar_tensor_tensor(
                xnbuf[:], xa[:], stt[:, 14:15], b1r[:], ALU.mult, ALU.add),
              reads=[kx, "rstd", "b1r"], writes=["xnbuf"])
            for c in range(4):
                o("tensor", lambda t, c=c: t.matmul(
                    pS[:, 1024 + c * 128:1024 + (c + 1) * 128],
                    xnbuf[:, 2 * c * 128:(2 * c + 1) * 128], eye_b[:],
                    start=True, stop=True),
                  reads=["xnbuf", "eye_b"], writes=["pS2"], inc=(c == 3))
                o("tensor", lambda t, c=c: t.matmul(
                    pS[:, 1536 + c * 128:1536 + (c + 1) * 128],
                    xnbuf[:, (2 * c + 1) * 128:(2 * c + 2) * 128], eye_b[:],
                    start=True, stop=True),
                  reads=["xnbuf", "eye_b"], writes=["pS3"], inc=(c == 3))
            for c in range(4):
                o("vector", lambda v, c=c, tt=tt: v.tensor_copy(
                    xnT[2 * c][:, tt * 128:(tt + 1) * 128],
                    pS[:, 1024 + c * 128:1024 + (c + 1) * 128]),
                  reads=["pS2"], writes=[f"xnT{2 * c}:{tt}"], inc=(c == 3))
                o("scalar", lambda sc, c=c, tt=tt: sc.copy(
                    xnT[2 * c + 1][:, tt * 128:(tt + 1) * 128],
                    pS[:, 1536 + c * 128:1536 + (c + 1) * 128]),
                  reads=["pS3"], writes=[f"xnT{2 * c + 1}:{tt}"], inc=(c == 3))
        xnall = [f"xnT{i}:{j}" for i in range(8) for j in range(NT)]

        for qki, wdr in ((0, whqd), (1, whkd)):
            for p in range(8):
                for par in range(2):
                    h_ = 2 * p + par
                    d("sync", wtmp[par * 64:(par + 1) * 64, :],
                      wdr[h_ * DH:(h_ + 1) * DH, :], writes=[f"wtmp{par}"], group="whq")
                o("vector", lambda v, p=p, qki=qki: v.tensor_scalar(
                    whq8[:, (qki * 8 + p) * NBC:(qki * 8 + p + 1) * NBC],
                    wtmp[:, :], 8.0 if qki == 0 else 1.0, None, ALU.mult),
                  reads=["wtmp0", "wtmp1"], writes=[f"whq8:{qki}:{p}"])

        for half in range(2):
            arB.push(); arF.push()
            qsb = [arB.alloc(S) for _ in range(4)]
            ksb = [arB.alloc(S) for _ in range(4)]
            qbb = [arB.alloc(S) for _ in range(4)]
            kbb = [arB.alloc(S) for _ in range(4)]
            vsb = [arB.alloc(NKC * 65 + 1) for _ in range(8)]
            qbn = arB.alloc(128)
            eqf = arF.alloc(128)
            pair0 = half * 4

            for hh in range(8):
                o("vector", lambda v, hh=hh: v.memset(
                    vsb[hh][:, 0:NKC * 65].rearrange("p (a b) -> p a b", b=65)[:, :, 64:65], 1.0),
                  writes=[f"vones{hh}"])

            arB.push()
            wpr = [arB.alloc(8 * 128) for _ in range(2)]
            wvh = arB.alloc(8 * 512)
            for which, wdr, dstl, scl in (("q", wqd, qsb, 0.125), ("k", wkd, ksb, 1.0)):
                wb = wpr[0] if which == "q" else wpr[1]
                for pi in range(4):
                    p = pair0 + pi
                    kwb = f"wpr{which}"
                    bdr = bwq if which == "q" else bwk
                    d("sync", wb[:].rearrange("p (a m) -> p a m", a=8),
                      bdr[:, p * 128:(p + 1) * 128].rearrange(
                          "(a p) m -> p a m", p=128),
                      writes=[f"{kwb}:{dc2}" for dc2 in range(8)], group=f"w{which}")
                    for nb_ in range(S // W512):
                        pb, kpb = (p4, "p4") if nb_ % 2 == 0 else (p5, "p5")
                        for dc in range(8):
                            o("tensor", lambda t, dc=dc, nb_=nb_, wb=wb, pb=pb: t.matmul(
                                pb[:, 0:W512], wb[:, dc * 128:(dc + 1) * 128],
                                xnT[dc][:, nb_ * W512:(nb_ + 1) * W512],
                                start=(dc == 0), stop=(dc == 7)),
                              reads=[f"{kwb}:{dc}"] + (xnall if dc == 0 else []),
                              writes=[kpb], inc=(dc == 7))
                        o("vector", lambda v, pb=pb, nb_=nb_, pi=pi, dstl=dstl, scl=scl: v.tensor_scalar(
                            dstl[pi][:, nb_ * W512:(nb_ + 1) * W512], pb[:, 0:W512],
                            scl, None, ALU.mult),
                          reads=[kpb], writes=[f"{which}sb{pi}:{nb_}"])
            d("sync", wvh[:].rearrange("p (a m) -> p a m", a=8),
              bwv[:, half * 512:(half + 1) * 512].rearrange(
                  "(a p) m -> p a m", p=128),
              writes=[f"wvh:{dc2}" for dc2 in range(8)], group="wv")
            for tt in range(NT):
                pb, kpb = (p6, "p6") if tt % 2 == 0 else (p7, "p7")
                for dc in range(8):
                    o("tensor", lambda t, dc=dc, tt=tt, pb=pb: t.matmul(
                        pb[:], xnT[dc][:, tt * 128:(tt + 1) * 128],
                        wvh[:, dc * 512:(dc + 1) * 512],
                        start=(dc == 0), stop=(dc == 7)),
                      reads=[f"wvh:{dc}"] + (xnall if dc == 0 else []),
                      writes=[kpb], inc=(dc == 7))
                for hh in range(8):
                    o("vector", lambda v, hh=hh, tt=tt, pb=pb: v.tensor_copy(
                        vsb[hh][:, tt * 65:tt * 65 + 64], pb[:, hh * 64:(hh + 1) * 64]),
                      reads=[kpb], writes=[f"v{hh}:{tt}"], inc=(hh == 7))
            arB.pop()
            ptb = [arB.alloc(2 * QB) for _ in range(2)]

            for qki, srcl, dstl, scl in ((0, qsb, qbb, 0.1), (1, ksb, kbb, 1.0)):
                sk = "q" if qki == 0 else "k"
                dk = "qb" if qki == 0 else "kb"
                for gg in range(2):
                    for tt in range(NT):
                        for j in range(4):
                            pi = 2 * gg + j // 2
                            par = j % 2
                            pb, kpb = (p6, "p6") if par == 0 else (p7, "p7")
                            o("tensor", lambda t, pi=pi, par=par, j=j, tt=tt, qki=qki, pb=pb, srcl=srcl: t.matmul(
                                pb[0:128, (j // 2) * 32:(j // 2) * 32 + 32],
                                srcl[pi][par * 64:(par + 1) * 64, tt * 128:(tt + 1) * 128],
                                whq8[par * 64:(par + 1) * 64,
                                     (qki * 8 + pair0 + pi) * NBC:(qki * 8 + pair0 + pi + 1) * NBC],
                                start=(j // 2 == 0), stop=(j // 2 == 1),
                                tile_position=(par * 64, 0)),
                              reads=[f"{sk}sb{pi}:{tt * 128 // W512}",
                                     f"whq8:{qki}:{pair0 + pi}"],
                              writes=[kpb], inc=(j // 2 == 1))
                        o("scalar", lambda sc: sc.activation(eqf[:, 0:64], p6[:, 0:64], AF.Exp),
                          reads=["p6"], writes=["eq0"])
                        o("scalar", lambda sc: sc.activation(eqf[:, 64:128], p7[:, 0:64], AF.Exp),
                          reads=["p7"], writes=["eq1"])
                        o("vector", lambda v: v.tensor_reduce(
                            stt[:, 0:4], eqf[:, 0:128].rearrange("p (a b) -> p a b", b=32),
                            AX.X, ALU.add),
                          reads=["eq0", "eq1"], writes=["eqs"])
                        o("vector", lambda v: v.reciprocal(stt[:, 4:8], stt[:, 0:4]),
                          reads=["eqs"], writes=["eqr"])
                        for ci, j in ((0, 0), (1, 2), (2, 1), (3, 3)):
                            o("vector", lambda v, ci=ci, scl=scl: v.tensor_scalar(
                                qbn[:, ci * 32:(ci + 1) * 32], eqf[:, ci * 32:(ci + 1) * 32],
                                stt[:, 4 + ci:5 + ci], scl, ALU.mult, ALU.mult),
                              reads=["eqr", "eq0", "eq1"], writes=["qbn"], inc=(ci == 3))
                        first = True
                        for ci, j in ((0, 0), (1, 2), (2, 1), (3, 3)):
                            par = j % 2
                            o("tensor", lambda t, ci=ci, par=par, first=first: t.matmul(
                                pS[par * 64:par * 64 + 32,
                                   512 + ci * 128:512 + (ci + 1) * 128],
                                qbn[:, ci * 32:(ci + 1) * 32], eye_b[:],
                                start=(ci in (0, 2)), stop=(ci in (1, 3)),
                                tile_position=(0, par * 64)),
                              reads=["qbn", "eye_b"], writes=["pS1"], inc=(ci == 3))
                            first = False
                        for ci, j in ((0, 0), (1, 2), (2, 1), (3, 3)):
                            par = j % 2
                            pi = 2 * gg + j // 2
                            o("vector", lambda v, ci=ci, par=par, pi=pi, tt=tt, dstl=dstl: v.tensor_copy(
                                dstl[pi][par * 64:par * 64 + 32, tt * 128:(tt + 1) * 128],
                                pS[par * 64:par * 64 + 32, 512 + ci * 128:512 + (ci + 1) * 128]),
                              reads=["pS1"], writes=[f"{dk}{pi}:{tt}"], inc=(ci == 3))

            for pi in range(4):
                for qb_ in range(NQB):
                    q0 = qb_ * QB
                    nb_ = q0 // W512
                    obpar = (pi * NQB + qb_) % 2
                    obE, okE = (p4, "p4") if obpar == 0 else (p6, "p6")
                    obO, okO = (p5, "p5") if obpar == 0 else (p7, "p7")
                    hhE, hhO = 2 * pi, 2 * pi + 1
                    pg = pair0 + pi
                    for kc in range(NKC):
                        par = kc % 2
                        sE = pS[:, par * 1024:par * 1024 + QB]
                        sO = pS[:, par * 1024 + 512:par * 1024 + 512 + QB]
                        kE, kO = f"pS{2 * par}", f"pS{2 * par + 1}"
                        o("tensor", lambda t, kc=kc, q0=q0, sE=sE: t.matmul(
                            sE, ksb[pi][0:64, kc * 128:(kc + 1) * 128],
                            qsb[pi][0:64, q0:q0 + QB],
                            start=True, stop=False, tile_position=(0, 0)),
                          reads=[f"qsb{pi}:{nb_}", f"ksb{pi}:{kc * 128 // W512}"],
                          writes=[kE], inc=False)
                        o("tensor", lambda t, kc=kc, q0=q0, sO=sO: t.matmul(
                            sO, ksb[pi][64:128, kc * 128:(kc + 1) * 128],
                            qsb[pi][64:128, q0:q0 + QB],
                            start=True, stop=False, tile_position=(64, 0)),
                          reads=[f"qsb{pi}:{nb_}", f"ksb{pi}:{kc * 128 // W512}"],
                          writes=[kO], inc=False)
                        o("tensor", lambda t, kc=kc, q0=q0, sE=sE: t.matmul(
                            sE, kbb[pi][0:32, kc * 128:(kc + 1) * 128],
                            qbb[pi][0:32, q0:q0 + QB],
                            start=False, stop=True, tile_position=(0, 0)),
                          reads=[f"qb{pi}:{q0 // 128 + z}" for z in range(QB // 128)] + [f"kb{pi}:{kc}"],
                          writes=[kE], inc=True)
                        o("tensor", lambda t, kc=kc, q0=q0, sO=sO: t.matmul(
                            sO, kbb[pi][64:96, kc * 128:(kc + 1) * 128],
                            qbb[pi][64:96, q0:q0 + QB],
                            start=False, stop=True, tile_position=(64, 0)),
                          reads=[f"qb{pi}:{q0 // 128 + z}" for z in range(QB // 128)] + [f"kb{pi}:{kc}"],
                          writes=[kO], inc=True)
                        kpt = f"ptb{kc % 2}"
                        o("scalar", lambda sc, kc=kc, par=par: sc.activation(
                            ptb[kc % 2][:].rearrange("p (a b) -> p a b", b=QB),
                            pS[:, par * 1024:(par + 1) * 1024].rearrange(
                                "p (a b) -> p a b", b=512)[:, :, 0:QB],
                            AF.Exp, bias=nb20[:, 0:1]),
                          reads=[kE, kO], writes=[kpt])
                        o("tensor", lambda t, kc=kc, hhE=hhE, obE=obE: t.matmul(
                            obE[0:65, 0:QB], vsb[hhE][:, kc * 65:kc * 65 + 65],
                            ptb[kc % 2][:, 0:QB],
                            start=(kc == 0), stop=(kc == NKC - 1)),
                          reads=[kpt, f"v{hhE}:{kc}", f"vones{hhE}"],
                          writes=[okE], inc=True)
                        o("tensor", lambda t, kc=kc, hhO=hhO, obO=obO: t.matmul(
                            obO[64:128, 0:QB], vsb[hhO][:, kc * 65:kc * 65 + 64],
                            ptb[kc % 2][:, QB:2 * QB],
                            start=(kc == 0), stop=(kc == NKC - 1),
                            tile_position=(0, 64)),
                          reads=[kpt, f"v{hhO}:{kc}"], writes=[okO], inc=False)
                        o("tensor", lambda t, kc=kc, hhO=hhO, obO=obO: t.matmul(
                            obO[32:33, 0:QB], vsb[hhO][:, kc * 65 + 64:kc * 65 + 65],
                            ptb[kc % 2][:, QB:2 * QB],
                            start=(kc == 0), stop=(kc == NKC - 1),
                            tile_position=(0, 32)),
                          reads=[kpt, f"vones{hhO}"], writes=[okO], inc=True)
                    o("vector", lambda v, obE=obE: v.reciprocal(
                        lrec[64:65, 0:QB], obE[64:65, 0:QB]),
                      reads=[okE], writes=["lrec"])
                    o("tensor", lambda t: t.matmul(
                        pS[0:64, 0:QB], ones_f[64:65, 0:64], lrec[64:65, 0:QB],
                        start=True, stop=True, tile_position=(64, 0)),
                      reads=["lrec", "ones_f"], writes=["pS0"])
                    o("scalar", lambda sc: sc.copy(bcst[0:64, 0:QB], pS[0:64, 0:QB]),
                      reads=["pS0"], writes=["bcst0"])
                    o("vector", lambda v, obE=obE, pg=pg, q0=q0: v.tensor_tensor(
                        oall[pg][0:64, q0:q0 + QB], obE[0:64, 0:QB],
                        bcst[0:64, 0:QB], ALU.mult),
                      reads=[okE, "bcst0"], writes=[f"oall{pg}:{q0}"])
                    o("vector", lambda v, obO=obO: v.reciprocal(
                        lrec[32:33, 0:QB], obO[32:33, 0:QB]),
                      reads=[okO], writes=["lrec2"])
                    o("tensor", lambda t: t.matmul(
                        pS[64:128, 0:QB], ones_f[32:33, 0:64], lrec[32:33, 0:QB],
                        start=True, stop=True, tile_position=(32, 64)),
                      reads=["lrec2", "ones_f"], writes=["pS0"])
                    o("scalar", lambda sc: sc.copy(bcst[64:128, 0:QB], pS[64:128, 0:QB]),
                      reads=["pS0"], writes=["bcst1"])
                    o("vector", lambda v, obO=obO, pg=pg, q0=q0: v.tensor_tensor(
                        oall[pg][64:128, q0:q0 + QB], obO[64:128, 0:QB],
                        bcst[64:128, 0:QB], ALU.mult),
                      reads=[okO, "bcst1"], writes=[f"oall{pg}:{q0}"])
            S_.barrier()
            arB.pop(); arF.pop()
        arB.pop()

        arB.push(); arF.push()
        wob = arB.alloc(8 * D)
        w1b = [arB.alloc(8 * 128) for _ in range(4)]
        w2b = [arB.alloc(D) for _ in range(4)]
        h1 = arB.alloc(32 * MLPB)
        xmT = [arB.alloc(MLPB) for _ in range(8)]
        xmb = arB.alloc(D)
        x3b = arB.alloc(2 * (MLPB // 128) * D).bitcast(F32)
        x1t = arF.alloc(D)
        for pg in range(8):
            d("sync", wob[:, pg * D:(pg + 1) * D], bwo[pg, :, :],
              writes=[f"wob{pg}"], group="wo")
        oallk = [f"oall{pg}:{q0}" for pg in range(8) for q0 in range(0, S, QB)]

        for mb in range(NMB):
            tiles = [mb * (MLPB // 128) + c for c in range(MLPB // 128)]
            for ci, tt in enumerate(tiles):
                for dh in range(2):
                    pb, kpb = (p4, "p4") if dh == 0 else (p5, "p5")
                    for pg in range(8):
                        o("tensor", lambda t, pg=pg, dh=dh, tt=tt, pb=pb: t.matmul(
                            pb[:], oall[pg][:, tt * 128:(tt + 1) * 128],
                            wob[:, pg * D + dh * 512:pg * D + (dh + 1) * 512],
                            start=(pg == 0), stop=(pg == 7)),
                          reads=[f"wob{pg}"] + (oallk if pg == 0 else []),
                          writes=[kpb], inc=(pg == 7))
                kx, xu = f"ut{tt % 2}", ut[tt % 2]
                d("sync", xu[:], su[seq, tt * 128:(tt + 1) * 128, :],
                  writes=[kx], group=f"bx{tt % 2}")
                for dh in range(2):
                    pb, kpb = (p4, "p4") if dh == 0 else (p5, "p5")
                    o("vector", lambda v, pb=pb, dh=dh, ci=ci, xu=xu: v.tensor_tensor(
                        x3b[:, ci * D + dh * 512:ci * D + (dh + 1) * 512],
                        pb[:], xu[:, dh * 512:(dh + 1) * 512], ALU.add),
                      reads=[kpb, kx], writes=[f"x3:{ci}"])
                kx3 = f"x3:{ci}"
                o("vector", lambda v, ci=ci: v.bn_stats(
                    stt[:, 0:6], x3b[:, ci * D:ci * D + 512]),
                  reads=[kx3], writes=["st0"])
                o("vector", lambda v, ci=ci: v.bn_stats(
                    stt[:, 6:12], x3b[:, ci * D + 512:ci * D + 1024]),
                  reads=[kx3], writes=["st1"])
                o("vector", lambda v: v.bn_aggr(stt[:, 12:14], stt[:, 0:12]),
                  reads=["st0", "st1"], writes=["mu"])
                o("vector", lambda v: v.tensor_scalar(stt[:, 14:15], stt[:, 13:14],
                                                      1e-6, None, ALU.add),
                  reads=["mu"], writes=["vpe"])
                o("scalar", lambda sc: sc.activation(stt[:, 15:16], stt[:, 14:15], AF.Sqrt),
                  reads=["vpe"], writes=["sig"])
                o("vector", lambda v: v.reciprocal(stt[:, 14:15], stt[:, 15:16]),
                  reads=["sig"], writes=["rstd"])
                o("vector", lambda v, ci=ci: v.scalar_tensor_tensor(
                    xmb[:], x3b[:, ci * D:(ci + 1) * D], stt[:, 12:13], g2r[:],
                    ALU.subtract, ALU.mult),
                  reads=[kx3, "mu", "g2r"], writes=["xmb"])
                o("vector", lambda v: v.scalar_tensor_tensor(
                    xmb[:], xmb[:], stt[:, 14:15], b2lr[:], ALU.mult, ALU.add),
                  reads=["xmb", "rstd", "b2lr"], writes=["xmb"])
                for c in range(4):
                    o("tensor", lambda t, c=c: t.matmul(
                        pS[:, 1024 + c * 128:1024 + (c + 1) * 128],
                        xmb[:, 2 * c * 128:(2 * c + 1) * 128], eye_b[:],
                        start=True, stop=True),
                      reads=["xmb", "eye_b"], writes=["pS2"], inc=(c == 3))
                    o("tensor", lambda t, c=c: t.matmul(
                        pS[:, 1536 + c * 128:1536 + (c + 1) * 128],
                        xmb[:, (2 * c + 1) * 128:(2 * c + 2) * 128], eye_b[:],
                        start=True, stop=True),
                      reads=["xmb", "eye_b"], writes=["pS3"], inc=(c == 3))
                for c in range(4):
                    o("vector", lambda v, c=c, ci=ci: v.tensor_copy(
                        xmT[2 * c][:, ci * 128:(ci + 1) * 128],
                        pS[:, 1024 + c * 128:1024 + (c + 1) * 128]),
                      reads=["pS2"], writes=[f"xmT{2 * c}:{ci}"], inc=(c == 3))
                    o("scalar", lambda sc, c=c, ci=ci: sc.copy(
                        xmT[2 * c + 1][:, ci * 128:(ci + 1) * 128],
                        pS[:, 1536 + c * 128:1536 + (c + 1) * 128]),
                      reads=["pS3"], writes=[f"xmT{2 * c + 1}:{ci}"], inc=(c == 3))
            xmk = [f"xmT{i}:{j}" for i in range(8) for j in range(MLPB // 128)]
            for mc in range(32):
                wb, kwb = w1b[mc % 4], f"w1b{mc % 4}"
                d("sync", wb[:].rearrange("p (a m) -> p a m", a=8),
                  bw1[:, mc * 128:(mc + 1) * 128].rearrange(
                      "(a p) m -> p a m", p=128),
                  writes=[f"{kwb}:{dc2}" for dc2 in range(8)], group=f"wm1{mc % 4}")
                pb, kpb = (p6, "p6") if mc % 2 == 0 else (p7, "p7")
                for dc in range(8):
                    o("tensor", lambda t, dc=dc, mc=mc, wb=wb, pb=pb: t.matmul(
                        pb[:, 0:MLPB], wb[:, dc * 128:(dc + 1) * 128],
                        xmT[dc][:, 0:MLPB], start=(dc == 0), stop=(dc == 7)),
                      reads=[f"{kwb}:{dc}"] + (xmk if dc == 0 else []),
                      writes=[kpb], inc=(dc == 7))
                o("vector", lambda v, mc=mc, pb=pb: v.tensor_scalar(
                    h1[:, mc * MLPB:(mc + 1) * MLPB], pb[:, 0:MLPB],
                    b1cc[:, mc:mc + 1], 0.0, ALU.add, ALU.max),
                  reads=[kpb, "b1cc"], writes=[f"h1:{mc}"])
            ypb = [pS[:, 0:512], pS[:, 512:1024], pS[:, 1024:1536],
                   pS[:, 1536:2048], p4[:], p5[:], p6[:], p7[:]]
            ypk = ["pS0", "pS1", "pS2", "pS3", "p4", "p5", "p6", "p7"]
            for mc in range(32):
                wb, kwb = w2b[mc % 4], f"w2b{mc % 4}"
                d("sync", wb[:], bw2[mc * 128:(mc + 1) * 128, :],
                  writes=[kwb], group=f"wm2{mc % 4}")
                for ci in range(MLPB // 128):
                    for dh in range(2):
                        bi = ci * 2 + dh
                        o("tensor", lambda t, mc=mc, ci=ci, dh=dh, wb=wb, bi=bi: t.matmul(
                            ypb[bi], h1[:, mc * MLPB + ci * 128:mc * MLPB + (ci + 1) * 128],
                            wb[:, dh * 512:(dh + 1) * 512],
                            start=(mc == 0), stop=(mc == 31)),
                          reads=[f"h1:{mc}", kwb],
                          writes=[ypk[bi]],
                          inc=(ci == MLPB // 128 - 1 and dh == 1))
            for ci, tt in enumerate(tiles):
                d("sync", x1t[:], sx1[seq, tt * 128:(tt + 1) * 128, :],
                  writes=["x1t"], group="bx1r")
                for dh in range(2):
                    bi = ci * 2 + dh
                    o("vector", lambda v, ci=ci, dh=dh, bi=bi: v.tensor_tensor(
                        x3b[:, ci * D + dh * 512:ci * D + (dh + 1) * 512],
                        x3b[:, ci * D + dh * 512:ci * D + (dh + 1) * 512],
                        ypb[bi], ALU.add),
                      reads=[ypk[bi], f"x3:{ci}"], writes=[f"x3:{ci}"])
                    o("vector", lambda v, ci=ci, dh=dh: v.tensor_tensor(
                        x3b[:, ci * D + dh * 512:ci * D + (dh + 1) * 512],
                        x3b[:, ci * D + dh * 512:ci * D + (dh + 1) * 512],
                        b2rr[:, dh * 512:(dh + 1) * 512], ALU.add),
                      reads=[f"x3:{ci}", "b2rr"], writes=[f"x3:{ci}"])
                    o("vector", lambda v, ci=ci, dh=dh: v.tensor_tensor(
                        x3b[:, ci * D + dh * 512:ci * D + (dh + 1) * 512],
                        x3b[:, ci * D + dh * 512:ci * D + (dh + 1) * 512],
                        x1t[:, dh * 512:(dh + 1) * 512], ALU.add),
                      reads=[f"x3:{ci}", "x1t"], writes=[f"x3:{ci}"])
                d("sync", sx4[seq, tt * 128:(tt + 1) * 128, :],
                  x3b[:, ci * D:(ci + 1) * D], reads=[f"x3:{ci}"], group="x4out")
                if debug:
                    d("sync", dbg["d_x4"][seq, tt * 128:(tt + 1) * 128, :],
                      x3b[:, ci * D:(ci + 1) * D], reads=[f"x3:{ci}"], group="dbg4")
        arB.pop(); arF.pop()
        arB.pop(); arF.pop()

    for seq in range(NSEQ):
        stage_ffn(seq, xin, sx1)
        if debug:
            S_.barrier()
            arF.push()
            tmp = arF.alloc(D)
            for tt in range(NT):
                dt_ = d("sync", tmp[:], sx1[seq, tt * 128:(tt + 1) * 128, :],
                        writes=["dtmp"], group="dbg1")
                d("sync", dbg["d_x1"][seq, tt * 128:(tt + 1) * 128, :], tmp[:],
                  reads=["dtmp"], group="dbg1b")
            arF.pop()
        stage_attn(seq)
        stage_ffn(seq, sx4, out)
    S_.emit("sync")
    return nc, st, dbg


def _host_inputs(inputs, S=2048):
    f32 = np.float32
    i = inputs
    shared = {
        "w1t": np.ascontiguousarray(i["L0_ffn_w1"].T.astype(f32)),
        "w2t": np.ascontiguousarray(i["L0_ffn_w2"].T.astype(f32)),
        "convw": np.ascontiguousarray(
            i["L0_ffn_conv_w"].reshape(32, 128, 3).transpose(1, 0, 2).reshape(128, 96).astype(f32)),
        "convb": np.ascontiguousarray(
            i["L0_ffn_conv_b"].reshape(32, 128).T.astype(f32)),
        "g1d": i["L0_attn_ln1_g"].reshape(1, D).astype(f32),
        "b1d": i["L0_attn_ln1_b"].reshape(1, D).astype(f32),
        "g2d": i["L0_attn_ln2_g"].reshape(1, D).astype(f32),
        "b2d": i["L0_attn_ln2_b"].reshape(1, D).astype(f32),
        "wqd": np.ascontiguousarray(i["L0_attn_Wq"].reshape(D, H * DH).astype(f32)),
        "wkd": np.ascontiguousarray(i["L0_attn_Wk"].reshape(D, H * DH).astype(f32)),
        "wvd": np.ascontiguousarray(i["L0_attn_Wv"].reshape(D, H * DH).astype(f32)),
        "whqd": np.ascontiguousarray(i["L0_attn_Whq"].reshape(H * DH, NBC).astype(f32)),
        "whkd": np.ascontiguousarray(i["L0_attn_Whk"].reshape(H * DH, NBC).astype(f32)),
        "wod": np.ascontiguousarray(
            i["L0_attn_Wo"].reshape(H // 2, 2 * DH, D).astype(f32)),
        "wm1d": np.ascontiguousarray(i["L0_attn_W1"].astype(f32)),
        "wm2d": np.ascontiguousarray(i["L0_attn_W2"].astype(f32)),
        "b1cd": np.ascontiguousarray(i["L0_attn_b1"].reshape(32, 128).T.astype(f32)),
        "b2rd": i["L0_attn_b2"].reshape(1, D).astype(f32),
        "eyebd": np.eye(128, dtype=f32).astype(bfdt),
        "eyefd": np.eye(128, dtype=f32),
        "onesfd": np.ones((128, 128), dtype=f32),
    }
    x = np.ascontiguousarray(i["L0_attn_inputs"].astype(f32))
    in_maps = []
    for c in range(NCORES):
        m = dict(shared)
        m["xin"] = np.ascontiguousarray(x[c * NSEQ:(c + 1) * NSEQ, :S, :])
        in_maps.append(m)
    return in_maps


_CACHE = {}


def kernel(**inputs):
    if "prog" not in _CACHE:
        nc, st, _ = build_program(S=2048, debug=False)
        _CACHE["prog"] = (nc, st)
    nc, st = _CACHE["prog"]
    in_maps = _host_inputs(inputs, S=2048)
    res = run_bass_kernel_spmd(nc, in_maps, core_ids=list(range(NCORES)))
    outp = np.concatenate([res.results[c]["out"] for c in range(NCORES)], axis=0)
    return outp.astype(np.float32)


# version 14
# speedup vs baseline: 1.9636x; 1.1029x over previous
import numpy as np
import ml_dtypes
from contextlib import ExitStack
from typing import NamedTuple

import concourse.bass as bass
import concourse.mybir as mybir
from concourse.bass_utils import run_bass_kernel_spmd

F32 = mybir.dt.float32
BF16 = mybir.dt.bfloat16
ALU = mybir.AluOpType
AF = mybir.ActivationFunctionType
AX = mybir.AxisListType
bfdt = ml_dtypes.bfloat16

MAGIC = 12582912.0
EXPB = -20.0
D, H, DH, NBC, FF = 1024, 16, 64, 32, 4096
NSEQ, NCORES = 2, 8
ENGINES = ("sync", "tensor", "vector", "scalar", "gpsimd")


class Tok(NamedTuple):
    key: str
    val: int


class _Rec:

    def __getattr__(self, name):
        def call(*a, **k):
            return (name, a, k)
        return call


class Sched:

    def __init__(self, nc, stack):
        self.nc = nc
        self.stack = stack
        self.sem = {e: stack.enter_context(nc.semaphore("S_" + e)) for e in ENGINES}
        self.cnt = {e: 0 for e in ENGINES}
        self.opl = {e: [] for e in ENGINES}
        self.water = {}
        self.dma_groups = {}
        self.pend_bar = {}
        self.wmap = {}
        self.rmap = {}
        self.pend_rw = {e: [] for e in ENGINES}

    def group(self, name):
        if name not in self.dma_groups:
            self.dma_groups[name] = self.stack.enter_context(
                self.nc.semaphore("G_" + name))
            self.cnt[name] = 0
        return name

    def barrier(self):
        toks = [Tok(k, v) for k, v in self.cnt.items() if v > 0]
        for e in ENGINES:
            self.pend_bar[e] = toks
        self.wmap = {}
        self.rmap = {}

    def _rw_deps(self, reads, writes):
        deps = []
        for k in reads:
            deps += self.wmap.get(k, [])
        for k in writes:
            deps += self.wmap.get(k, [])
            deps += self.rmap.get(k, [])
        return deps

    def _rw_commit(self, tok, reads, writes):
        for k in writes:
            self.wmap[k] = [tok]
            self.rmap[k] = []
        for k in reads:
            self.rmap.setdefault(k, []).append(tok)

    def _resolve(self, eng, deps):
        byk = {}
        for t in deps:
            if t is None:
                continue
            v = t.val
            if t.key in self.dma_groups:
                v = self.cnt[t.key]
            if byk.get(t.key, 0) < v:
                byk[t.key] = v
        waits = []
        for k, v in sorted(byk.items()):
            wk = (eng, k)
            if self.water.get(wk, 0) >= v:
                continue
            self.water[wk] = v
            waits.append((k, v))
        return waits

    def op(self, eng, fn, reads=(), writes=(), deps=(), inc=True):
        deps = list(deps) + self._rw_deps(reads, writes)
        if self.pend_bar.get(eng):
            deps += self.pend_bar[eng]
            self.pend_bar[eng] = None
        waits = self._resolve(eng, deps)
        tok, inckey = None, None
        if inc:
            self.cnt[eng] += 1
            tok = Tok(eng, self.cnt[eng])
            inckey = eng
        self.opl[eng].append((fn(_Rec()), waits, inckey, 1))
        if tok is not None:
            for (r2, w2) in self.pend_rw[eng]:
                self._rw_commit(tok, r2, w2)
            self.pend_rw[eng] = []
            self._rw_commit(tok, reads, writes)
        else:
            self.pend_rw[eng].append((list(reads), list(writes)))
        return tok

    def dma(self, eng, out, in_, reads=(), writes=(), deps=(), group="misc", **kw):
        g = self.group(group)
        deps = list(deps) + self._rw_deps(reads, writes)
        if self.pend_bar.get(eng):
            deps += self.pend_bar[eng]
            self.pend_bar[eng] = None
        waits = self._resolve(eng, deps)
        self.cnt[g] += 16
        tok = Tok(g, self.cnt[g])
        self.opl[eng].append((("dma_start", (out, in_), kw), waits, g, 16))
        self._rw_commit(tok, reads, writes)
        return tok

    def emit(self, final_eng="sync"):
        finals = [Tok(g, self.cnt[g]) for g in self.dma_groups if self.cnt[g] > 0]
        self.opl[final_eng].append((None, self._resolve(final_eng, finals), None, 0))
        with self.nc.Block() as block:
            for e in ENGINES:
                if not self.opl[e]:
                    continue
                def body(E, _ops=tuple(self.opl[e])):
                    for spec, waits, inckey, amt in _ops:
                        for (k, v) in waits:
                            E.wait_ge(self._sem_of(k), v)
                        if spec is not None:
                            name, a, kw = spec
                            inst = getattr(E, name)(*a, **kw)
                            if inckey is not None:
                                inst.then_inc(self._sem_of(inckey), amt)
                getattr(block, e)(body)

    def _sem_of(self, key):
        return self.sem[key] if key in self.sem else self.dma_groups[key]


class Arena:
    def __init__(self, tensor, cols):
        self.t, self.cols, self.off, self.marks = tensor, cols, 0, []

    def alloc(self, n):
        n = (n + 1) & ~1
        assert self.off + n <= self.cols, f"arena overflow {self.off}+{n}>{self.cols}"
        ap = self.t[:, self.off:self.off + n]
        self.off += n
        return ap

    def push(self):
        self.marks.append(self.off)

    def pop(self):
        self.off = self.marks.pop()


def build_program(S=2048, debug=False):
    TB = 128
    MLPB = min(512, S)
    NT = S // 128
    NKC = S // 128
    QB = min(512, S)
    NQB = S // QB
    NTB = S // TB
    NMB = S // MLPB
    W512 = min(512, S)

    nc = bass.Bass("TRN2", target_bir_lowering=False)
    st = ExitStack()
    S_ = Sched(nc, st)
    o, d = S_.op, S_.dma

    xin = nc.dram_tensor("xin", [NSEQ, S, D], F32, kind="ExternalInput")
    out = nc.dram_tensor("out", [NSEQ, S, D], F32, kind="ExternalOutput")
    w1t = nc.dram_tensor("w1t", [D, FF], F32, kind="ExternalInput")
    w2t = nc.dram_tensor("w2t", [FF, D], F32, kind="ExternalInput")
    convw = nc.dram_tensor("convw", [128, 3 * (FF // 128)], F32, kind="ExternalInput")
    convb = nc.dram_tensor("convb", [128, FF // 128], F32, kind="ExternalInput")
    g1d = nc.dram_tensor("g1d", [1, D], F32, kind="ExternalInput")
    b1d = nc.dram_tensor("b1d", [1, D], F32, kind="ExternalInput")
    g2d = nc.dram_tensor("g2d", [1, D], F32, kind="ExternalInput")
    b2d = nc.dram_tensor("b2d", [1, D], F32, kind="ExternalInput")
    wqd = nc.dram_tensor("wqd", [D, H * DH], F32, kind="ExternalInput")
    wkd = nc.dram_tensor("wkd", [D, H * DH], F32, kind="ExternalInput")
    wvd = nc.dram_tensor("wvd", [D, H * DH], F32, kind="ExternalInput")
    whqd = nc.dram_tensor("whqd", [H * DH, NBC], F32, kind="ExternalInput")
    whkd = nc.dram_tensor("whkd", [H * DH, NBC], F32, kind="ExternalInput")
    wod = nc.dram_tensor("wod", [H // 2, 2 * DH, D], F32, kind="ExternalInput")
    wm1d = nc.dram_tensor("wm1d", [D, FF], F32, kind="ExternalInput")
    wm2d = nc.dram_tensor("wm2d", [FF, D], F32, kind="ExternalInput")
    b1cd = nc.dram_tensor("b1cd", [128, FF // 128], F32, kind="ExternalInput")
    b2rd = nc.dram_tensor("b2rd", [1, D], F32, kind="ExternalInput")
    eyebd = nc.dram_tensor("eyebd", [128, 128], BF16, kind="ExternalInput")
    eyefd = nc.dram_tensor("eyefd", [128, 128], F32, kind="ExternalInput")
    onesfd = nc.dram_tensor("onesfd", [128, 128], F32, kind="ExternalInput")

    sx1 = nc.dram_tensor("sx1", [NSEQ, S, D], F32)
    su = nc.dram_tensor("su", [NSEQ, S, D], F32)
    sx4 = nc.dram_tensor("sx4", [NSEQ, S, D], F32)
    sq1 = nc.dram_tensor("sq1", [D, FF], BF16)
    sq2 = nc.dram_tensor("sq2", [FF, D], BF16)
    bwq = nc.dram_tensor("bwq", [D, H * DH], BF16)
    bwk = nc.dram_tensor("bwk", [D, H * DH], BF16)
    bwv = nc.dram_tensor("bwv", [D, H * DH], BF16)
    bwo = nc.dram_tensor("bwo", [H // 2, 2 * DH, D], BF16)
    bw1 = nc.dram_tensor("bw1", [D, FF], BF16)
    bw2 = nc.dram_tensor("bw2", [FF, D], BF16)
    dbg = {}
    if debug:
        for nm in ("d_x1", "d_u", "d_x4"):
            dbg[nm] = nc.dram_tensor(nm, [NSEQ, S, D], F32, kind="ExternalOutput")

    BCOLS = 84 * 1024
    FCOLS = 6592
    arB_t = st.enter_context(nc.sbuf_tensor("arB", [128, BCOLS], BF16))
    arF_t = st.enter_context(nc.sbuf_tensor("arF", [128, FCOLS], F32))
    eye_b = st.enter_context(nc.sbuf_tensor("eye_b", [128, 128], BF16))
    eye_f = st.enter_context(nc.sbuf_tensor("eye_f", [128, 128], F32))
    ones_f = st.enter_context(nc.sbuf_tensor("ones_f", [128, 128], F32))
    cvw = st.enter_context(nc.sbuf_tensor("cvw", [128, 3 * (FF // 128)], F32))
    cvb = st.enter_context(nc.sbuf_tensor("cvb", [128, FF // 128], F32))
    g1r = st.enter_context(nc.sbuf_tensor("g1r", [128, D], BF16))
    b1r = st.enter_context(nc.sbuf_tensor("b1r", [128, D], BF16))
    g2r = st.enter_context(nc.sbuf_tensor("g2r", [128, D], BF16))
    b2lr = st.enter_context(nc.sbuf_tensor("b2lr", [128, D], BF16))
    b2rr = st.enter_context(nc.sbuf_tensor("b2rr", [128, D], F32))
    b1cc = st.enter_context(nc.sbuf_tensor("b1cc", [128, FF // 128], F32))
    mwr = st.enter_context(nc.sbuf_tensor("mwr", [128, 8], F32))
    nb20 = st.enter_context(nc.sbuf_tensor("nb20", [128, 1], F32))
    pS = st.enter_context(nc.psum_tensor("pS", [128, 2048], F32))
    p4 = st.enter_context(nc.psum_tensor("p4", [128, 512], F32))
    p5 = st.enter_context(nc.psum_tensor("p5", [128, 512], F32))
    p6 = st.enter_context(nc.psum_tensor("p6", [128, 512], F32))
    p7 = st.enter_context(nc.psum_tensor("p7", [128, 512], F32))
    arB = Arena(arB_t, BCOLS)
    arF = Arena(arF_t, FCOLS)

    for t_, src_ in ((eye_b, eyebd), (eye_f, eyefd), (ones_f, onesfd), (cvw, convw),
                     (cvb, convb), (b1cc, b1cd)):
        d("sync", t_[:], src_[:], writes=[t_.name], group="cst")
    o("vector", lambda v: v.memset(nb20[:], EXPB), writes=["nb20"])
    for t_, src_ in ((g1r, g1d), (b1r, b1d), (g2r, g2d), (b2lr, b2d)):
        d("gpsimd", t_[:], src_[0:1, :].partition_broadcast(128), writes=[t_.name],
          group="cst2")
    d("sync", b2rr[:], b2rd[0:1, :].partition_broadcast(128), writes=[b2rr.name],
      group="cst")

    def quantize_weight(wdram, sqdram, nrow, ncol, mcol):
        nstrip = nrow // 128
        arF.push(); arB.push()
        wstrip = arF.alloc(ncol)
        qs = arB.alloc(ncol)
        o("vector", lambda v: v.memset(mwr[:, 7:8], 0.0), writes=["acc"])
        for i in range(nstrip):
            d("sync", wstrip[:], wdram[i * 128:(i + 1) * 128, :],
              writes=["wstrip"], group="wqld")
            o("vector", lambda v: v.tensor_reduce(
                mwr[:, 6:7], wstrip[:], AX.X, ALU.add, apply_absolute_value=True),
              reads=["wstrip"], writes=["red"])
            o("vector", lambda v: v.tensor_tensor(
                mwr[:, 7:8], mwr[:, 7:8], mwr[:, 6:7], ALU.add),
              reads=["red", "acc"], writes=["acc"])
        o("tensor", lambda t: t.matmul(p4[0:1, 0:1], ones_f[:, 0:1], mwr[:, 7:8],
                                       start=True, stop=True),
          reads=["acc", "ones_f"], writes=["p4"])
        o("vector", lambda v: v.tensor_scalar(
            mwr[0:1, 6:7], p4[0:1, 0:1], 1.0 / (nrow * ncol), 1e-5, ALU.mult, ALU.max),
          reads=["p4"], writes=["mean1"])
        o("tensor", lambda t: t.matmul(p4[:, 1:2], ones_f[0:1, :], mwr[0:1, 6:7],
                                       start=True, stop=True),
          reads=["mean1", "ones_f"], writes=["p4"])
        o("vector", lambda v: v.tensor_copy(mwr[:, mcol:mcol + 1], p4[:, 1:2]),
          reads=["p4"], writes=[f"mw{mcol}"])
        o("vector", lambda v: v.reciprocal(mwr[:, mcol + 1:mcol + 2],
                                           mwr[:, mcol:mcol + 1]),
          reads=[f"mw{mcol}"], writes=[f"mwi{mcol}"])
        for i in range(nstrip):
            d("sync", wstrip[:], wdram[i * 128:(i + 1) * 128, :],
              writes=["wstrip"], group="wqld")
            o("vector", lambda v: v.tensor_scalar(
                wstrip[:], wstrip[:], mwr[:, mcol + 1:mcol + 2], MAGIC,
                ALU.mult, ALU.add),
              reads=["wstrip", f"mwi{mcol}"], writes=["wstrip"])
            o("vector", lambda v: v.tensor_scalar(
                wstrip[:], wstrip[:], MAGIC, None, ALU.subtract),
              reads=["wstrip"], writes=["wstrip"])
            o("vector", lambda v: v.tensor_scalar(
                qs[:], wstrip[:], -1.0, 1.0, ALU.max, ALU.min),
              reads=["wstrip"], writes=["qs"])
            d("sync", sqdram[i * 128:(i + 1) * 128, :], qs[:],
              reads=["qs"], group="wqst")
        arF.pop(); arB.pop()

    for dst_, src_, nr in ((bwq, wqd, D), (bwk, wkd, D), (bwv, wvd, D),
                           (bw1, wm1d, D)):
        for c_ in range(2):
            d("gpsimd", dst_[c_ * nr // 2:(c_ + 1) * nr // 2, :],
              src_[c_ * nr // 2:(c_ + 1) * nr // 2, :], group="wcast")
    for c_ in range(8):
        d("gpsimd", bw2[c_ * FF // 8:(c_ + 1) * FF // 8, :],
          wm2d[c_ * FF // 8:(c_ + 1) * FF // 8, :], group="wcast")
    for pg_ in range(8):
        d("gpsimd", bwo[pg_, :, :], wod[pg_, :, :], group="wcast")
    quantize_weight(w1t, sq1, D, FF, 0)
    quantize_weight(w2t, sq2, FF, D, 2)
    o("vector", lambda v: v.tensor_scalar(mwr[:, 4:5], mwr[:, 0:1],
                                          1.0 / 127.0, None, ALU.mult),
      reads=["mw0"], writes=["mw1s"])
    o("vector", lambda v: v.tensor_scalar(mwr[:, 5:6], mwr[:, 2:3],
                                          0.5 / 127.0, None, ALU.mult),
      reads=["mw2"], writes=["mw2s"])

    def stage_ffn(seq, src, dst):
        S_.barrier()
        arB.push(); arF.push()
        w1q = [arB.alloc(FF) for _ in range(8)]
        w2q = [arB.alloc(D) for _ in range(32)]
        xqw = [[arB.alloc(TB + 2) for _ in range(8)] for _ in range(3)]
        xq2 = [arB.alloc(32 * TB) for _ in range(2)]
        q1b = arB.alloc(D)
        csr = arB.alloc(2 * S).bitcast(F32)
        f2r = arB.alloc(2 * TB).bitcast(F32)
        xt = [arF.alloc(D) for _ in range(2)]
        sl = arF.alloc(32 * TB)
        hb = [arB.alloc(2 * (TB + 2)).bitcast(F32) for _ in range(2)]
        yb = [arB.alloc(2 * TB).bitcast(F32) for _ in range(2)]
        ab = [arB.alloc(2 * TB).bitcast(F32) for _ in range(2)]
        bcc = arF.alloc(TB + 2)
        bc2 = arF.alloc(TB)
        m1b = arF.alloc(TB)
        stt = arF.alloc(40)

        o("vector", lambda v: v.memset(bcc[:], 0.0), writes=["bcc"])
        for i in range(8):
            d("sync", w1q[i][:], sq1[i * 128:(i + 1) * 128, :],
              writes=[f"w1q{i}"], group="wst")
        for i in range(32):
            d("sync", w2q[i][:], sq2[i * 128:(i + 1) * 128, :],
              writes=[f"w2q{i}"], group="wst")

        def quant_tile(tt):
            xa, kx = xt[tt % 2], f"xt{tt % 2}"
            pq = 16 * (tt % 2)
            kp = f"q{tt % 2}"
            w = tt % 3
            d("sync", xa[:], src[seq, tt * 128:(tt + 1) * 128, :],
              writes=[kx], group=f"xin{tt % 2}")
            o("vector", lambda v: v.bn_stats(stt[:, pq:pq + 6], xa[:, 0:512]),
              reads=[kx], writes=[kp + "st0"])
            o("vector", lambda v: v.bn_stats(stt[:, pq + 6:pq + 12], xa[:, 512:1024]),
              reads=[kx], writes=[kp + "st1"])
            o("vector", lambda v: v.bn_aggr(stt[:, pq + 12:pq + 14], stt[:, pq:pq + 12]),
              reads=[kp + "st0", kp + "st1"], writes=[kp + "mu"])
            o("vector", lambda v: v.tensor_scalar(
                xa[:], xa[:], stt[:, pq + 12:pq + 13], None, ALU.subtract),
              reads=[kx, kp + "mu"], writes=[kx])
            o("vector", lambda v: v.tensor_reduce(
                stt[:, pq:pq + 1], xa[:], AX.X, ALU.max, apply_absolute_value=True),
              reads=[kx], writes=[kp + "am"])
            o("vector", lambda v: v.reciprocal(stt[:, pq + 1:pq + 2], stt[:, pq:pq + 1]),
              reads=[kp + "am"], writes=[kp + "iam"])
            o("vector", lambda v: v.tensor_scalar(
                stt[:, pq + 1:pq + 2], stt[:, pq + 1:pq + 2], 127.0, None, ALU.mult),
              reads=[kp + "iam"], writes=[kp + "f"])
            o("vector", lambda v: v.tensor_scalar(
                xa[:], xa[:], stt[:, pq + 1:pq + 2], MAGIC, ALU.mult, ALU.add),
              reads=[kx, kp + "f"], writes=[kx])
            o("vector", lambda v: v.tensor_scalar(
                q1b[:], xa[:], MAGIC, None, ALU.subtract),
              reads=[kx], writes=["q1b"])
            o("vector", lambda v: v.tensor_scalar(
                stt[:, pq + 2:pq + 3], stt[:, pq + 13:pq + 14], 1e-5, None, ALU.add),
              reads=[kp + "mu"], writes=[kp + "vpe"])
            o("scalar", lambda sc: sc.activation(
                stt[:, pq + 3:pq + 4], stt[:, pq + 2:pq + 3], AF.Sqrt),
              reads=[kp + "vpe"], writes=[kp + "sig"])
            o("vector", lambda v: v.reciprocal(stt[:, pq + 4:pq + 5], stt[:, pq + 3:pq + 4]),
              reads=[kp + "sig"], writes=[kp + "rstd"])
            o("vector", lambda v: v.tensor_scalar(
                stt[:, pq + 5:pq + 6], stt[:, pq:pq + 1], stt[:, pq + 4:pq + 5],
                1e-5, ALU.mult, ALU.max),
              reads=[kp + "am", kp + "rstd"], writes=[kp + "u2"])
            o("vector", lambda v: v.tensor_tensor(
                stt[:, pq + 6:pq + 7], stt[:, pq + 5:pq + 6], mwr[:, 4:5], ALU.mult),
              reads=[kp + "u2", "mw1s"], writes=[kp + "cs"])
            o("tensor", lambda t: t.matmul(pS[0:1, 0:128], stt[:, pq + 6:pq + 7],
                                           eye_f[:], start=True, stop=True),
              reads=[kp + "cs", "eye_f"], writes=["pS0"])
            o("vector", lambda v: v.tensor_copy(
                csr[0:1, tt * 128:(tt + 1) * 128], pS[0:1, 0:128]),
              reads=["pS0"], writes=[f"csr{tt}"])
            for c in range(4):
                o("tensor", lambda t, c=c: t.matmul(
                    pS[:, 1024 + c * 128:1024 + (c + 1) * 128],
                    q1b[:, 2 * c * 128:(2 * c + 1) * 128], eye_b[:],
                    start=True, stop=True),
                  reads=["q1b", "eye_b"], writes=["pS2"], inc=(c == 3))
                o("tensor", lambda t, c=c: t.matmul(
                    pS[:, 1536 + c * 128:1536 + (c + 1) * 128],
                    q1b[:, (2 * c + 1) * 128:(2 * c + 2) * 128], eye_b[:],
                    start=True, stop=True),
                  reads=["q1b", "eye_b"], writes=["pS3"], inc=(c == 3))
            for c in range(4):
                o("vector", lambda v, c=c, w=w: v.tensor_copy(
                    xqw[w][2 * c][:, 1:129],
                    pS[:, 1024 + c * 128:1024 + (c + 1) * 128]),
                  reads=["pS2"], writes=[f"win{w}:{2 * c}"], inc=(c == 3))
                o("scalar", lambda sc, c=c, w=w: sc.copy(
                    xqw[w][2 * c + 1][:, 1:129],
                    pS[:, 1536 + c * 128:1536 + (c + 1) * 128]),
                  reads=["pS3"], writes=[f"win{w}:{2 * c + 1}"], inc=(c == 3))
            if tt == 0:
                for dc in range(8):
                    o("vector", lambda v, dc=dc: v.memset(xqw[0][dc][:, 0:1], 0.0),
                      reads=[], writes=[f"win0:{dc}"], inc=(dc == 7))
            else:
                pw = (tt - 1) % 3
                for dc in range(8):
                    o("vector", lambda v, dc=dc, w=w, pw=pw: v.tensor_copy(
                        xqw[w][dc][:, 0:1], xqw[pw][dc][:, 128:129]),
                      reads=[f"win{pw}:{dc}"], writes=[f"win{w}:{dc}"], inc=(dc == 7))
                for dc in range(8):
                    o("vector", lambda v, dc=dc, w=w, pw=pw: v.tensor_copy(
                        xqw[pw][dc][:, 129:130], xqw[w][dc][:, 1:2]),
                      reads=[f"win{w}:{dc}"], writes=[f"win{pw}:{dc}"], inc=(dc == 7))
            if tt == NT - 1:
                for dc in range(8):
                    o("vector", lambda v, dc=dc, w=w: v.memset(xqw[w][dc][:, 129:130], 0.0),
                      reads=[], writes=[f"win{w}:{dc}"], inc=(dc == 7))

        def do_mm2(tb):
            t0 = tb * TB
            xq = xq2[tb % 2]
            kcs2 = f"cs2:{tb % 2}"
            pcs2 = 32 + 4 * (tb % 2)
            xa, kx = xt[tb % 2], f"xt{tb % 2}"
            d("sync", xa[:], src[seq, t0:t0 + TB, :], writes=[kx], group=f"xin{tb % 2}")
            for dh in range(2):
                pb, kpb = p7, "p7"
                for mc in range(32):
                    o("tensor", lambda t, mc=mc, dh=dh, pb=pb, xq=xq: t.matmul(
                        pb[:], xq[:, mc * TB:(mc + 1) * TB],
                        w2q[mc][:, dh * 512:(dh + 1) * 512],
                        start=(mc == 0), stop=(mc == 31)),
                      reads=[f"xq2{tb % 2}:{mc}", f"w2q{mc}"],
                      writes=[kpb], inc=(mc == 31))
                o("vector", lambda v, pb=pb, dh=dh, xa=xa: v.scalar_tensor_tensor(
                    xa[:, dh * 512:(dh + 1) * 512], pb[:], stt[:, pcs2:pcs2 + 1],
                    xa[:, dh * 512:(dh + 1) * 512], ALU.mult, ALU.add),
                  reads=[kpb, kcs2, kx], writes=[kx])
            d("sync", dst[seq, t0:t0 + TB, :], xa[:], reads=[kx], group="xout")

        quant_tile(0)
        quant_tile(1)
        for tb in range(NTB):
            w = tb % 3
            t0 = tb * TB
            lo, hi = max(0, t0 - 1), min(S, t0 + TB + 1)
            rt = [f"csr{i}" for i in range(max(0, tb - 1), min(NT, tb + 2))]
            o("tensor", lambda t: t.matmul(
                pS[:, 0:hi - lo], ones_f[0:1, :], csr[0:1, lo:hi],
                start=True, stop=True),
              reads=rt + ["ones_f"], writes=["pS0"])
            o("vector", lambda v: v.tensor_copy(
                bcc[:, lo - (t0 - 1):hi - (t0 - 1)], pS[:, 0:hi - lo]),
              reads=["pS0"], writes=["bcc"])
            if t0 == 0:
                o("vector", lambda v: v.memset(bcc[:, 0:1], 0.0), writes=["bcc"])
            if t0 + TB == S:
                o("vector", lambda v: v.memset(bcc[:, TB + 1:TB + 2], 0.0), writes=["bcc"])
            o("vector", lambda v: v.memset(m1b[:], 0.0), writes=["m1b"])

            winkeys = [f"win{w}:{dc}" for dc in range(8)]
            for mc in range(32):
                pb, kpb = [(p4, "p4"), (p5, "p5"), (p6, "p6")][mc % 3]
                for dc in range(8):
                    o("tensor", lambda t, mc=mc, dc=dc, pb=pb, w=w: t.matmul(
                        pb[:, 0:TB + 2], w1q[dc][:, mc * 128:(mc + 1) * 128],
                        xqw[w][dc][:, 0:TB + 2],
                        start=(dc == 0), stop=(dc == 7)),
                      reads=[f"w1q{dc}"] + (winkeys if dc == 0 else []),
                      writes=[kpb], inc=(dc == 7))
                hbx, khb = (hb[0], "hb0") if mc % 2 == 0 else (hb[1], "hb1")
                o("vector", lambda v, pb=pb, hbx=hbx: v.tensor_tensor(
                    hbx[:], pb[:, 0:TB + 2], bcc[:], ALU.mult),
                  reads=[kpb, "bcc"], writes=[khb])
                ybx, kyb = (yb[0], "yb0") if mc % 2 == 0 else (yb[1], "yb1")
                o("vector", lambda v, hbx=hbx, mc=mc, ybx=ybx: v.tensor_scalar(
                    ybx[:], hbx[:, 1:TB + 1], cvw[:, 3 * mc + 1:3 * mc + 2],
                    cvb[:, mc:mc + 1], ALU.mult, ALU.add),
                  reads=[khb, "cvw", "cvb"], writes=[kyb])
                o("vector", lambda v, hbx=hbx, mc=mc, ybx=ybx: v.scalar_tensor_tensor(
                    ybx[:], hbx[:, 0:TB], cvw[:, 3 * mc:3 * mc + 1],
                    ybx[:], ALU.mult, ALU.add),
                  reads=[khb, kyb, "cvw"], writes=[kyb])
                o("vector", lambda v, hbx=hbx, mc=mc, ybx=ybx: v.scalar_tensor_tensor(
                    ybx[:], hbx[:, 2:TB + 2], cvw[:, 3 * mc + 2:3 * mc + 3],
                    ybx[:], ALU.mult, ALU.add),
                  reads=[khb, kyb, "cvw"], writes=[kyb])
                o("scalar", lambda sc, mc=mc, ybx=ybx: sc.activation(
                    sl[:, mc * TB:(mc + 1) * TB], ybx[:], AF.Silu),
                  reads=[kyb], writes=[f"sl{mc}"])
                abx, kab = (ab[0], "ab0") if mc % 2 == 0 else (ab[1], "ab1")
                o("scalar", lambda sc, mc=mc, abx=abx: sc.activation(
                    abx[:], sl[:, mc * TB:(mc + 1) * TB], AF.Abs),
                  reads=[f"sl{mc}"], writes=[kab])
                if mc > 0:
                    pab, kpab = (ab[0], "ab0") if (mc - 1) % 2 == 0 else (ab[1], "ab1")
                    o("vector", lambda v, pab=pab: v.tensor_tensor(
                        m1b[:], m1b[:], pab[:], ALU.max),
                      reads=[kpab, "m1b"], writes=["m1b"])
            o("vector", lambda v: v.tensor_tensor(
                m1b[:], m1b[:], ab[1][:], ALU.max),
              reads=["ab1", "m1b"], writes=["m1b"])
            if tb >= 1:
                do_mm2(tb - 1)
            if tb + 2 < NT:
                quant_tile(tb + 2)
            pcs2 = 32 + 4 * (tb % 2)
            kcs2 = f"cs2:{tb % 2}"
            o("tensor", lambda t: t.matmul(pS[:, 512:640], m1b[:], eye_f[:],
                                           start=True, stop=True),
              reads=["m1b", "eye_f"], writes=["pS1"])
            o("vector", lambda v: v.tensor_reduce(
                stt[:, pcs2 + 1:pcs2 + 2], pS[:, 512:640], AX.X, ALU.max),
              reads=["pS1"], writes=["m2r"])
            o("vector", lambda v: v.tensor_scalar(
                stt[:, pcs2 + 1:pcs2 + 2], stt[:, pcs2 + 1:pcs2 + 2],
                1e-5, None, ALU.max),
              reads=["m2r"], writes=["m2"])
            o("vector", lambda v: v.reciprocal(
                stt[:, pcs2 + 2:pcs2 + 3], stt[:, pcs2 + 1:pcs2 + 2]),
              reads=["m2"], writes=["im2"])
            o("vector", lambda v: v.tensor_scalar(
                stt[:, pcs2 + 2:pcs2 + 3], stt[:, pcs2 + 2:pcs2 + 3],
                127.0, None, ALU.mult),
              reads=["im2"], writes=["f2"])
            o("vector", lambda v: v.tensor_tensor(
                stt[:, pcs2:pcs2 + 1], stt[:, pcs2 + 1:pcs2 + 2], mwr[:, 5:6],
                ALU.mult),
              reads=["m2", "mw2s"], writes=[kcs2])
            o("tensor", lambda t: t.matmul(
                pS[0:1, 512:640], stt[:, pcs2 + 2:pcs2 + 3], eye_f[:],
                start=True, stop=True),
              reads=["f2", "eye_f"], writes=["pS1"])
            o("vector", lambda v: v.tensor_copy(f2r[0:1, 0:TB], pS[0:1, 512:640]),
              reads=["pS1"], writes=["f2r"])
            o("tensor", lambda t: t.matmul(pS[:, 512:640], ones_f[0:1, :],
                                           f2r[0:1, 0:TB], start=True, stop=True),
              reads=["f2r", "ones_f"], writes=["pS1"])
            o("vector", lambda v: v.tensor_copy(bc2[:], pS[:, 512:640]),
              reads=["pS1"], writes=["bc2"])
            xq = xq2[tb % 2]
            for mc in range(32):
                o("vector", lambda v, mc=mc: v.tensor_tensor(
                    sl[:, mc * TB:(mc + 1) * TB], sl[:, mc * TB:(mc + 1) * TB],
                    bc2[:], ALU.mult),
                  reads=[f"sl{mc}", "bc2"], writes=[f"sl{mc}"])
                o("vector", lambda v, mc=mc, xq=xq: v.tensor_scalar(
                    xq[:, mc * TB:(mc + 1) * TB], sl[:, mc * TB:(mc + 1) * TB],
                    MAGIC, MAGIC, ALU.add, ALU.subtract),
                  reads=[f"sl{mc}"], writes=[f"xq2{tb % 2}:{mc}"])
        do_mm2(NTB - 1)
        arB.pop(); arF.pop()

    def stage_attn(seq):
        S_.barrier()
        arB.push(); arF.push()
        oall = [arB.alloc(S) for _ in range(8)]
        xnbuf = arB.alloc(D)
        whq8 = arB.alloc(2 * 8 * NBC)
        ut = [arF.alloc(D) for _ in range(2)]
        stt = arF.alloc(16)
        lrec = arF.alloc(QB)
        bcst = arF.alloc(QB)
        wtmp = arF.alloc(NBC)
        arB.push()
        xnT = [arB.alloc(S) for _ in range(8)]

        for tt in range(NT):
            xa, kx = ut[tt % 2], f"ut{tt % 2}"
            d("sync", xa[:], sx1[seq, tt * 128:(tt + 1) * 128, :],
              writes=[kx], group=f"bx{tt % 2}")
            o("vector", lambda v: v.bn_stats(stt[:, 0:6], xa[:, 0:512]),
              reads=[kx], writes=["st0"])
            o("vector", lambda v: v.bn_stats(stt[:, 6:12], xa[:, 512:1024]),
              reads=[kx], writes=["st1"])
            o("vector", lambda v: v.bn_aggr(stt[:, 12:14], stt[:, 0:12]),
              reads=["st0", "st1"], writes=["mu"])
            o("vector", lambda v: v.tensor_scalar(stt[:, 14:15], stt[:, 13:14],
                                                  1e-5, None, ALU.add),
              reads=["mu"], writes=["vpe"])
            o("scalar", lambda sc: sc.activation(stt[:, 15:16], stt[:, 14:15], AF.Sqrt),
              reads=["vpe"], writes=["sig"])
            o("vector", lambda v: v.reciprocal(stt[:, 14:15], stt[:, 15:16]),
              reads=["sig"], writes=["rstd"])
            o("vector", lambda v: v.tensor_scalar(
                xa[:], xa[:], stt[:, 12:13], stt[:, 14:15], ALU.subtract, ALU.mult),
              reads=[kx, "mu", "rstd"], writes=[kx])
            d("sync", su[seq, tt * 128:(tt + 1) * 128, :], xa[:], reads=[kx], group="su")
            if debug:
                d("sync", dbg["d_u"][seq, tt * 128:(tt + 1) * 128, :], xa[:],
                  reads=[kx], group="dbgu")
            o("vector", lambda v: v.bn_stats(stt[:, 0:6], xa[:, 0:512]),
              reads=[kx], writes=["st0"])
            o("vector", lambda v: v.bn_stats(stt[:, 6:12], xa[:, 512:1024]),
              reads=[kx], writes=["st1"])
            o("vector", lambda v: v.bn_aggr(stt[:, 12:14], stt[:, 0:12]),
              reads=["st0", "st1"], writes=["mu"])
            o("vector", lambda v: v.tensor_scalar(stt[:, 14:15], stt[:, 13:14],
                                                  1e-6, None, ALU.add),
              reads=["mu"], writes=["vpe"])
            o("scalar", lambda sc: sc.activation(stt[:, 15:16], stt[:, 14:15], AF.Sqrt),
              reads=["vpe"], writes=["sig"])
            o("vector", lambda v: v.reciprocal(stt[:, 14:15], stt[:, 15:16]),
              reads=["sig"], writes=["rstd"])
            o("vector", lambda v: v.scalar_tensor_tensor(
                xa[:], xa[:], stt[:, 12:13], g1r[:], ALU.subtract, ALU.mult),
              reads=[kx, "mu", "g1r"], writes=[kx])
            o("vector", lambda v: v.scalar_tensor_tensor(
                xnbuf[:], xa[:], stt[:, 14:15], b1r[:], ALU.mult, ALU.add),
              reads=[kx, "rstd", "b1r"], writes=["xnbuf"])
            for c in range(4):
                o("tensor", lambda t, c=c: t.matmul(
                    pS[:, 1024 + c * 128:1024 + (c + 1) * 128],
                    xnbuf[:, 2 * c * 128:(2 * c + 1) * 128], eye_b[:],
                    start=True, stop=True),
                  reads=["xnbuf", "eye_b"], writes=["pS2"], inc=(c == 3))
                o("tensor", lambda t, c=c: t.matmul(
                    pS[:, 1536 + c * 128:1536 + (c + 1) * 128],
                    xnbuf[:, (2 * c + 1) * 128:(2 * c + 2) * 128], eye_b[:],
                    start=True, stop=True),
                  reads=["xnbuf", "eye_b"], writes=["pS3"], inc=(c == 3))
            for c in range(4):
                o("vector", lambda v, c=c, tt=tt: v.tensor_copy(
                    xnT[2 * c][:, tt * 128:(tt + 1) * 128],
                    pS[:, 1024 + c * 128:1024 + (c + 1) * 128]),
                  reads=["pS2"], writes=[f"xnT{2 * c}:{tt}"], inc=(c == 3))
                o("scalar", lambda sc, c=c, tt=tt: sc.copy(
                    xnT[2 * c + 1][:, tt * 128:(tt + 1) * 128],
                    pS[:, 1536 + c * 128:1536 + (c + 1) * 128]),
                  reads=["pS3"], writes=[f"xnT{2 * c + 1}:{tt}"], inc=(c == 3))
        xnall = [f"xnT{i}:{j}" for i in range(8) for j in range(NT)]

        for qki, wdr in ((0, whqd), (1, whkd)):
            for p in range(8):
                for par in range(2):
                    h_ = 2 * p + par
                    d("sync", wtmp[par * 64:(par + 1) * 64, :],
                      wdr[h_ * DH:(h_ + 1) * DH, :], writes=[f"wtmp{par}"], group="whq")
                o("vector", lambda v, p=p, qki=qki: v.tensor_scalar(
                    whq8[:, (qki * 8 + p) * NBC:(qki * 8 + p + 1) * NBC],
                    wtmp[:, :], 8.0 if qki == 0 else 1.0, None, ALU.mult),
                  reads=["wtmp0", "wtmp1"], writes=[f"whq8:{qki}:{p}"])

        for half in range(2):
            arB.push(); arF.push()
            qsb = [arB.alloc(S) for _ in range(4)]
            ksb = [arB.alloc(S) for _ in range(4)]
            qbb = [arB.alloc(S) for _ in range(4)]
            kbb = [arB.alloc(S) for _ in range(4)]
            vsb = [arB.alloc(NKC * 65 + 1) for _ in range(8)]
            qbn = [arB.alloc(128) for _ in range(2)]
            eqf = [arF.alloc(128) for _ in range(2)]
            pair0 = half * 4

            for hh in range(8):
                o("vector", lambda v, hh=hh: v.memset(
                    vsb[hh][:, 0:NKC * 65].rearrange("p (a b) -> p a b", b=65)[:, :, 64:65], 1.0),
                  writes=[f"vones{hh}"])

            arB.push()
            wpr = [arB.alloc(8 * 128) for _ in range(2)]
            wvh = arB.alloc(8 * 512)
            for which, wdr, dstl, scl in (("q", wqd, qsb, 0.125), ("k", wkd, ksb, 1.0)):
                wb = wpr[0] if which == "q" else wpr[1]
                for pi in range(4):
                    p = pair0 + pi
                    kwb = f"wpr{which}"
                    bdr = bwq if which == "q" else bwk
                    d("sync", wb[:].rearrange("p (a m) -> p a m", a=8),
                      bdr[:, p * 128:(p + 1) * 128].rearrange(
                          "(a p) m -> p a m", p=128),
                      writes=[f"{kwb}:{dc2}" for dc2 in range(8)], group=f"w{which}")
                    for nb_ in range(S // W512):
                        pb, kpb = (p4, "p4") if nb_ % 2 == 0 else (p5, "p5")
                        for dc in range(8):
                            o("tensor", lambda t, dc=dc, nb_=nb_, wb=wb, pb=pb: t.matmul(
                                pb[:, 0:W512], wb[:, dc * 128:(dc + 1) * 128],
                                xnT[dc][:, nb_ * W512:(nb_ + 1) * W512],
                                start=(dc == 0), stop=(dc == 7)),
                              reads=[f"{kwb}:{dc}"] + (xnall if dc == 0 else []),
                              writes=[kpb], inc=(dc == 7))
                        o("vector", lambda v, pb=pb, nb_=nb_, pi=pi, dstl=dstl, scl=scl: v.tensor_scalar(
                            dstl[pi][:, nb_ * W512:(nb_ + 1) * W512], pb[:, 0:W512],
                            scl, None, ALU.mult),
                          reads=[kpb], writes=[f"{which}sb{pi}:{nb_}"])
            d("sync", wvh[:].rearrange("p (a m) -> p a m", a=8),
              bwv[:, half * 512:(half + 1) * 512].rearrange(
                  "(a p) m -> p a m", p=128),
              writes=[f"wvh:{dc2}" for dc2 in range(8)], group="wv")
            for tt in range(NT):
                pb, kpb = (p6, "p6") if tt % 2 == 0 else (p7, "p7")
                for dc in range(8):
                    o("tensor", lambda t, dc=dc, tt=tt, pb=pb: t.matmul(
                        pb[:], xnT[dc][:, tt * 128:(tt + 1) * 128],
                        wvh[:, dc * 512:(dc + 1) * 512],
                        start=(dc == 0), stop=(dc == 7)),
                      reads=[f"wvh:{dc}"] + (xnall if dc == 0 else []),
                      writes=[kpb], inc=(dc == 7))
                for hh in range(8):
                    o("vector", lambda v, hh=hh, tt=tt, pb=pb: v.tensor_copy(
                        vsb[hh][:, tt * 65:tt * 65 + 64], pb[:, hh * 64:(hh + 1) * 64]),
                      reads=[kpb], writes=[f"v{hh}:{tt}"], inc=(hh == 7))
            arB.pop()
            ptb = [arB.alloc(2 * QB) for _ in range(2)]

            for qki, srcl, dstl, scl in ((0, qsb, qbb, 0.1), (1, ksb, kbb, 1.0)):
                sk = "q" if qki == 0 else "k"
                dk = "qb" if qki == 0 else "kb"
                it = 0
                for gg in range(2):
                    for tt in range(NT):
                        itp = it % 2
                        it += 1
                        if itp == 0:
                            pbE, kpbE = p6, "p6"
                            pbO, kpbO = p7, "p7"
                        else:
                            pbE, kpbE = pS[:, 0:512], "pS0"
                            pbO, kpbO = pS[:, 512:1024], "pS1"
                        trp = pS[:, 1024:1536] if itp == 0 else pS[:, 1536:2048]
                        ktr = "pS2" if itp == 0 else "pS3"
                        eqx, keq = eqf[itp], f"eq{itp}"
                        qbx, kqb = qbn[itp], f"qbn{itp}"
                        sb_ = 8 * itp
                        for j in range(4):
                            pi = 2 * gg + j // 2
                            par = j % 2
                            pbx = pbE if par == 0 else pbO
                            kpx = kpbE if par == 0 else kpbO
                            cj = j // 2
                            o("tensor", lambda t, pi=pi, par=par, cj=cj, tt=tt, qki=qki, pbx=pbx, srcl=srcl: t.matmul(
                                pbx[0:128, cj * 32:(cj + 1) * 32] if not isinstance(pbx, type(pS)) else pbx[0:128, cj * 32:(cj + 1) * 32],
                                srcl[pi][par * 64:(par + 1) * 64, tt * 128:(tt + 1) * 128],
                                whq8[par * 64:(par + 1) * 64,
                                     (qki * 8 + pair0 + pi) * NBC:(qki * 8 + pair0 + pi + 1) * NBC],
                                start=(cj == 0), stop=(cj == 1),
                                tile_position=(par * 64, 0)),
                              reads=[f"{sk}sb{pi}:{tt * 128 // W512}",
                                     f"whq8:{qki}:{pair0 + pi}"],
                              writes=[kpx], inc=(j >= 2))
                        o("scalar", lambda sc, pbE=pbE, eqx=eqx: sc.activation(
                            eqx[:, 0:64], pbE[0:128, 0:64] if True else None, AF.Exp),
                          reads=[kpbE], writes=[keq])
                        o("scalar", lambda sc, pbO=pbO, eqx=eqx: sc.activation(
                            eqx[:, 64:128], pbO[0:128, 0:64], AF.Exp),
                          reads=[kpbO], writes=[keq])
                        o("vector", lambda v, eqx=eqx, sb_=sb_: v.tensor_reduce(
                            stt[:, sb_:sb_ + 4],
                            eqx[:].rearrange("p (a b) -> p a b", b=32),
                            AX.X, ALU.add),
                          reads=[keq], writes=[f"eqs{itp}"])
                        o("vector", lambda v, sb_=sb_: v.reciprocal(
                            stt[:, sb_ + 4:sb_ + 8], stt[:, sb_:sb_ + 4]),
                          reads=[f"eqs{itp}"], writes=[f"eqr{itp}"])
                        for ci, j in ((0, 0), (1, 2), (2, 1), (3, 3)):
                            o("vector", lambda v, ci=ci, scl=scl, eqx=eqx, qbx=qbx, sb_=sb_: v.tensor_scalar(
                                qbx[:, ci * 32:(ci + 1) * 32], eqx[:, ci * 32:(ci + 1) * 32],
                                stt[:, sb_ + 4 + ci:sb_ + 5 + ci], scl, ALU.mult, ALU.mult),
                              reads=[f"eqr{itp}", keq], writes=[kqb], inc=(ci == 3))
                        for ci, j in ((0, 0), (1, 2), (2, 1), (3, 3)):
                            par = j % 2
                            o("tensor", lambda t, ci=ci, par=par, qbx=qbx, trp=trp: t.matmul(
                                trp[par * 64:par * 64 + 32, ci * 128:(ci + 1) * 128],
                                qbx[:, ci * 32:(ci + 1) * 32], eye_b[:],
                                start=(ci in (0, 2)), stop=(ci in (1, 3)),
                                tile_position=(0, par * 64)),
                              reads=[kqb, "eye_b"], writes=[ktr], inc=(ci == 3))
                        for ci, j in ((0, 0), (1, 2), (2, 1), (3, 3)):
                            par = j % 2
                            pi = 2 * gg + j // 2
                            o("vector", lambda v, ci=ci, par=par, pi=pi, tt=tt, dstl=dstl, trp=trp: v.tensor_copy(
                                dstl[pi][par * 64:par * 64 + 32, tt * 128:(tt + 1) * 128],
                                trp[par * 64:par * 64 + 32, ci * 128:(ci + 1) * 128]),
                              reads=[ktr], writes=[f"{dk}{pi}:{tt}"], inc=(ci == 3))

            def emit_scores_exp(pi, q0, nb_, kc):
                par = kc % 2
                sE = pS[:, par * 1024:par * 1024 + QB]
                sO = pS[:, par * 1024 + 512:par * 1024 + 512 + QB]
                kE, kO = f"pS{2 * par}", f"pS{2 * par + 1}"
                o("tensor", lambda t: t.matmul(
                    sE, ksb[pi][0:64, kc * 128:(kc + 1) * 128],
                    qsb[pi][0:64, q0:q0 + QB],
                    start=True, stop=False, tile_position=(0, 0)),
                  reads=[f"qsb{pi}:{nb_}", f"ksb{pi}:{kc * 128 // W512}"],
                  writes=[kE], inc=False)
                o("tensor", lambda t: t.matmul(
                    sO, ksb[pi][64:128, kc * 128:(kc + 1) * 128],
                    qsb[pi][64:128, q0:q0 + QB],
                    start=True, stop=False, tile_position=(64, 0)),
                  reads=[f"qsb{pi}:{nb_}", f"ksb{pi}:{kc * 128 // W512}"],
                  writes=[kO], inc=False)
                o("tensor", lambda t: t.matmul(
                    sE, kbb[pi][0:32, kc * 128:(kc + 1) * 128],
                    qbb[pi][0:32, q0:q0 + QB],
                    start=False, stop=True, tile_position=(0, 0)),
                  reads=[f"qb{pi}:{q0 // 128 + z}" for z in range(QB // 128)] + [f"kb{pi}:{kc}"],
                  writes=[kE], inc=True)
                o("tensor", lambda t: t.matmul(
                    sO, kbb[pi][64:96, kc * 128:(kc + 1) * 128],
                    qbb[pi][64:96, q0:q0 + QB],
                    start=False, stop=True, tile_position=(64, 0)),
                  reads=[f"qb{pi}:{q0 // 128 + z}" for z in range(QB // 128)] + [f"kb{pi}:{kc}"],
                  writes=[kO], inc=True)
                o("scalar", lambda sc: sc.activation(
                    ptb[kc % 2][:].rearrange("p (a b) -> p a b", b=QB),
                    pS[:, par * 1024:(par + 1) * 1024].rearrange(
                        "p (a b) -> p a b", b=512)[:, :, 0:QB],
                    AF.Exp, bias=nb20[:, 0:1]),
                  reads=[kE, kO], writes=[f"ptb{kc % 2}"])

            def emit_pv(pi, kc, obE, okE, obO, okO):
                kpt = f"ptb{kc % 2}"
                hhE, hhO = 2 * pi, 2 * pi + 1
                o("tensor", lambda t: t.matmul(
                    obE[0:65, 0:QB], vsb[hhE][:, kc * 65:kc * 65 + 65],
                    ptb[kc % 2][:, 0:QB],
                    start=(kc == 0), stop=(kc == NKC - 1)),
                  reads=[kpt, f"v{hhE}:{kc}", f"vones{hhE}"],
                  writes=[okE], inc=True)
                o("tensor", lambda t: t.matmul(
                    obO[64:128, 0:QB], vsb[hhO][:, kc * 65:kc * 65 + 64],
                    ptb[kc % 2][:, QB:2 * QB],
                    start=(kc == 0), stop=(kc == NKC - 1),
                    tile_position=(0, 64)),
                  reads=[kpt, f"v{hhO}:{kc}"], writes=[okO], inc=False)
                o("tensor", lambda t: t.matmul(
                    obO[32:33, 0:QB], vsb[hhO][:, kc * 65 + 64:kc * 65 + 65],
                    ptb[kc % 2][:, QB:2 * QB],
                    start=(kc == 0), stop=(kc == NKC - 1),
                    tile_position=(0, 32)),
                  reads=[kpt, f"vones{hhO}"], writes=[okO], inc=True)

            def emit_evac(pg, q0, obE, okE, obO, okO):
                o("vector", lambda v: v.reciprocal(
                    lrec[64:65, 0:QB], obE[64:65, 0:QB]),
                  reads=[okE], writes=["lrec"])
                o("tensor", lambda t: t.matmul(
                    pS[0:64, 0:QB], ones_f[64:65, 0:64], lrec[64:65, 0:QB],
                    start=True, stop=True, tile_position=(64, 0)),
                  reads=["lrec", "ones_f"], writes=["pS0"])
                o("scalar", lambda sc: sc.copy(bcst[0:64, 0:QB], pS[0:64, 0:QB]),
                  reads=["pS0"], writes=["bcst0"])
                o("vector", lambda v: v.tensor_tensor(
                    oall[pg][0:64, q0:q0 + QB], obE[0:64, 0:QB],
                    bcst[0:64, 0:QB], ALU.mult),
                  reads=[okE, "bcst0"], writes=[f"oall{pg}:{q0}"])
                o("vector", lambda v: v.reciprocal(
                    lrec[32:33, 0:QB], obO[32:33, 0:QB]),
                  reads=[okO], writes=["lrec2"])
                o("tensor", lambda t: t.matmul(
                    pS[64:128, 0:QB], ones_f[32:33, 0:64], lrec[32:33, 0:QB],
                    start=True, stop=True, tile_position=(32, 64)),
                  reads=["lrec2", "ones_f"], writes=["pS0"])
                o("scalar", lambda sc: sc.copy(bcst[64:128, 0:QB], pS[64:128, 0:QB]),
                  reads=["pS0"], writes=["bcst1"])
                o("vector", lambda v: v.tensor_tensor(
                    oall[pg][64:128, q0:q0 + QB], obO[64:128, 0:QB],
                    bcst[64:128, 0:QB], ALU.mult),
                  reads=[okO, "bcst1"], writes=[f"oall{pg}:{q0}"])

            pend_evac = None
            for pi in range(4):
                for qb_ in range(NQB):
                    q0 = qb_ * QB
                    nb_ = q0 // W512
                    obpar = (pi * NQB + qb_) % 2
                    obE, okE = (p4, "p4") if obpar == 0 else (p6, "p6")
                    obO, okO = (p5, "p5") if obpar == 0 else (p7, "p7")
                    pg = pair0 + pi
                    emit_scores_exp(pi, q0, nb_, 0)
                    if pend_evac is not None:
                        emit_evac(*pend_evac)
                    for kc in range(1, NKC):
                        emit_scores_exp(pi, q0, nb_, kc)
                        emit_pv(pi, kc - 1, obE, okE, obO, okO)
                    emit_pv(pi, NKC - 1, obE, okE, obO, okO)
                    pend_evac = (pg, q0, obE, okE, obO, okO)
            emit_evac(*pend_evac)
            S_.barrier()
            arB.pop(); arF.pop()
        arB.pop()

        arB.push(); arF.push()
        wob = arB.alloc(8 * D)
        w1b = [arB.alloc(8 * 128) for _ in range(4)]
        w2b = [arB.alloc(D) for _ in range(4)]
        h1 = arB.alloc(32 * MLPB)
        xmT = [arB.alloc(MLPB) for _ in range(8)]
        xmb = arB.alloc(D)
        x3b = arB.alloc(2 * (MLPB // 128) * D).bitcast(F32)
        x1t = arF.alloc(D)
        for pg in range(8):
            d("sync", wob[:, pg * D:(pg + 1) * D], bwo[pg, :, :],
              writes=[f"wob{pg}"], group="wo")
        oallk = [f"oall{pg}:{q0}" for pg in range(8) for q0 in range(0, S, QB)]

        for mb in range(NMB):
            tiles = [mb * (MLPB // 128) + c for c in range(MLPB // 128)]
            for ci, tt in enumerate(tiles):
                for dh in range(2):
                    pb, kpb = (p4, "p4") if dh == 0 else (p5, "p5")
                    for pg in range(8):
                        o("tensor", lambda t, pg=pg, dh=dh, tt=tt, pb=pb: t.matmul(
                            pb[:], oall[pg][:, tt * 128:(tt + 1) * 128],
                            wob[:, pg * D + dh * 512:pg * D + (dh + 1) * 512],
                            start=(pg == 0), stop=(pg == 7)),
                          reads=[f"wob{pg}"] + (oallk if pg == 0 else []),
                          writes=[kpb], inc=(pg == 7))
                kx, xu = f"ut{tt % 2}", ut[tt % 2]
                d("sync", xu[:], su[seq, tt * 128:(tt + 1) * 128, :],
                  writes=[kx], group=f"bx{tt % 2}")
                for dh in range(2):
                    pb, kpb = (p4, "p4") if dh == 0 else (p5, "p5")
                    o("vector", lambda v, pb=pb, dh=dh, ci=ci, xu=xu: v.tensor_tensor(
                        x3b[:, ci * D + dh * 512:ci * D + (dh + 1) * 512],
                        pb[:], xu[:, dh * 512:(dh + 1) * 512], ALU.add),
                      reads=[kpb, kx], writes=[f"x3:{ci}"])
                kx3 = f"x3:{ci}"
                o("vector", lambda v, ci=ci: v.bn_stats(
                    stt[:, 0:6], x3b[:, ci * D:ci * D + 512]),
                  reads=[kx3], writes=["st0"])
                o("vector", lambda v, ci=ci: v.bn_stats(
                    stt[:, 6:12], x3b[:, ci * D + 512:ci * D + 1024]),
                  reads=[kx3], writes=["st1"])
                o("vector", lambda v: v.bn_aggr(stt[:, 12:14], stt[:, 0:12]),
                  reads=["st0", "st1"], writes=["mu"])
                o("vector", lambda v: v.tensor_scalar(stt[:, 14:15], stt[:, 13:14],
                                                      1e-6, None, ALU.add),
                  reads=["mu"], writes=["vpe"])
                o("scalar", lambda sc: sc.activation(stt[:, 15:16], stt[:, 14:15], AF.Sqrt),
                  reads=["vpe"], writes=["sig"])
                o("vector", lambda v: v.reciprocal(stt[:, 14:15], stt[:, 15:16]),
                  reads=["sig"], writes=["rstd"])
                o("vector", lambda v, ci=ci: v.scalar_tensor_tensor(
                    xmb[:], x3b[:, ci * D:(ci + 1) * D], stt[:, 12:13], g2r[:],
                    ALU.subtract, ALU.mult),
                  reads=[kx3, "mu", "g2r"], writes=["xmb"])
                o("vector", lambda v: v.scalar_tensor_tensor(
                    xmb[:], xmb[:], stt[:, 14:15], b2lr[:], ALU.mult, ALU.add),
                  reads=["xmb", "rstd", "b2lr"], writes=["xmb"])
                for c in range(4):
                    o("tensor", lambda t, c=c: t.matmul(
                        pS[:, 1024 + c * 128:1024 + (c + 1) * 128],
                        xmb[:, 2 * c * 128:(2 * c + 1) * 128], eye_b[:],
                        start=True, stop=True),
                      reads=["xmb", "eye_b"], writes=["pS2"], inc=(c == 3))
                    o("tensor", lambda t, c=c: t.matmul(
                        pS[:, 1536 + c * 128:1536 + (c + 1) * 128],
                        xmb[:, (2 * c + 1) * 128:(2 * c + 2) * 128], eye_b[:],
                        start=True, stop=True),
                      reads=["xmb", "eye_b"], writes=["pS3"], inc=(c == 3))
                for c in range(4):
                    o("vector", lambda v, c=c, ci=ci: v.tensor_copy(
                        xmT[2 * c][:, ci * 128:(ci + 1) * 128],
                        pS[:, 1024 + c * 128:1024 + (c + 1) * 128]),
                      reads=["pS2"], writes=[f"xmT{2 * c}:{ci}"], inc=(c == 3))
                    o("scalar", lambda sc, c=c, ci=ci: sc.copy(
                        xmT[2 * c + 1][:, ci * 128:(ci + 1) * 128],
                        pS[:, 1536 + c * 128:1536 + (c + 1) * 128]),
                      reads=["pS3"], writes=[f"xmT{2 * c + 1}:{ci}"], inc=(c == 3))
            xmk = [f"xmT{i}:{j}" for i in range(8) for j in range(MLPB // 128)]
            for mc in range(32):
                wb, kwb = w1b[mc % 4], f"w1b{mc % 4}"
                d("sync", wb[:].rearrange("p (a m) -> p a m", a=8),
                  bw1[:, mc * 128:(mc + 1) * 128].rearrange(
                      "(a p) m -> p a m", p=128),
                  writes=[f"{kwb}:{dc2}" for dc2 in range(8)], group=f"wm1{mc % 4}")
                pb, kpb = (p6, "p6") if mc % 2 == 0 else (p7, "p7")
                for dc in range(8):
                    o("tensor", lambda t, dc=dc, mc=mc, wb=wb, pb=pb: t.matmul(
                        pb[:, 0:MLPB], wb[:, dc * 128:(dc + 1) * 128],
                        xmT[dc][:, 0:MLPB], start=(dc == 0), stop=(dc == 7)),
                      reads=[f"{kwb}:{dc}"] + (xmk if dc == 0 else []),
                      writes=[kpb], inc=(dc == 7))
                o("vector", lambda v, mc=mc, pb=pb: v.tensor_scalar(
                    h1[:, mc * MLPB:(mc + 1) * MLPB], pb[:, 0:MLPB],
                    b1cc[:, mc:mc + 1], 0.0, ALU.add, ALU.max),
                  reads=[kpb, "b1cc"], writes=[f"h1:{mc}"])
            ypb = [pS[:, 0:512], pS[:, 512:1024], pS[:, 1024:1536],
                   pS[:, 1536:2048], p4[:], p5[:], p6[:], p7[:]]
            ypk = ["pS0", "pS1", "pS2", "pS3", "p4", "p5", "p6", "p7"]
            for mc in range(32):
                wb, kwb = w2b[mc % 4], f"w2b{mc % 4}"
                d("sync", wb[:], bw2[mc * 128:(mc + 1) * 128, :],
                  writes=[kwb], group=f"wm2{mc % 4}")
                for ci in range(MLPB // 128):
                    for dh in range(2):
                        bi = ci * 2 + dh
                        o("tensor", lambda t, mc=mc, ci=ci, dh=dh, wb=wb, bi=bi: t.matmul(
                            ypb[bi], h1[:, mc * MLPB + ci * 128:mc * MLPB + (ci + 1) * 128],
                            wb[:, dh * 512:(dh + 1) * 512],
                            start=(mc == 0), stop=(mc == 31)),
                          reads=[f"h1:{mc}", kwb],
                          writes=[ypk[bi]],
                          inc=(ci == MLPB // 128 - 1 and dh == 1))
            for ci, tt in enumerate(tiles):
                d("sync", x1t[:], sx1[seq, tt * 128:(tt + 1) * 128, :],
                  writes=["x1t"], group="bx1r")
                for dh in range(2):
                    bi = ci * 2 + dh
                    o("vector", lambda v, ci=ci, dh=dh, bi=bi: v.tensor_tensor(
                        x3b[:, ci * D + dh * 512:ci * D + (dh + 1) * 512],
                        x3b[:, ci * D + dh * 512:ci * D + (dh + 1) * 512],
                        ypb[bi], ALU.add),
                      reads=[ypk[bi], f"x3:{ci}"], writes=[f"x3:{ci}"])
                    o("vector", lambda v, ci=ci, dh=dh: v.tensor_tensor(
                        x3b[:, ci * D + dh * 512:ci * D + (dh + 1) * 512],
                        x3b[:, ci * D + dh * 512:ci * D + (dh + 1) * 512],
                        b2rr[:, dh * 512:(dh + 1) * 512], ALU.add),
                      reads=[f"x3:{ci}", "b2rr"], writes=[f"x3:{ci}"])
                    o("vector", lambda v, ci=ci, dh=dh: v.tensor_tensor(
                        x3b[:, ci * D + dh * 512:ci * D + (dh + 1) * 512],
                        x3b[:, ci * D + dh * 512:ci * D + (dh + 1) * 512],
                        x1t[:, dh * 512:(dh + 1) * 512], ALU.add),
                      reads=[f"x3:{ci}", "x1t"], writes=[f"x3:{ci}"])
                d("sync", sx4[seq, tt * 128:(tt + 1) * 128, :],
                  x3b[:, ci * D:(ci + 1) * D], reads=[f"x3:{ci}"], group="x4out")
                if debug:
                    d("sync", dbg["d_x4"][seq, tt * 128:(tt + 1) * 128, :],
                      x3b[:, ci * D:(ci + 1) * D], reads=[f"x3:{ci}"], group="dbg4")
        arB.pop(); arF.pop()
        arB.pop(); arF.pop()

    for seq in range(NSEQ):
        stage_ffn(seq, xin, sx1)
        if debug:
            S_.barrier()
            arF.push()
            tmp = arF.alloc(D)
            for tt in range(NT):
                dt_ = d("sync", tmp[:], sx1[seq, tt * 128:(tt + 1) * 128, :],
                        writes=["dtmp"], group="dbg1")
                d("sync", dbg["d_x1"][seq, tt * 128:(tt + 1) * 128, :], tmp[:],
                  reads=["dtmp"], group="dbg1b")
            arF.pop()
        stage_attn(seq)
        stage_ffn(seq, sx4, out)
    S_.emit("sync")
    return nc, st, dbg


def _host_inputs(inputs, S=2048):
    f32 = np.float32
    i = inputs
    shared = {
        "w1t": np.ascontiguousarray(i["L0_ffn_w1"].T.astype(f32)),
        "w2t": np.ascontiguousarray(i["L0_ffn_w2"].T.astype(f32)),
        "convw": np.ascontiguousarray(
            i["L0_ffn_conv_w"].reshape(32, 128, 3).transpose(1, 0, 2).reshape(128, 96).astype(f32)),
        "convb": np.ascontiguousarray(
            i["L0_ffn_conv_b"].reshape(32, 128).T.astype(f32)),
        "g1d": i["L0_attn_ln1_g"].reshape(1, D).astype(f32),
        "b1d": i["L0_attn_ln1_b"].reshape(1, D).astype(f32),
        "g2d": i["L0_attn_ln2_g"].reshape(1, D).astype(f32),
        "b2d": i["L0_attn_ln2_b"].reshape(1, D).astype(f32),
        "wqd": np.ascontiguousarray(i["L0_attn_Wq"].reshape(D, H * DH).astype(f32)),
        "wkd": np.ascontiguousarray(i["L0_attn_Wk"].reshape(D, H * DH).astype(f32)),
        "wvd": np.ascontiguousarray(i["L0_attn_Wv"].reshape(D, H * DH).astype(f32)),
        "whqd": np.ascontiguousarray(i["L0_attn_Whq"].reshape(H * DH, NBC).astype(f32)),
        "whkd": np.ascontiguousarray(i["L0_attn_Whk"].reshape(H * DH, NBC).astype(f32)),
        "wod": np.ascontiguousarray(
            i["L0_attn_Wo"].reshape(H // 2, 2 * DH, D).astype(f32)),
        "wm1d": np.ascontiguousarray(i["L0_attn_W1"].astype(f32)),
        "wm2d": np.ascontiguousarray(i["L0_attn_W2"].astype(f32)),
        "b1cd": np.ascontiguousarray(i["L0_attn_b1"].reshape(32, 128).T.astype(f32)),
        "b2rd": i["L0_attn_b2"].reshape(1, D).astype(f32),
        "eyebd": np.eye(128, dtype=f32).astype(bfdt),
        "eyefd": np.eye(128, dtype=f32),
        "onesfd": np.ones((128, 128), dtype=f32),
    }
    x = np.ascontiguousarray(i["L0_attn_inputs"].astype(f32))
    in_maps = []
    for c in range(NCORES):
        m = dict(shared)
        m["xin"] = np.ascontiguousarray(x[c * NSEQ:(c + 1) * NSEQ, :S, :])
        in_maps.append(m)
    return in_maps


_CACHE = {}


def kernel(**inputs):
    if "prog" not in _CACHE:
        nc, st, _ = build_program(S=2048, debug=False)
        _CACHE["prog"] = (nc, st)
    nc, st = _CACHE["prog"]
    in_maps = _host_inputs(inputs, S=2048)
    res = run_bass_kernel_spmd(nc, in_maps, core_ids=list(range(NCORES)))
    outp = np.concatenate([res.results[c]["out"] for c in range(NCORES)], axis=0)
    return outp.astype(np.float32)


# version 15 (final)
# speedup vs baseline: 2.2022x; 1.1215x over previous
import numpy as np
import ml_dtypes
from contextlib import ExitStack
from typing import NamedTuple

import concourse.bass as bass
import concourse.mybir as mybir
from concourse.bass_utils import run_bass_kernel_spmd

F32 = mybir.dt.float32
BF16 = mybir.dt.bfloat16
ALU = mybir.AluOpType
AF = mybir.ActivationFunctionType
AX = mybir.AxisListType
bfdt = ml_dtypes.bfloat16

MAGIC = 12582912.0
EXPB = -20.0
D, H, DH, NBC, FF = 1024, 16, 64, 32, 4096
NSEQ, NCORES = 2, 8
ENGINES = ("sync", "tensor", "vector", "scalar", "gpsimd")


class Tok(NamedTuple):
    key: str
    val: int


class _Rec:

    def __getattr__(self, name):
        def call(*a, **k):
            return (name, a, k)
        return call


class Sched:

    def __init__(self, nc, stack):
        self.nc = nc
        self.stack = stack
        self.sem = {e: stack.enter_context(nc.semaphore("S_" + e)) for e in ENGINES}
        self.cnt = {e: 0 for e in ENGINES}
        self.opl = {e: [] for e in ENGINES}
        self.water = {}
        self.dma_groups = {}
        self.pend_bar = {}
        self.wmap = {}
        self.rmap = {}
        self.pend_rw = {e: [] for e in ENGINES}

    def group(self, name):
        if name not in self.dma_groups:
            self.dma_groups[name] = self.stack.enter_context(
                self.nc.semaphore("G_" + name))
            self.cnt[name] = 0
        return name

    def barrier(self):
        toks = [Tok(k, v) for k, v in self.cnt.items() if v > 0]
        for e in ENGINES:
            self.pend_bar[e] = toks
        self.wmap = {}
        self.rmap = {}

    def _rw_deps(self, reads, writes):
        deps = []
        for k in reads:
            deps += self.wmap.get(k, [])
        for k in writes:
            deps += self.wmap.get(k, [])
            deps += self.rmap.get(k, [])
        return deps

    def _rw_commit(self, tok, reads, writes):
        for k in writes:
            self.wmap[k] = [tok]
            self.rmap[k] = []
        for k in reads:
            self.rmap.setdefault(k, []).append(tok)

    def _resolve(self, eng, deps):
        byk = {}
        for t in deps:
            if t is None:
                continue
            v = t.val
            if t.key in self.dma_groups:
                v = self.cnt[t.key]
            if byk.get(t.key, 0) < v:
                byk[t.key] = v
        waits = []
        for k, v in sorted(byk.items()):
            wk = (eng, k)
            if self.water.get(wk, 0) >= v:
                continue
            self.water[wk] = v
            waits.append((k, v))
        return waits

    def op(self, eng, fn, reads=(), writes=(), deps=(), inc=True):
        deps = list(deps) + self._rw_deps(reads, writes)
        if self.pend_bar.get(eng):
            deps += self.pend_bar[eng]
            self.pend_bar[eng] = None
        waits = self._resolve(eng, deps)
        tok, inckey = None, None
        if inc:
            self.cnt[eng] += 1
            tok = Tok(eng, self.cnt[eng])
            inckey = eng
        self.opl[eng].append((fn(_Rec()), waits, inckey, 1))
        if tok is not None:
            for (r2, w2) in self.pend_rw[eng]:
                self._rw_commit(tok, r2, w2)
            self.pend_rw[eng] = []
            self._rw_commit(tok, reads, writes)
        else:
            self.pend_rw[eng].append((list(reads), list(writes)))
        return tok

    def dma(self, eng, out, in_, reads=(), writes=(), deps=(), group="misc", **kw):
        g = self.group(group)
        deps = list(deps) + self._rw_deps(reads, writes)
        if self.pend_bar.get(eng):
            deps += self.pend_bar[eng]
            self.pend_bar[eng] = None
        waits = self._resolve(eng, deps)
        self.cnt[g] += 16
        tok = Tok(g, self.cnt[g])
        self.opl[eng].append((("dma_start", (out, in_), kw), waits, g, 16))
        self._rw_commit(tok, reads, writes)
        return tok

    def emit(self, final_eng="sync"):
        finals = [Tok(g, self.cnt[g]) for g in self.dma_groups if self.cnt[g] > 0]
        self.opl[final_eng].append((None, self._resolve(final_eng, finals), None, 0))
        with self.nc.Block() as block:
            for e in ENGINES:
                if not self.opl[e]:
                    continue
                def body(E, _ops=tuple(self.opl[e])):
                    for spec, waits, inckey, amt in _ops:
                        for (k, v) in waits:
                            E.wait_ge(self._sem_of(k), v)
                        if spec is not None:
                            name, a, kw = spec
                            inst = getattr(E, name)(*a, **kw)
                            if inckey is not None:
                                inst.then_inc(self._sem_of(inckey), amt)
                getattr(block, e)(body)

    def _sem_of(self, key):
        return self.sem[key] if key in self.sem else self.dma_groups[key]


class Arena:
    def __init__(self, tensor, cols):
        self.t, self.cols, self.off, self.marks = tensor, cols, 0, []

    def alloc(self, n):
        n = (n + 1) & ~1
        assert self.off + n <= self.cols, f"arena overflow {self.off}+{n}>{self.cols}"
        ap = self.t[:, self.off:self.off + n]
        self.off += n
        return ap

    def push(self):
        self.marks.append(self.off)

    def pop(self):
        self.off = self.marks.pop()


def build_program(S=2048, debug=False):
    TB = min(256, S)
    MLPB = min(512, S)
    NT = S // 128
    NKC = S // 128
    QB = min(512, S)
    NQB = S // QB
    NTB = S // TB
    NMB = S // MLPB
    W512 = min(512, S)

    nc = bass.Bass("TRN2", target_bir_lowering=False)
    st = ExitStack()
    S_ = Sched(nc, st)
    o, d = S_.op, S_.dma

    xin = nc.dram_tensor("xin", [NSEQ, S, D], F32, kind="ExternalInput")
    out = nc.dram_tensor("out", [NSEQ, S, D], F32, kind="ExternalOutput")
    w1t = nc.dram_tensor("w1t", [D, FF], F32, kind="ExternalInput")
    w2t = nc.dram_tensor("w2t", [FF, D], F32, kind="ExternalInput")
    convw = nc.dram_tensor("convw", [128, 3 * (FF // 128)], F32, kind="ExternalInput")
    convb = nc.dram_tensor("convb", [128, FF // 128], F32, kind="ExternalInput")
    g1d = nc.dram_tensor("g1d", [1, D], F32, kind="ExternalInput")
    b1d = nc.dram_tensor("b1d", [1, D], F32, kind="ExternalInput")
    g2d = nc.dram_tensor("g2d", [1, D], F32, kind="ExternalInput")
    b2d = nc.dram_tensor("b2d", [1, D], F32, kind="ExternalInput")
    wqd = nc.dram_tensor("wqd", [D, H * DH], F32, kind="ExternalInput")
    wkd = nc.dram_tensor("wkd", [D, H * DH], F32, kind="ExternalInput")
    wvd = nc.dram_tensor("wvd", [D, H * DH], F32, kind="ExternalInput")
    whqd = nc.dram_tensor("whqd", [H * DH, NBC], F32, kind="ExternalInput")
    whkd = nc.dram_tensor("whkd", [H * DH, NBC], F32, kind="ExternalInput")
    wod = nc.dram_tensor("wod", [H // 2, 2 * DH, D], F32, kind="ExternalInput")
    wm1d = nc.dram_tensor("wm1d", [D, FF], F32, kind="ExternalInput")
    wm2d = nc.dram_tensor("wm2d", [FF, D], F32, kind="ExternalInput")
    b1cd = nc.dram_tensor("b1cd", [128, FF // 128], F32, kind="ExternalInput")
    b2rd = nc.dram_tensor("b2rd", [1, D], F32, kind="ExternalInput")
    eyebd = nc.dram_tensor("eyebd", [128, 128], BF16, kind="ExternalInput")
    eyefd = nc.dram_tensor("eyefd", [128, 128], F32, kind="ExternalInput")
    onesfd = nc.dram_tensor("onesfd", [128, 128], F32, kind="ExternalInput")

    sx1 = nc.dram_tensor("sx1", [NSEQ, S, D], F32)
    su = nc.dram_tensor("su", [NSEQ, S, D], F32)
    sx4 = nc.dram_tensor("sx4", [NSEQ, S, D], F32)
    sq1 = nc.dram_tensor("sq1", [D, FF], BF16)
    sq2 = nc.dram_tensor("sq2", [FF, D], BF16)
    bwq = nc.dram_tensor("bwq", [D, H * DH], BF16)
    bwk = nc.dram_tensor("bwk", [D, H * DH], BF16)
    bwv = nc.dram_tensor("bwv", [D, H * DH], BF16)
    bwo = nc.dram_tensor("bwo", [H // 2, 2 * DH, D], BF16)
    bw1 = nc.dram_tensor("bw1", [D, FF], BF16)
    bw2 = nc.dram_tensor("bw2", [FF, D], BF16)
    dbg = {}
    if debug:
        for nm in ("d_x1", "d_u", "d_x4"):
            dbg[nm] = nc.dram_tensor(nm, [NSEQ, S, D], F32, kind="ExternalOutput")

    BCOLS = 84 * 1024
    FCOLS = 6592
    arB_t = st.enter_context(nc.sbuf_tensor("arB", [128, BCOLS], BF16))
    arF_t = st.enter_context(nc.sbuf_tensor("arF", [128, FCOLS], F32))
    eye_b = st.enter_context(nc.sbuf_tensor("eye_b", [128, 128], BF16))
    eye_f = st.enter_context(nc.sbuf_tensor("eye_f", [128, 128], F32))
    ones_f = st.enter_context(nc.sbuf_tensor("ones_f", [128, 128], F32))
    cvw = st.enter_context(nc.sbuf_tensor("cvw", [128, 3 * (FF // 128)], F32))
    cvb = st.enter_context(nc.sbuf_tensor("cvb", [128, FF // 128], F32))
    g1r = st.enter_context(nc.sbuf_tensor("g1r", [128, D], BF16))
    b1r = st.enter_context(nc.sbuf_tensor("b1r", [128, D], BF16))
    g2r = st.enter_context(nc.sbuf_tensor("g2r", [128, D], BF16))
    b2lr = st.enter_context(nc.sbuf_tensor("b2lr", [128, D], BF16))
    b2rr = st.enter_context(nc.sbuf_tensor("b2rr", [128, D], F32))
    b1cc = st.enter_context(nc.sbuf_tensor("b1cc", [128, FF // 128], F32))
    mwr = st.enter_context(nc.sbuf_tensor("mwr", [128, 8], F32))
    nb20 = st.enter_context(nc.sbuf_tensor("nb20", [128, 1], F32))
    pS = st.enter_context(nc.psum_tensor("pS", [128, 2048], F32))
    p4 = st.enter_context(nc.psum_tensor("p4", [128, 512], F32))
    p5 = st.enter_context(nc.psum_tensor("p5", [128, 512], F32))
    p6 = st.enter_context(nc.psum_tensor("p6", [128, 512], F32))
    p7 = st.enter_context(nc.psum_tensor("p7", [128, 512], F32))
    arB = Arena(arB_t, BCOLS)
    arF = Arena(arF_t, FCOLS)

    for t_, src_ in ((eye_b, eyebd), (eye_f, eyefd), (ones_f, onesfd), (cvw, convw),
                     (cvb, convb), (b1cc, b1cd)):
        d("sync", t_[:], src_[:], writes=[t_.name], group="cst")
    o("vector", lambda v: v.memset(nb20[:], EXPB), writes=["nb20"])
    for t_, src_ in ((g1r, g1d), (b1r, b1d), (g2r, g2d), (b2lr, b2d)):
        d("gpsimd", t_[:], src_[0:1, :].partition_broadcast(128), writes=[t_.name],
          group="cst2")
    d("sync", b2rr[:], b2rd[0:1, :].partition_broadcast(128), writes=[b2rr.name],
      group="cst")

    def quantize_weight(wdram, sqdram, nrow, ncol, mcol):
        nstrip = nrow // 128
        arF.push(); arB.push()
        wstrip = arF.alloc(ncol)
        qs = arB.alloc(ncol)
        o("vector", lambda v: v.memset(mwr[:, 7:8], 0.0), writes=["acc"])
        for i in range(nstrip):
            d("sync", wstrip[:], wdram[i * 128:(i + 1) * 128, :],
              writes=["wstrip"], group="wqld")
            o("vector", lambda v: v.tensor_reduce(
                mwr[:, 6:7], wstrip[:], AX.X, ALU.add, apply_absolute_value=True),
              reads=["wstrip"], writes=["red"])
            o("vector", lambda v: v.tensor_tensor(
                mwr[:, 7:8], mwr[:, 7:8], mwr[:, 6:7], ALU.add),
              reads=["red", "acc"], writes=["acc"])
        o("tensor", lambda t: t.matmul(p4[0:1, 0:1], ones_f[:, 0:1], mwr[:, 7:8],
                                       start=True, stop=True),
          reads=["acc", "ones_f"], writes=["p4"])
        o("vector", lambda v: v.tensor_scalar(
            mwr[0:1, 6:7], p4[0:1, 0:1], 1.0 / (nrow * ncol), 1e-5, ALU.mult, ALU.max),
          reads=["p4"], writes=["mean1"])
        o("tensor", lambda t: t.matmul(p4[:, 1:2], ones_f[0:1, :], mwr[0:1, 6:7],
                                       start=True, stop=True),
          reads=["mean1", "ones_f"], writes=["p4"])
        o("vector", lambda v: v.tensor_copy(mwr[:, mcol:mcol + 1], p4[:, 1:2]),
          reads=["p4"], writes=[f"mw{mcol}"])
        o("vector", lambda v: v.reciprocal(mwr[:, mcol + 1:mcol + 2],
                                           mwr[:, mcol:mcol + 1]),
          reads=[f"mw{mcol}"], writes=[f"mwi{mcol}"])
        for i in range(nstrip):
            d("sync", wstrip[:], wdram[i * 128:(i + 1) * 128, :],
              writes=["wstrip"], group="wqld")
            o("vector", lambda v: v.tensor_scalar(
                wstrip[:], wstrip[:], mwr[:, mcol + 1:mcol + 2], MAGIC,
                ALU.mult, ALU.add),
              reads=["wstrip", f"mwi{mcol}"], writes=["wstrip"])
            o("vector", lambda v: v.tensor_scalar(
                wstrip[:], wstrip[:], MAGIC, None, ALU.subtract),
              reads=["wstrip"], writes=["wstrip"])
            o("vector", lambda v: v.tensor_scalar(
                qs[:], wstrip[:], -1.0, 1.0, ALU.max, ALU.min),
              reads=["wstrip"], writes=["qs"])
            d("sync", sqdram[i * 128:(i + 1) * 128, :], qs[:],
              reads=["qs"], group="wqst")
        arF.pop(); arB.pop()

    for dst_, src_, nr in ((bwq, wqd, D), (bwk, wkd, D), (bwv, wvd, D),
                           (bw1, wm1d, D)):
        for c_ in range(2):
            d("gpsimd", dst_[c_ * nr // 2:(c_ + 1) * nr // 2, :],
              src_[c_ * nr // 2:(c_ + 1) * nr // 2, :], group="wcast")
    for c_ in range(8):
        d("gpsimd", bw2[c_ * FF // 8:(c_ + 1) * FF // 8, :],
          wm2d[c_ * FF // 8:(c_ + 1) * FF // 8, :], group="wcast")
    for pg_ in range(8):
        d("gpsimd", bwo[pg_, :, :], wod[pg_, :, :], group="wcast")
    quantize_weight(w1t, sq1, D, FF, 0)
    quantize_weight(w2t, sq2, FF, D, 2)
    o("vector", lambda v: v.tensor_scalar(mwr[:, 4:5], mwr[:, 0:1],
                                          1.0 / 127.0, None, ALU.mult),
      reads=["mw0"], writes=["mw1s"])
    o("vector", lambda v: v.tensor_scalar(mwr[:, 5:6], mwr[:, 2:3],
                                          0.5 / 127.0, None, ALU.mult),
      reads=["mw2"], writes=["mw2s"])

    def stage_ffn(seq, src, dst):
        S_.barrier()
        arB.push(); arF.push()
        w1q = [arB.alloc(FF) for _ in range(8)]
        w2s = [arB.alloc(D) for _ in range(4)]
        xqw = [[arB.alloc(TB + 2) for _ in range(8)] for _ in range(3)]
        xq2 = [arB.alloc(32 * TB) for _ in range(2)]
        q1b = arB.alloc(D)
        csr = arB.alloc(2 * S).bitcast(F32)
        f2r = arB.alloc(2 * TB).bitcast(F32)
        xt = [arF.alloc(D) for _ in range(2)]
        xr = [arF.alloc(D) for _ in range(2)]
        sl = arB.alloc(2 * 32 * TB).bitcast(F32)
        hb = [arB.alloc(2 * (TB + 2)).bitcast(F32) for _ in range(2)]
        yb = [arB.alloc(2 * TB).bitcast(F32) for _ in range(2)]
        ab = [arB.alloc(2 * TB).bitcast(F32) for _ in range(2)]
        bcc = arF.alloc(TB + 2)
        bc2 = arF.alloc(TB)
        m1b = arF.alloc(TB)
        stt = arF.alloc(48)

        o("vector", lambda v: v.memset(bcc[:], 0.0), writes=["bcc"])
        for i in range(8):
            d("sync", w1q[i][:], sq1[i * 128:(i + 1) * 128, :],
              writes=[f"w1q{i}"], group="wst")

        def quant_tile(tt):
            xa, kx = xt[tt % 2], f"xt{tt % 2}"
            pq = 16 * (tt % 2)
            kp = f"q{tt % 2}"
            wi = tt * 128 // TB
            w = wi % 3
            co = (tt * 128) % TB
            d("sync", xa[:], src[seq, tt * 128:(tt + 1) * 128, :],
              writes=[kx], group=f"xin{tt % 2}")
            o("vector", lambda v: v.bn_stats(stt[:, pq:pq + 6], xa[:, 0:512]),
              reads=[kx], writes=[kp + "st0"])
            o("vector", lambda v: v.bn_stats(stt[:, pq + 6:pq + 12], xa[:, 512:1024]),
              reads=[kx], writes=[kp + "st1"])
            o("vector", lambda v: v.bn_aggr(stt[:, pq + 12:pq + 14], stt[:, pq:pq + 12]),
              reads=[kp + "st0", kp + "st1"], writes=[kp + "mu"])
            o("vector", lambda v: v.tensor_scalar(
                xa[:], xa[:], stt[:, pq + 12:pq + 13], None, ALU.subtract),
              reads=[kx, kp + "mu"], writes=[kx])
            o("vector", lambda v: v.tensor_reduce(
                stt[:, pq:pq + 1], xa[:], AX.X, ALU.max, apply_absolute_value=True),
              reads=[kx], writes=[kp + "am"])
            o("vector", lambda v: v.reciprocal(stt[:, pq + 1:pq + 2], stt[:, pq:pq + 1]),
              reads=[kp + "am"], writes=[kp + "iam"])
            o("vector", lambda v: v.tensor_scalar(
                stt[:, pq + 1:pq + 2], stt[:, pq + 1:pq + 2], 127.0, None, ALU.mult),
              reads=[kp + "iam"], writes=[kp + "f"])
            o("vector", lambda v: v.tensor_scalar(
                xa[:], xa[:], stt[:, pq + 1:pq + 2], MAGIC, ALU.mult, ALU.add),
              reads=[kx, kp + "f"], writes=[kx])
            o("vector", lambda v: v.tensor_scalar(
                q1b[:], xa[:], MAGIC, None, ALU.subtract),
              reads=[kx], writes=["q1b"])
            o("vector", lambda v: v.tensor_scalar(
                stt[:, pq + 2:pq + 3], stt[:, pq + 13:pq + 14], 1e-5, None, ALU.add),
              reads=[kp + "mu"], writes=[kp + "vpe"])
            o("scalar", lambda sc: sc.activation(
                stt[:, pq + 3:pq + 4], stt[:, pq + 2:pq + 3], AF.Sqrt),
              reads=[kp + "vpe"], writes=[kp + "sig"])
            o("vector", lambda v: v.reciprocal(stt[:, pq + 4:pq + 5], stt[:, pq + 3:pq + 4]),
              reads=[kp + "sig"], writes=[kp + "rstd"])
            o("vector", lambda v: v.tensor_scalar(
                stt[:, pq + 5:pq + 6], stt[:, pq:pq + 1], stt[:, pq + 4:pq + 5],
                1e-5, ALU.mult, ALU.max),
              reads=[kp + "am", kp + "rstd"], writes=[kp + "u2"])
            o("vector", lambda v: v.tensor_tensor(
                stt[:, pq + 6:pq + 7], stt[:, pq + 5:pq + 6], mwr[:, 4:5], ALU.mult),
              reads=[kp + "u2", "mw1s"], writes=[kp + "cs"])
            o("tensor", lambda t: t.matmul(pS[0:1, 0:128], stt[:, pq + 6:pq + 7],
                                           eye_f[:], start=True, stop=True),
              reads=[kp + "cs", "eye_f"], writes=["pS0"])
            o("vector", lambda v: v.tensor_copy(
                csr[0:1, tt * 128:(tt + 1) * 128], pS[0:1, 0:128]),
              reads=["pS0"], writes=[f"csr{tt}"])
            for c in range(4):
                o("tensor", lambda t, c=c: t.matmul(
                    pS[:, 1024 + c * 128:1024 + (c + 1) * 128],
                    q1b[:, 2 * c * 128:(2 * c + 1) * 128], eye_b[:],
                    start=True, stop=True),
                  reads=["q1b", "eye_b"], writes=["pS2"], inc=(c == 3))
                o("tensor", lambda t, c=c: t.matmul(
                    pS[:, 1536 + c * 128:1536 + (c + 1) * 128],
                    q1b[:, (2 * c + 1) * 128:(2 * c + 2) * 128], eye_b[:],
                    start=True, stop=True),
                  reads=["q1b", "eye_b"], writes=["pS3"], inc=(c == 3))
            for c in range(4):
                o("vector", lambda v, c=c, w=w, co=co: v.tensor_copy(
                    xqw[w][2 * c][:, 1 + co:129 + co],
                    pS[:, 1024 + c * 128:1024 + (c + 1) * 128]),
                  reads=["pS2"], writes=[f"win{w}:{2 * c}"], inc=(c == 3))
                o("scalar", lambda sc, c=c, w=w, co=co: sc.copy(
                    xqw[w][2 * c + 1][:, 1 + co:129 + co],
                    pS[:, 1536 + c * 128:1536 + (c + 1) * 128]),
                  reads=["pS3"], writes=[f"win{w}:{2 * c + 1}"], inc=(c == 3))
            if co == 0:
                if tt == 0:
                    for dc in range(8):
                        o("vector", lambda v, dc=dc: v.memset(xqw[0][dc][:, 0:1], 0.0),
                          reads=[], writes=[f"win0:{dc}"], inc=(dc == 7))
                else:
                    pw = (wi - 1) % 3
                    for dc in range(8):
                        o("vector", lambda v, dc=dc, w=w, pw=pw: v.tensor_copy(
                            xqw[w][dc][:, 0:1], xqw[pw][dc][:, TB:TB + 1]),
                          reads=[f"win{pw}:{dc}"], writes=[f"win{w}:{dc}"], inc=(dc == 7))
                    for dc in range(8):
                        o("vector", lambda v, dc=dc, w=w, pw=pw: v.tensor_copy(
                            xqw[pw][dc][:, TB + 1:TB + 2], xqw[w][dc][:, 1:2]),
                          reads=[f"win{w}:{dc}"], writes=[f"win{pw}:{dc}"], inc=(dc == 7))
            if tt == NT - 1:
                for dc in range(8):
                    o("vector", lambda v, dc=dc, w=w: v.memset(
                        xqw[w][dc][:, TB + 1:TB + 2], 0.0),
                      reads=[], writes=[f"win{w}:{dc}"], inc=(dc == 7))

        def do_mm2(tb):
            t0 = tb * TB
            xq = xq2[tb % 2]
            kcs2 = f"cs2:{tb % 2}"
            pcs2 = 32 + 8 * (tb % 2)
            NH = TB // 128
            banks = [(p7[:], "p7"), (pS[:, 512:1024], "pS1"),
                     (pS[:, 1024:1536], "pS2"), (pS[:, 1536:2048], "pS3")]
            for h in range(NH):
                d("sync", xr[h][:], src[seq, t0 + h * 128:t0 + (h + 1) * 128, :],
                  writes=[f"xr{h}"], group=f"xr{h}")
            for mc in range(32):
                d("sync", w2s[mc % 4][:], sq2[mc * 128:(mc + 1) * 128, :],
                  writes=[f"w2s{mc % 4}"], group=f"w2s{mc % 4}")
                for h in range(NH):
                    for dh in range(2):
                        pb, kpb = banks[2 * h + dh]
                        o("tensor", lambda t, mc=mc, h=h, dh=dh, pb=pb, xq=xq: t.matmul(
                            pb, xq[:, mc * TB + h * 128:mc * TB + (h + 1) * 128],
                            w2s[mc % 4][:, dh * 512:(dh + 1) * 512],
                            start=(mc == 0), stop=(mc == 31)),
                          reads=[f"xq2{tb % 2}:{mc}", f"w2s{mc % 4}"],
                          writes=[kpb], inc=(h == NH - 1 and dh == 1))
            for h in range(NH):
                xa, kx = xr[h], f"xr{h}"
                ph = pcs2 + 3 * h
                for dh in range(2):
                    pb, kpb = banks[2 * h + dh]
                    o("vector", lambda v, pb=pb, dh=dh, xa=xa, ph=ph:
                      v.scalar_tensor_tensor(
                        xa[:, dh * 512:(dh + 1) * 512], pb, stt[:, ph:ph + 1],
                        xa[:, dh * 512:(dh + 1) * 512], ALU.mult, ALU.add),
                      reads=[kpb, kcs2, kx], writes=[kx])
                d("sync", dst[seq, t0 + h * 128:t0 + (h + 1) * 128, :], xa[:],
                  reads=[kx], group="xout")

        TPW = TB // 128
        for tt in range(min(NT, 2 * TPW)):
            quant_tile(tt)
        for tb in range(NTB):
            w = tb % 3
            t0 = tb * TB
            lo, hi = max(0, t0 - 1), min(S, t0 + TB + 1)
            rt = [f"csr{i}" for i in range(lo // 128, min(NT, (hi + 127) // 128))]
            o("tensor", lambda t: t.matmul(
                pS[:, 0:hi - lo], ones_f[0:1, :], csr[0:1, lo:hi],
                start=True, stop=True),
              reads=rt + ["ones_f"], writes=["pS0"])
            o("vector", lambda v: v.tensor_copy(
                bcc[:, lo - (t0 - 1):hi - (t0 - 1)], pS[:, 0:hi - lo]),
              reads=["pS0"], writes=["bcc"])
            if t0 == 0:
                o("vector", lambda v: v.memset(bcc[:, 0:1], 0.0), writes=["bcc"])
            if t0 + TB == S:
                o("vector", lambda v: v.memset(bcc[:, TB + 1:TB + 2], 0.0), writes=["bcc"])
            o("vector", lambda v: v.memset(m1b[:], 0.0), writes=["m1b"])

            winkeys = [f"win{w}:{dc}" for dc in range(8)]
            for mc in range(32):
                pb, kpb = [(p4, "p4"), (p5, "p5"), (p6, "p6")][mc % 3]
                for dc in range(8):
                    o("tensor", lambda t, mc=mc, dc=dc, pb=pb, w=w: t.matmul(
                        pb[:, 0:TB + 2], w1q[dc][:, mc * 128:(mc + 1) * 128],
                        xqw[w][dc][:, 0:TB + 2],
                        start=(dc == 0), stop=(dc == 7)),
                      reads=[f"w1q{dc}"] + (winkeys if dc == 0 else []),
                      writes=[kpb], inc=(dc == 7))
                hbx, khb = (hb[0], "hb0") if mc % 2 == 0 else (hb[1], "hb1")
                o("vector", lambda v, pb=pb, hbx=hbx: v.tensor_tensor(
                    hbx[:], pb[:, 0:TB + 2], bcc[:], ALU.mult),
                  reads=[kpb, "bcc"], writes=[khb])
                ybx, kyb = (yb[0], "yb0") if mc % 2 == 0 else (yb[1], "yb1")
                o("vector", lambda v, hbx=hbx, mc=mc, ybx=ybx: v.tensor_scalar(
                    ybx[:], hbx[:, 1:TB + 1], cvw[:, 3 * mc + 1:3 * mc + 2],
                    cvb[:, mc:mc + 1], ALU.mult, ALU.add),
                  reads=[khb, "cvw", "cvb"], writes=[kyb])
                o("vector", lambda v, hbx=hbx, mc=mc, ybx=ybx: v.scalar_tensor_tensor(
                    ybx[:], hbx[:, 0:TB], cvw[:, 3 * mc:3 * mc + 1],
                    ybx[:], ALU.mult, ALU.add),
                  reads=[khb, kyb, "cvw"], writes=[kyb])
                o("vector", lambda v, hbx=hbx, mc=mc, ybx=ybx: v.scalar_tensor_tensor(
                    ybx[:], hbx[:, 2:TB + 2], cvw[:, 3 * mc + 2:3 * mc + 3],
                    ybx[:], ALU.mult, ALU.add),
                  reads=[khb, kyb, "cvw"], writes=[kyb])
                o("scalar", lambda sc, mc=mc, ybx=ybx: sc.activation(
                    sl[:, mc * TB:(mc + 1) * TB], ybx[:], AF.Silu),
                  reads=[kyb], writes=[f"sl{mc}"])
                abx, kab = (ab[0], "ab0") if mc % 2 == 0 else (ab[1], "ab1")
                o("scalar", lambda sc, mc=mc, abx=abx: sc.activation(
                    abx[:], sl[:, mc * TB:(mc + 1) * TB], AF.Abs),
                  reads=[f"sl{mc}"], writes=[kab])
                if mc > 0:
                    pab, kpab = (ab[0], "ab0") if (mc - 1) % 2 == 0 else (ab[1], "ab1")
                    o("vector", lambda v, pab=pab: v.tensor_tensor(
                        m1b[:], m1b[:], pab[:], ALU.max),
                      reads=[kpab, "m1b"], writes=["m1b"])
            o("vector", lambda v: v.tensor_tensor(
                m1b[:], m1b[:], ab[1][:], ALU.max),
              reads=["ab1", "m1b"], writes=["m1b"])
            if tb >= 1:
                do_mm2(tb - 1)
            for j in range(TPW):
                if (tb + 2) * TPW + j < NT:
                    quant_tile((tb + 2) * TPW + j)
            pcs2 = 32 + 8 * (tb % 2)
            kcs2 = f"cs2:{tb % 2}"
            for h in range(TPW):
                ph = pcs2 + 3 * h
                o("tensor", lambda t, h=h: t.matmul(
                    pS[:, 512 + h * 128:640 + h * 128],
                    m1b[:, h * 128:(h + 1) * 128], eye_f[:],
                    start=True, stop=True),
                  reads=["m1b", "eye_f"], writes=["pS1"])
                o("vector", lambda v, h=h, ph=ph: v.tensor_reduce(
                    stt[:, ph + 1:ph + 2], pS[:, 512 + h * 128:640 + h * 128],
                    AX.X, ALU.max),
                  reads=["pS1"], writes=[f"m2r{h}"])
                o("vector", lambda v, ph=ph: v.tensor_scalar(
                    stt[:, ph + 1:ph + 2], stt[:, ph + 1:ph + 2],
                    1e-5, None, ALU.max),
                  reads=[f"m2r{h}"], writes=[f"m2{h}"])
                o("vector", lambda v, ph=ph: v.reciprocal(
                    stt[:, ph + 2:ph + 3], stt[:, ph + 1:ph + 2]),
                  reads=[f"m2{h}"], writes=[f"im2{h}"])
                o("vector", lambda v, ph=ph: v.tensor_scalar(
                    stt[:, ph + 2:ph + 3], stt[:, ph + 2:ph + 3],
                    127.0, None, ALU.mult),
                  reads=[f"im2{h}"], writes=[f"f2{h}"])
                o("vector", lambda v, ph=ph: v.tensor_tensor(
                    stt[:, ph:ph + 1], stt[:, ph + 1:ph + 2], mwr[:, 5:6],
                    ALU.mult),
                  reads=[f"m2{h}", "mw2s"], writes=[kcs2])
                o("tensor", lambda t, h=h, ph=ph: t.matmul(
                    pS[0:1, 512 + h * 128:640 + h * 128], stt[:, ph + 2:ph + 3],
                    eye_f[:], start=True, stop=True),
                  reads=[f"f2{h}", "eye_f"], writes=["pS1"])
                o("vector", lambda v, h=h: v.tensor_copy(
                    f2r[0:1, h * 128:(h + 1) * 128],
                    pS[0:1, 512 + h * 128:640 + h * 128]),
                  reads=["pS1"], writes=["f2r"])
            o("tensor", lambda t: t.matmul(pS[:, 512:512 + TB], ones_f[0:1, :],
                                           f2r[0:1, 0:TB], start=True, stop=True),
              reads=["f2r", "ones_f"], writes=["pS1"])
            o("vector", lambda v: v.tensor_copy(bc2[:], pS[:, 512:512 + TB]),
              reads=["pS1"], writes=["bc2"])
            xq = xq2[tb % 2]
            for mc in range(32):
                o("vector", lambda v, mc=mc: v.tensor_tensor(
                    sl[:, mc * TB:(mc + 1) * TB], sl[:, mc * TB:(mc + 1) * TB],
                    bc2[:], ALU.mult),
                  reads=[f"sl{mc}", "bc2"], writes=[f"sl{mc}"])
                o("vector", lambda v, mc=mc, xq=xq: v.tensor_scalar(
                    xq[:, mc * TB:(mc + 1) * TB], sl[:, mc * TB:(mc + 1) * TB],
                    MAGIC, MAGIC, ALU.add, ALU.subtract),
                  reads=[f"sl{mc}"], writes=[f"xq2{tb % 2}:{mc}"])
        do_mm2(NTB - 1)
        arB.pop(); arF.pop()

    def stage_attn(seq):
        S_.barrier()
        arB.push(); arF.push()
        oall = [arB.alloc(S) for _ in range(8)]
        xnbuf = arB.alloc(D)
        whq8 = arB.alloc(2 * 8 * NBC)
        ut = [arF.alloc(D) for _ in range(2)]
        stt = arF.alloc(16)
        lrec = arF.alloc(QB)
        bcst = arF.alloc(QB)
        wtmp = arF.alloc(NBC)
        arB.push()
        xnT = [arB.alloc(S) for _ in range(8)]

        for tt in range(NT):
            xa, kx = ut[tt % 2], f"ut{tt % 2}"
            d("sync", xa[:], sx1[seq, tt * 128:(tt + 1) * 128, :],
              writes=[kx], group=f"bx{tt % 2}")
            o("vector", lambda v: v.bn_stats(stt[:, 0:6], xa[:, 0:512]),
              reads=[kx], writes=["st0"])
            o("vector", lambda v: v.bn_stats(stt[:, 6:12], xa[:, 512:1024]),
              reads=[kx], writes=["st1"])
            o("vector", lambda v: v.bn_aggr(stt[:, 12:14], stt[:, 0:12]),
              reads=["st0", "st1"], writes=["mu"])
            o("vector", lambda v: v.tensor_scalar(stt[:, 14:15], stt[:, 13:14],
                                                  1e-5, None, ALU.add),
              reads=["mu"], writes=["vpe"])
            o("scalar", lambda sc: sc.activation(stt[:, 15:16], stt[:, 14:15], AF.Sqrt),
              reads=["vpe"], writes=["sig"])
            o("vector", lambda v: v.reciprocal(stt[:, 14:15], stt[:, 15:16]),
              reads=["sig"], writes=["rstd"])
            o("vector", lambda v: v.tensor_scalar(
                xa[:], xa[:], stt[:, 12:13], stt[:, 14:15], ALU.subtract, ALU.mult),
              reads=[kx, "mu", "rstd"], writes=[kx])
            d("sync", su[seq, tt * 128:(tt + 1) * 128, :], xa[:], reads=[kx], group="su")
            if debug:
                d("sync", dbg["d_u"][seq, tt * 128:(tt + 1) * 128, :], xa[:],
                  reads=[kx], group="dbgu")
            o("vector", lambda v: v.bn_stats(stt[:, 0:6], xa[:, 0:512]),
              reads=[kx], writes=["st0"])
            o("vector", lambda v: v.bn_stats(stt[:, 6:12], xa[:, 512:1024]),
              reads=[kx], writes=["st1"])
            o("vector", lambda v: v.bn_aggr(stt[:, 12:14], stt[:, 0:12]),
              reads=["st0", "st1"], writes=["mu"])
            o("vector", lambda v: v.tensor_scalar(stt[:, 14:15], stt[:, 13:14],
                                                  1e-6, None, ALU.add),
              reads=["mu"], writes=["vpe"])
            o("scalar", lambda sc: sc.activation(stt[:, 15:16], stt[:, 14:15], AF.Sqrt),
              reads=["vpe"], writes=["sig"])
            o("vector", lambda v: v.reciprocal(stt[:, 14:15], stt[:, 15:16]),
              reads=["sig"], writes=["rstd"])
            o("vector", lambda v: v.scalar_tensor_tensor(
                xa[:], xa[:], stt[:, 12:13], g1r[:], ALU.subtract, ALU.mult),
              reads=[kx, "mu", "g1r"], writes=[kx])
            o("vector", lambda v: v.scalar_tensor_tensor(
                xnbuf[:], xa[:], stt[:, 14:15], b1r[:], ALU.mult, ALU.add),
              reads=[kx, "rstd", "b1r"], writes=["xnbuf"])
            for c in range(4):
                o("tensor", lambda t, c=c: t.matmul(
                    pS[:, 1024 + c * 128:1024 + (c + 1) * 128],
                    xnbuf[:, 2 * c * 128:(2 * c + 1) * 128], eye_b[:],
                    start=True, stop=True),
                  reads=["xnbuf", "eye_b"], writes=["pS2"], inc=(c == 3))
                o("tensor", lambda t, c=c: t.matmul(
                    pS[:, 1536 + c * 128:1536 + (c + 1) * 128],
                    xnbuf[:, (2 * c + 1) * 128:(2 * c + 2) * 128], eye_b[:],
                    start=True, stop=True),
                  reads=["xnbuf", "eye_b"], writes=["pS3"], inc=(c == 3))
            for c in range(4):
                o("vector", lambda v, c=c, tt=tt: v.tensor_copy(
                    xnT[2 * c][:, tt * 128:(tt + 1) * 128],
                    pS[:, 1024 + c * 128:1024 + (c + 1) * 128]),
                  reads=["pS2"], writes=[f"xnT{2 * c}:{tt}"], inc=(c == 3))
                o("scalar", lambda sc, c=c, tt=tt: sc.copy(
                    xnT[2 * c + 1][:, tt * 128:(tt + 1) * 128],
                    pS[:, 1536 + c * 128:1536 + (c + 1) * 128]),
                  reads=["pS3"], writes=[f"xnT{2 * c + 1}:{tt}"], inc=(c == 3))
        xnall = [f"xnT{i}:{j}" for i in range(8) for j in range(NT)]

        for qki, wdr in ((0, whqd), (1, whkd)):
            for p in range(8):
                for par in range(2):
                    h_ = 2 * p + par
                    d("sync", wtmp[par * 64:(par + 1) * 64, :],
                      wdr[h_ * DH:(h_ + 1) * DH, :], writes=[f"wtmp{par}"], group="whq")
                o("vector", lambda v, p=p, qki=qki: v.tensor_scalar(
                    whq8[:, (qki * 8 + p) * NBC:(qki * 8 + p + 1) * NBC],
                    wtmp[:, :], 8.0 if qki == 0 else 1.0, None, ALU.mult),
                  reads=["wtmp0", "wtmp1"], writes=[f"whq8:{qki}:{p}"])

        for half in range(2):
            arB.push(); arF.push()
            qsb = [arB.alloc(S) for _ in range(4)]
            ksb = [arB.alloc(S) for _ in range(4)]
            qbb = [arB.alloc(S) for _ in range(4)]
            kbb = [arB.alloc(S) for _ in range(4)]
            vsb = [arB.alloc(NKC * 65 + 1) for _ in range(8)]
            qbn = [arB.alloc(128) for _ in range(2)]
            eqf = [arF.alloc(128) for _ in range(2)]
            pair0 = half * 4

            for hh in range(8):
                o("vector", lambda v, hh=hh: v.memset(
                    vsb[hh][:, 0:NKC * 65].rearrange("p (a b) -> p a b", b=65)[:, :, 64:65], 1.0),
                  writes=[f"vones{hh}"])

            arB.push()
            wpr = [arB.alloc(8 * 128) for _ in range(2)]
            wvh = arB.alloc(8 * 512)
            for which, wdr, dstl, scl in (("q", wqd, qsb, 0.125), ("k", wkd, ksb, 1.0)):
                wb = wpr[0] if which == "q" else wpr[1]
                for pi in range(4):
                    p = pair0 + pi
                    kwb = f"wpr{which}"
                    bdr = bwq if which == "q" else bwk
                    d("sync", wb[:].rearrange("p (a m) -> p a m", a=8),
                      bdr[:, p * 128:(p + 1) * 128].rearrange(
                          "(a p) m -> p a m", p=128),
                      writes=[f"{kwb}:{dc2}" for dc2 in range(8)], group=f"w{which}")
                    for nb_ in range(S // W512):
                        pb, kpb = (p4, "p4") if nb_ % 2 == 0 else (p5, "p5")
                        for dc in range(8):
                            o("tensor", lambda t, dc=dc, nb_=nb_, wb=wb, pb=pb: t.matmul(
                                pb[:, 0:W512], wb[:, dc * 128:(dc + 1) * 128],
                                xnT[dc][:, nb_ * W512:(nb_ + 1) * W512],
                                start=(dc == 0), stop=(dc == 7)),
                              reads=[f"{kwb}:{dc}"] + (xnall if dc == 0 else []),
                              writes=[kpb], inc=(dc == 7))
                        o("vector", lambda v, pb=pb, nb_=nb_, pi=pi, dstl=dstl, scl=scl: v.tensor_scalar(
                            dstl[pi][:, nb_ * W512:(nb_ + 1) * W512], pb[:, 0:W512],
                            scl, None, ALU.mult),
                          reads=[kpb], writes=[f"{which}sb{pi}:{nb_}"])
            d("sync", wvh[:].rearrange("p (a m) -> p a m", a=8),
              bwv[:, half * 512:(half + 1) * 512].rearrange(
                  "(a p) m -> p a m", p=128),
              writes=[f"wvh:{dc2}" for dc2 in range(8)], group="wv")
            for tt in range(NT):
                pb, kpb = (p6, "p6") if tt % 2 == 0 else (p7, "p7")
                for dc in range(8):
                    o("tensor", lambda t, dc=dc, tt=tt, pb=pb: t.matmul(
                        pb[:], xnT[dc][:, tt * 128:(tt + 1) * 128],
                        wvh[:, dc * 512:(dc + 1) * 512],
                        start=(dc == 0), stop=(dc == 7)),
                      reads=[f"wvh:{dc}"] + (xnall if dc == 0 else []),
                      writes=[kpb], inc=(dc == 7))
                for hh in range(8):
                    o("vector", lambda v, hh=hh, tt=tt, pb=pb: v.tensor_copy(
                        vsb[hh][:, tt * 65:tt * 65 + 64], pb[:, hh * 64:(hh + 1) * 64]),
                      reads=[kpb], writes=[f"v{hh}:{tt}"], inc=(hh == 7))
            arB.pop()
            ptb = [arB.alloc(2 * QB) for _ in range(2)]

            for qki, srcl, dstl, scl in ((0, qsb, qbb, 0.1), (1, ksb, kbb, 1.0)):
                sk = "q" if qki == 0 else "k"
                dk = "qb" if qki == 0 else "kb"
                it = 0
                for gg in range(2):
                    for tt in range(NT):
                        itp = it % 2
                        it += 1
                        if itp == 0:
                            pbE, kpbE = p6, "p6"
                            pbO, kpbO = p7, "p7"
                        else:
                            pbE, kpbE = pS[:, 0:512], "pS0"
                            pbO, kpbO = pS[:, 512:1024], "pS1"
                        trp = pS[:, 1024:1536] if itp == 0 else pS[:, 1536:2048]
                        ktr = "pS2" if itp == 0 else "pS3"
                        eqx, keq = eqf[itp], f"eq{itp}"
                        qbx, kqb = qbn[itp], f"qbn{itp}"
                        sb_ = 8 * itp
                        for j in range(4):
                            pi = 2 * gg + j // 2
                            par = j % 2
                            pbx = pbE if par == 0 else pbO
                            kpx = kpbE if par == 0 else kpbO
                            cj = j // 2
                            o("tensor", lambda t, pi=pi, par=par, cj=cj, tt=tt, qki=qki, pbx=pbx, srcl=srcl: t.matmul(
                                pbx[0:128, cj * 32:(cj + 1) * 32] if not isinstance(pbx, type(pS)) else pbx[0:128, cj * 32:(cj + 1) * 32],
                                srcl[pi][par * 64:(par + 1) * 64, tt * 128:(tt + 1) * 128],
                                whq8[par * 64:(par + 1) * 64,
                                     (qki * 8 + pair0 + pi) * NBC:(qki * 8 + pair0 + pi + 1) * NBC],
                                start=(cj == 0), stop=(cj == 1),
                                tile_position=(par * 64, 0)),
                              reads=[f"{sk}sb{pi}:{tt * 128 // W512}",
                                     f"whq8:{qki}:{pair0 + pi}"],
                              writes=[kpx], inc=(j >= 2))
                        o("scalar", lambda sc, pbE=pbE, eqx=eqx: sc.activation(
                            eqx[:, 0:64], pbE[0:128, 0:64] if True else None, AF.Exp),
                          reads=[kpbE], writes=[keq])
                        o("scalar", lambda sc, pbO=pbO, eqx=eqx: sc.activation(
                            eqx[:, 64:128], pbO[0:128, 0:64], AF.Exp),
                          reads=[kpbO], writes=[keq])
                        o("vector", lambda v, eqx=eqx, sb_=sb_: v.tensor_reduce(
                            stt[:, sb_:sb_ + 4],
                            eqx[:].rearrange("p (a b) -> p a b", b=32),
                            AX.X, ALU.add),
                          reads=[keq], writes=[f"eqs{itp}"])
                        o("vector", lambda v, sb_=sb_: v.reciprocal(
                            stt[:, sb_ + 4:sb_ + 8], stt[:, sb_:sb_ + 4]),
                          reads=[f"eqs{itp}"], writes=[f"eqr{itp}"])
                        for ci, j in ((0, 0), (1, 2), (2, 1), (3, 3)):
                            o("vector", lambda v, ci=ci, scl=scl, eqx=eqx, qbx=qbx, sb_=sb_: v.tensor_scalar(
                                qbx[:, ci * 32:(ci + 1) * 32], eqx[:, ci * 32:(ci + 1) * 32],
                                stt[:, sb_ + 4 + ci:sb_ + 5 + ci], scl, ALU.mult, ALU.mult),
                              reads=[f"eqr{itp}", keq], writes=[kqb], inc=(ci == 3))
                        for ci, j in ((0, 0), (1, 2), (2, 1), (3, 3)):
                            par = j % 2
                            o("tensor", lambda t, ci=ci, par=par, qbx=qbx, trp=trp: t.matmul(
                                trp[par * 64:par * 64 + 32, ci * 128:(ci + 1) * 128],
                                qbx[:, ci * 32:(ci + 1) * 32], eye_b[:],
                                start=(ci in (0, 2)), stop=(ci in (1, 3)),
                                tile_position=(0, par * 64)),
                              reads=[kqb, "eye_b"], writes=[ktr], inc=(ci == 3))
                        for ci, j in ((0, 0), (1, 2), (2, 1), (3, 3)):
                            par = j % 2
                            pi = 2 * gg + j // 2
                            o("vector", lambda v, ci=ci, par=par, pi=pi, tt=tt, dstl=dstl, trp=trp: v.tensor_copy(
                                dstl[pi][par * 64:par * 64 + 32, tt * 128:(tt + 1) * 128],
                                trp[par * 64:par * 64 + 32, ci * 128:(ci + 1) * 128]),
                              reads=[ktr], writes=[f"{dk}{pi}:{tt}"], inc=(ci == 3))

            def emit_scores_exp(pi, q0, nb_, kc):
                par = kc % 2
                sE = pS[:, par * 1024:par * 1024 + QB]
                sO = pS[:, par * 1024 + 512:par * 1024 + 512 + QB]
                kE, kO = f"pS{2 * par}", f"pS{2 * par + 1}"
                o("tensor", lambda t: t.matmul(
                    sE, ksb[pi][0:64, kc * 128:(kc + 1) * 128],
                    qsb[pi][0:64, q0:q0 + QB],
                    start=True, stop=False, tile_position=(0, 0)),
                  reads=[f"qsb{pi}:{nb_}", f"ksb{pi}:{kc * 128 // W512}"],
                  writes=[kE], inc=False)
                o("tensor", lambda t: t.matmul(
                    sO, ksb[pi][64:128, kc * 128:(kc + 1) * 128],
                    qsb[pi][64:128, q0:q0 + QB],
                    start=True, stop=False, tile_position=(64, 0)),
                  reads=[f"qsb{pi}:{nb_}", f"ksb{pi}:{kc * 128 // W512}"],
                  writes=[kO], inc=False)
                o("tensor", lambda t: t.matmul(
                    sE, kbb[pi][0:32, kc * 128:(kc + 1) * 128],
                    qbb[pi][0:32, q0:q0 + QB],
                    start=False, stop=True, tile_position=(0, 0)),
                  reads=[f"qb{pi}:{q0 // 128 + z}" for z in range(QB // 128)] + [f"kb{pi}:{kc}"],
                  writes=[kE], inc=True)
                o("tensor", lambda t: t.matmul(
                    sO, kbb[pi][64:96, kc * 128:(kc + 1) * 128],
                    qbb[pi][64:96, q0:q0 + QB],
                    start=False, stop=True, tile_position=(64, 0)),
                  reads=[f"qb{pi}:{q0 // 128 + z}" for z in range(QB // 128)] + [f"kb{pi}:{kc}"],
                  writes=[kO], inc=True)
                o("scalar", lambda sc: sc.activation(
                    ptb[kc % 2][:].rearrange("p (a b) -> p a b", b=QB),
                    pS[:, par * 1024:(par + 1) * 1024].rearrange(
                        "p (a b) -> p a b", b=512)[:, :, 0:QB],
                    AF.Exp, bias=nb20[:, 0:1]),
                  reads=[kE, kO], writes=[f"ptb{kc % 2}"])

            def emit_pv(pi, kc, obE, okE, obO, okO):
                kpt = f"ptb{kc % 2}"
                hhE, hhO = 2 * pi, 2 * pi + 1
                o("tensor", lambda t: t.matmul(
                    obE[0:65, 0:QB], vsb[hhE][:, kc * 65:kc * 65 + 65],
                    ptb[kc % 2][:, 0:QB],
                    start=(kc == 0), stop=(kc == NKC - 1)),
                  reads=[kpt, f"v{hhE}:{kc}", f"vones{hhE}"],
                  writes=[okE], inc=True)
                o("tensor", lambda t: t.matmul(
                    obO[64:128, 0:QB], vsb[hhO][:, kc * 65:kc * 65 + 64],
                    ptb[kc % 2][:, QB:2 * QB],
                    start=(kc == 0), stop=(kc == NKC - 1),
                    tile_position=(0, 64)),
                  reads=[kpt, f"v{hhO}:{kc}"], writes=[okO], inc=False)
                o("tensor", lambda t: t.matmul(
                    obO[32:33, 0:QB], vsb[hhO][:, kc * 65 + 64:kc * 65 + 65],
                    ptb[kc % 2][:, QB:2 * QB],
                    start=(kc == 0), stop=(kc == NKC - 1),
                    tile_position=(0, 32)),
                  reads=[kpt, f"vones{hhO}"], writes=[okO], inc=True)

            def emit_evac(pg, q0, obE, okE, obO, okO):
                o("vector", lambda v: v.reciprocal(
                    lrec[64:65, 0:QB], obE[64:65, 0:QB]),
                  reads=[okE], writes=["lrec"])
                o("tensor", lambda t: t.matmul(
                    pS[0:64, 0:QB], ones_f[64:65, 0:64], lrec[64:65, 0:QB],
                    start=True, stop=True, tile_position=(64, 0)),
                  reads=["lrec", "ones_f"], writes=["pS0"])
                o("scalar", lambda sc: sc.copy(bcst[0:64, 0:QB], pS[0:64, 0:QB]),
                  reads=["pS0"], writes=["bcst0"])
                o("vector", lambda v: v.tensor_tensor(
                    oall[pg][0:64, q0:q0 + QB], obE[0:64, 0:QB],
                    bcst[0:64, 0:QB], ALU.mult),
                  reads=[okE, "bcst0"], writes=[f"oall{pg}:{q0}"])
                o("vector", lambda v: v.reciprocal(
                    lrec[32:33, 0:QB], obO[32:33, 0:QB]),
                  reads=[okO], writes=["lrec2"])
                o("tensor", lambda t: t.matmul(
                    pS[64:128, 0:QB], ones_f[32:33, 0:64], lrec[32:33, 0:QB],
                    start=True, stop=True, tile_position=(32, 64)),
                  reads=["lrec2", "ones_f"], writes=["pS0"])
                o("scalar", lambda sc: sc.copy(bcst[64:128, 0:QB], pS[64:128, 0:QB]),
                  reads=["pS0"], writes=["bcst1"])
                o("vector", lambda v: v.tensor_tensor(
                    oall[pg][64:128, q0:q0 + QB], obO[64:128, 0:QB],
                    bcst[64:128, 0:QB], ALU.mult),
                  reads=[okO, "bcst1"], writes=[f"oall{pg}:{q0}"])

            pend_evac = None
            for pi in range(4):
                for qb_ in range(NQB):
                    q0 = qb_ * QB
                    nb_ = q0 // W512
                    obpar = (pi * NQB + qb_) % 2
                    obE, okE = (p4, "p4") if obpar == 0 else (p6, "p6")
                    obO, okO = (p5, "p5") if obpar == 0 else (p7, "p7")
                    pg = pair0 + pi
                    emit_scores_exp(pi, q0, nb_, 0)
                    if pend_evac is not None:
                        emit_evac(*pend_evac)
                    for kc in range(1, NKC):
                        emit_scores_exp(pi, q0, nb_, kc)
                        emit_pv(pi, kc - 1, obE, okE, obO, okO)
                    emit_pv(pi, NKC - 1, obE, okE, obO, okO)
                    pend_evac = (pg, q0, obE, okE, obO, okO)
            emit_evac(*pend_evac)
            S_.barrier()
            arB.pop(); arF.pop()
        arB.pop()

        arB.push(); arF.push()
        wob = arB.alloc(8 * D)
        w1b = [arB.alloc(8 * 128) for _ in range(4)]
        w2b = [arB.alloc(D) for _ in range(4)]
        h1 = arB.alloc(32 * MLPB)
        xmT = [arB.alloc(MLPB) for _ in range(8)]
        xmb = arB.alloc(D)
        x3b = arB.alloc(2 * (MLPB // 128) * D).bitcast(F32)
        x1t = arF.alloc(D)
        for pg in range(8):
            d("sync", wob[:, pg * D:(pg + 1) * D], bwo[pg, :, :],
              writes=[f"wob{pg}"], group="wo")
        oallk = [f"oall{pg}:{q0}" for pg in range(8) for q0 in range(0, S, QB)]

        for mb in range(NMB):
            tiles = [mb * (MLPB // 128) + c for c in range(MLPB // 128)]
            for ci, tt in enumerate(tiles):
                for dh in range(2):
                    pb, kpb = (p4, "p4") if dh == 0 else (p5, "p5")
                    for pg in range(8):
                        o("tensor", lambda t, pg=pg, dh=dh, tt=tt, pb=pb: t.matmul(
                            pb[:], oall[pg][:, tt * 128:(tt + 1) * 128],
                            wob[:, pg * D + dh * 512:pg * D + (dh + 1) * 512],
                            start=(pg == 0), stop=(pg == 7)),
                          reads=[f"wob{pg}"] + (oallk if pg == 0 else []),
                          writes=[kpb], inc=(pg == 7))
                kx, xu = f"ut{tt % 2}", ut[tt % 2]
                d("sync", xu[:], su[seq, tt * 128:(tt + 1) * 128, :],
                  writes=[kx], group=f"bx{tt % 2}")
                for dh in range(2):
                    pb, kpb = (p4, "p4") if dh == 0 else (p5, "p5")
                    o("vector", lambda v, pb=pb, dh=dh, ci=ci, xu=xu: v.tensor_tensor(
                        x3b[:, ci * D + dh * 512:ci * D + (dh + 1) * 512],
                        pb[:], xu[:, dh * 512:(dh + 1) * 512], ALU.add),
                      reads=[kpb, kx], writes=[f"x3:{ci}"])
                kx3 = f"x3:{ci}"
                o("vector", lambda v, ci=ci: v.bn_stats(
                    stt[:, 0:6], x3b[:, ci * D:ci * D + 512]),
                  reads=[kx3], writes=["st0"])
                o("vector", lambda v, ci=ci: v.bn_stats(
                    stt[:, 6:12], x3b[:, ci * D + 512:ci * D + 1024]),
                  reads=[kx3], writes=["st1"])
                o("vector", lambda v: v.bn_aggr(stt[:, 12:14], stt[:, 0:12]),
                  reads=["st0", "st1"], writes=["mu"])
                o("vector", lambda v: v.tensor_scalar(stt[:, 14:15], stt[:, 13:14],
                                                      1e-6, None, ALU.add),
                  reads=["mu"], writes=["vpe"])
                o("scalar", lambda sc: sc.activation(stt[:, 15:16], stt[:, 14:15], AF.Sqrt),
                  reads=["vpe"], writes=["sig"])
                o("vector", lambda v: v.reciprocal(stt[:, 14:15], stt[:, 15:16]),
                  reads=["sig"], writes=["rstd"])
                o("vector", lambda v, ci=ci: v.scalar_tensor_tensor(
                    xmb[:], x3b[:, ci * D:(ci + 1) * D], stt[:, 12:13], g2r[:],
                    ALU.subtract, ALU.mult),
                  reads=[kx3, "mu", "g2r"], writes=["xmb"])
                o("vector", lambda v: v.scalar_tensor_tensor(
                    xmb[:], xmb[:], stt[:, 14:15], b2lr[:], ALU.mult, ALU.add),
                  reads=["xmb", "rstd", "b2lr"], writes=["xmb"])
                for c in range(4):
                    o("tensor", lambda t, c=c: t.matmul(
                        pS[:, 1024 + c * 128:1024 + (c + 1) * 128],
                        xmb[:, 2 * c * 128:(2 * c + 1) * 128], eye_b[:],
                        start=True, stop=True),
                      reads=["xmb", "eye_b"], writes=["pS2"], inc=(c == 3))
                    o("tensor", lambda t, c=c: t.matmul(
                        pS[:, 1536 + c * 128:1536 + (c + 1) * 128],
                        xmb[:, (2 * c + 1) * 128:(2 * c + 2) * 128], eye_b[:],
                        start=True, stop=True),
                      reads=["xmb", "eye_b"], writes=["pS3"], inc=(c == 3))
                for c in range(4):
                    o("vector", lambda v, c=c, ci=ci: v.tensor_copy(
                        xmT[2 * c][:, ci * 128:(ci + 1) * 128],
                        pS[:, 1024 + c * 128:1024 + (c + 1) * 128]),
                      reads=["pS2"], writes=[f"xmT{2 * c}:{ci}"], inc=(c == 3))
                    o("scalar", lambda sc, c=c, ci=ci: sc.copy(
                        xmT[2 * c + 1][:, ci * 128:(ci + 1) * 128],
                        pS[:, 1536 + c * 128:1536 + (c + 1) * 128]),
                      reads=["pS3"], writes=[f"xmT{2 * c + 1}:{ci}"], inc=(c == 3))
            xmk = [f"xmT{i}:{j}" for i in range(8) for j in range(MLPB // 128)]
            for mc in range(32):
                wb, kwb = w1b[mc % 4], f"w1b{mc % 4}"
                d("sync", wb[:].rearrange("p (a m) -> p a m", a=8),
                  bw1[:, mc * 128:(mc + 1) * 128].rearrange(
                      "(a p) m -> p a m", p=128),
                  writes=[f"{kwb}:{dc2}" for dc2 in range(8)], group=f"wm1{mc % 4}")
                pb, kpb = (p6, "p6") if mc % 2 == 0 else (p7, "p7")
                for dc in range(8):
                    o("tensor", lambda t, dc=dc, mc=mc, wb=wb, pb=pb: t.matmul(
                        pb[:, 0:MLPB], wb[:, dc * 128:(dc + 1) * 128],
                        xmT[dc][:, 0:MLPB], start=(dc == 0), stop=(dc == 7)),
                      reads=[f"{kwb}:{dc}"] + (xmk if dc == 0 else []),
                      writes=[kpb], inc=(dc == 7))
                o("vector", lambda v, mc=mc, pb=pb: v.tensor_scalar(
                    h1[:, mc * MLPB:(mc + 1) * MLPB], pb[:, 0:MLPB],
                    b1cc[:, mc:mc + 1], 0.0, ALU.add, ALU.max),
                  reads=[kpb, "b1cc"], writes=[f"h1:{mc}"])
            ypb = [pS[:, 0:512], pS[:, 512:1024], pS[:, 1024:1536],
                   pS[:, 1536:2048], p4[:], p5[:], p6[:], p7[:]]
            ypk = ["pS0", "pS1", "pS2", "pS3", "p4", "p5", "p6", "p7"]
            for mc in range(32):
                wb, kwb = w2b[mc % 4], f"w2b{mc % 4}"
                d("sync", wb[:], bw2[mc * 128:(mc + 1) * 128, :],
                  writes=[kwb], group=f"wm2{mc % 4}")
                for ci in range(MLPB // 128):
                    for dh in range(2):
                        bi = ci * 2 + dh
                        o("tensor", lambda t, mc=mc, ci=ci, dh=dh, wb=wb, bi=bi: t.matmul(
                            ypb[bi], h1[:, mc * MLPB + ci * 128:mc * MLPB + (ci + 1) * 128],
                            wb[:, dh * 512:(dh + 1) * 512],
                            start=(mc == 0), stop=(mc == 31)),
                          reads=[f"h1:{mc}", kwb],
                          writes=[ypk[bi]],
                          inc=(ci == MLPB // 128 - 1 and dh == 1))
            for ci, tt in enumerate(tiles):
                d("sync", x1t[:], sx1[seq, tt * 128:(tt + 1) * 128, :],
                  writes=["x1t"], group="bx1r")
                for dh in range(2):
                    bi = ci * 2 + dh
                    o("vector", lambda v, ci=ci, dh=dh, bi=bi: v.tensor_tensor(
                        x3b[:, ci * D + dh * 512:ci * D + (dh + 1) * 512],
                        x3b[:, ci * D + dh * 512:ci * D + (dh + 1) * 512],
                        ypb[bi], ALU.add),
                      reads=[ypk[bi], f"x3:{ci}"], writes=[f"x3:{ci}"])
                    o("vector", lambda v, ci=ci, dh=dh: v.tensor_tensor(
                        x3b[:, ci * D + dh * 512:ci * D + (dh + 1) * 512],
                        x3b[:, ci * D + dh * 512:ci * D + (dh + 1) * 512],
                        b2rr[:, dh * 512:(dh + 1) * 512], ALU.add),
                      reads=[f"x3:{ci}", "b2rr"], writes=[f"x3:{ci}"])
                    o("vector", lambda v, ci=ci, dh=dh: v.tensor_tensor(
                        x3b[:, ci * D + dh * 512:ci * D + (dh + 1) * 512],
                        x3b[:, ci * D + dh * 512:ci * D + (dh + 1) * 512],
                        x1t[:, dh * 512:(dh + 1) * 512], ALU.add),
                      reads=[f"x3:{ci}", "x1t"], writes=[f"x3:{ci}"])
                d("sync", sx4[seq, tt * 128:(tt + 1) * 128, :],
                  x3b[:, ci * D:(ci + 1) * D], reads=[f"x3:{ci}"], group="x4out")
                if debug:
                    d("sync", dbg["d_x4"][seq, tt * 128:(tt + 1) * 128, :],
                      x3b[:, ci * D:(ci + 1) * D], reads=[f"x3:{ci}"], group="dbg4")
        arB.pop(); arF.pop()
        arB.pop(); arF.pop()

    for seq in range(NSEQ):
        stage_ffn(seq, xin, sx1)
        if debug:
            S_.barrier()
            arF.push()
            tmp = arF.alloc(D)
            for tt in range(NT):
                dt_ = d("sync", tmp[:], sx1[seq, tt * 128:(tt + 1) * 128, :],
                        writes=["dtmp"], group="dbg1")
                d("sync", dbg["d_x1"][seq, tt * 128:(tt + 1) * 128, :], tmp[:],
                  reads=["dtmp"], group="dbg1b")
            arF.pop()
        stage_attn(seq)
        stage_ffn(seq, sx4, out)
    S_.emit("sync")
    return nc, st, dbg


def _host_inputs(inputs, S=2048):
    f32 = np.float32
    i = inputs
    shared = {
        "w1t": np.ascontiguousarray(i["L0_ffn_w1"].T.astype(f32)),
        "w2t": np.ascontiguousarray(i["L0_ffn_w2"].T.astype(f32)),
        "convw": np.ascontiguousarray(
            i["L0_ffn_conv_w"].reshape(32, 128, 3).transpose(1, 0, 2).reshape(128, 96).astype(f32)),
        "convb": np.ascontiguousarray(
            i["L0_ffn_conv_b"].reshape(32, 128).T.astype(f32)),
        "g1d": i["L0_attn_ln1_g"].reshape(1, D).astype(f32),
        "b1d": i["L0_attn_ln1_b"].reshape(1, D).astype(f32),
        "g2d": i["L0_attn_ln2_g"].reshape(1, D).astype(f32),
        "b2d": i["L0_attn_ln2_b"].reshape(1, D).astype(f32),
        "wqd": np.ascontiguousarray(i["L0_attn_Wq"].reshape(D, H * DH).astype(f32)),
        "wkd": np.ascontiguousarray(i["L0_attn_Wk"].reshape(D, H * DH).astype(f32)),
        "wvd": np.ascontiguousarray(i["L0_attn_Wv"].reshape(D, H * DH).astype(f32)),
        "whqd": np.ascontiguousarray(i["L0_attn_Whq"].reshape(H * DH, NBC).astype(f32)),
        "whkd": np.ascontiguousarray(i["L0_attn_Whk"].reshape(H * DH, NBC).astype(f32)),
        "wod": np.ascontiguousarray(
            i["L0_attn_Wo"].reshape(H // 2, 2 * DH, D).astype(f32)),
        "wm1d": np.ascontiguousarray(i["L0_attn_W1"].astype(f32)),
        "wm2d": np.ascontiguousarray(i["L0_attn_W2"].astype(f32)),
        "b1cd": np.ascontiguousarray(i["L0_attn_b1"].reshape(32, 128).T.astype(f32)),
        "b2rd": i["L0_attn_b2"].reshape(1, D).astype(f32),
        "eyebd": np.eye(128, dtype=f32).astype(bfdt),
        "eyefd": np.eye(128, dtype=f32),
        "onesfd": np.ones((128, 128), dtype=f32),
    }
    x = np.ascontiguousarray(i["L0_attn_inputs"].astype(f32))
    in_maps = []
    for c in range(NCORES):
        m = dict(shared)
        m["xin"] = np.ascontiguousarray(x[c * NSEQ:(c + 1) * NSEQ, :S, :])
        in_maps.append(m)
    return in_maps


_CACHE = {}


def kernel(**inputs):
    if "prog" not in _CACHE:
        nc, st, _ = build_program(S=2048, debug=False)
        _CACHE["prog"] = (nc, st)
    nc, st = _CACHE["prog"]
    in_maps = _host_inputs(inputs, S=2048)
    res = run_bass_kernel_spmd(nc, in_maps, core_ids=list(range(NCORES)))
    outp = np.concatenate([res.results[c]["out"] for c in range(NCORES)], axis=0)
    return outp.astype(np.float32)
